# version 17
# baseline (speedup 1.0000x reference)
import numpy as np
import ml_dtypes
import concourse.bass as bass
import concourse.mybir as mybir
from concourse.bass_utils import run_bass_kernel_spmd
from contextlib import ExitStack

F32 = mybir.dt.float32
BF16 = mybir.dt.bfloat16
AF = mybir.ActivationFunctionType
ALU = mybir.AluOpType
AX = mybir.AxisListType

B, L, D = 4, 2048, 2048
T = 1024
TE = T + 3
NH, HD, DS = 32, 64, 64
NK = 16
E, FF = 8, 768
NJZ, NJX, NJF = 16, 48, 6
CONVD, AOFF = 6144, 8192
HD = 64
Q = 64
NG = 8
NC = 16
EPS = 1e-5
INV_D = 1.0 / D


class Ev:
    __slots__ = ("eng", "fn", "deps", "sem", "incv", "needed", "val", "tag")

    def __init__(self, eng, fn, deps, sem, incv, tag):
        self.eng, self.fn, self.deps = eng, fn, deps
        self.sem, self.incv = sem, incv
        self.needed = False
        self.val = None
        self.tag = tag


class Res:
    __slots__ = ("w", "r")

    def __init__(self):
        self.w = None
        self.r = []


def succeed(new, *old):
    rs = []
    for o in old:
        rs.extend(o.r)
        if o.w is not None:
            rs.append(o.w)
    new.r = rs
    new.w = None


class Sched:
    def __init__(self):
        self.events = []
        self.last_per_eng = {}
        self.pending_fence = {}

    def ev(self, eng, fn, deps=(), reads=(), writes=(), sem=None, incv=1,
           tag=""):
        dl = [d for d in deps if d is not None]
        if eng in self.pending_fence:
            dl.extend(self.pending_fence.pop(eng))
        for r in reads:
            if r.w is not None:
                dl.append(r.w)
        e = Ev(eng, fn, dl, sem if sem is not None else eng, incv, tag)
        for r in reads:
            r.r.append(e)
        for w in writes:
            e.deps.extend(w.r)
            if w.w is not None:
                e.deps.append(w.w)
            w.w = e
            w.r = []
        self.events.append(e)
        self.last_per_eng[eng] = e
        return e

    def fence(self):
        lasts = [e for e in self.last_per_eng.values()]
        for eng in ("pe", "dve", "act", "pool", "sp"):
            self.pending_fence[eng] = list(lasts)

    def finalize_plan(self):
        for e in self.events:
            e.deps = [d for d in e.deps if d is not e]
            for d in e.deps:
                d.needed = True
            if e.sem != e.eng:
                e.needed = True
        counts = {}
        for e in self.events:
            if e.needed:
                counts[e.sem] = counts.get(e.sem, 0) + e.incv
                e.val = counts[e.sem]

    def emit(self, eng_name, eng, sems):
        hw = {}
        for e in self.events:
            if e.eng != eng_name:
                continue
            for d in e.deps:
                assert d.needed, (e.tag, d.tag)
                if hw.get(d.sem, -1) < d.val:
                    eng.wait_ge(sems[d.sem], d.val)
                    hw[d.sem] = d.val
            inst = e.fn(eng)
            if e.needed:
                inst.then_inc(sems[e.sem], e.incv)
                if eng_name == "pool" and e.sem == "pool":
                    eng.wait_ge(sems["pool"], e.val)
                    hw["pool"] = e.val


class Arena:
    def __init__(self, lo, hi):
        self.lo, self.hi, self.cur = lo, hi, lo

    def take(self, nbytes):
        off = (self.cur + 31) & ~31
        assert off + nbytes <= self.hi, (off, nbytes, self.hi)
        self.cur = off + nbytes
        return off

    def reset(self):
        self.cur = self.lo


def build_nc(debug=0, phases=99, p3sub=9):
    p3sub = 9 if p3sub == 9 else p3sub
    nc = bass.Bass("TRN2", target_bir_lowering=False, debug=False)
    s = Sched()
    KB = 1024
    dbg_outs = {}

    dram = lambda n, shp, dt=F32, kind="ExternalInput": nc.dram_tensor(
        n, shp, dt, kind=kind).ap()
    d_u = dram("u", [D, TE])
    d_wxbc = dram("wxbc", [128, NJX * NK * 128])
    d_wz = dram("wz", [128, NJZ * NK * 128])
    d_wa = dram("wa", [128, NK * 32])
    d_wout = dram("wout", [128, NK * NK * 128])
    d_wr = dram("wr", [128, NK * 8])
    d_wg = dram("wg", [E, 128, NJF * NK * 128])
    d_wu = dram("wu", [E, 128, NJF * NK * 128])
    d_wd = dram("wd", [128, NK * E * NJF * 128])
    d_cw = dram("cw", [128, NJX * 4])
    d_cb = dram("cb", [128, NJX])
    d_zb = dram("zb", [128, NJZ])
    d_drep = dram("drep", [128, NK])
    d_smask = dram("smask", [128, 1])
    d_o = dram("o", [D, T], kind="ExternalOutput")
    d_ccin = dram("cc_in", [128, NC * 64], BF16, kind="Internal")
    d_ccout = dram("cc_out", [256, NC * 64], BF16, kind="Internal")

    KBo = 1024

    SBUF_BASE = 16512
    def sb(name, shape, dt, off):
        nb = int(np.prod(shape[1:])) * (2 if dt == BF16 else 4)
        assert SBUF_BASE + off + nb <= 229344, (name, off, nb)
        return nc.alloc_sbuf_tensor_at(name, shape, dt, offset=SBUF_BASE + off)

    identb = sb("identb", [128, 128], BF16, 0)
    identf = sb("identf", [128, 128], F32, 256)
    ones1f = sb("ones1f", [1, 128], F32, 768)
    ones1b = sb("ones1b", [1, 128], BF16, 1280)
    onescol = sb("onescol", [128, 1], BF16, 1536)
    u01 = sb("u01", [128, 128], F32, 1568)
    c_eps = sb("c_eps", [1, 1], F32, 2080)
    c_nhalf = sb("c_nhalf", [1, 1], F32, 2112)
    c_phalf = sb("c_phalf", [1, 1], F32, 4064)
    onescolf = sb("onescolf", [128, 1], F32, 2144)
    cw_sb = sb("cw_sb", [128, NJX * 4], F32, 2176)
    cb_sb = sb("cb_sb", [128, NJX], F32, 2944)
    zb_sb = sb("zb_sb", [128, NJZ], F32, 3136)
    drep_sb = sb("drep_sb", [128, NK], F32, 3200)
    smask_sb = sb("smask_sb", [128, 1], F32, 3264)
    wr_sb = sb("wr_sb", [128, NK * 8], BF16, 3296)
    SL = 4480
    wslab = [sb(f"wslab{i}", [128, 2 * NK * 128], BF16, SL + i * 8 * KBo)
             for i in range(2)]
    wos = [sb(f"wos{i}", [128, NK * 128], BF16, SL + i * 4 * KBo)
           for i in range(2)]
    din = sb("din", [128, 256], F32, SL + 8 * KBo)
    rcol = sb("rcol", [128, 256], F32, SL + 9 * KBo)
    ecol = sb("ecol", [128, 256], F32, SL + 10 * KBo)
    dbnd = sb("dbnd", [128, 256], F32, SL + 11 * KBo)
    s_inb = sb("s_inb", [128, NC * 64], BF16, SL + 12 * KBo)
    scores = sb("scores", [8, T], F32, SL + 0 * KBo)
    sigbc = sb("sigbc", [128, T], BF16, SL + 4 * KBo)
    m2row = sb("m2row", [1, T], BF16, SL + 6 * KBo)
    wbc = sb("wbc", [128, T], BF16, SL + 8 * KBo)
    sttok = sb("sttok", [128, 64], F32, SL + 10 * KBo)
    wtok = sb("wtok", [128, 64], F32, SL + 10 * KBo + 256)
    wrow = [sb(f"wrow{i}", [1, 128], F32, SL + 10 * KBo + 512 + i * 512)
            for i in range(2)]
    r_m1 = sb("r_m1", [128, 1], F32, SL + 11 * KBo + 512)
    r_nm1 = sb("r_nm1", [128, 1], F32, SL + 11 * KBo + 544)
    r_m2 = sb("r_m2", [128, 1], F32, SL + 11 * KBo + 576)
    r_em = sb("r_em", [128, 1], F32, SL + 11 * KBo + 608)
    r_p1 = sb("r_p1", [128, 1], F32, SL + 11 * KBo + 640)
    r_p2 = sb("r_p2", [128, 1], F32, SL + 11 * KBo + 672)
    r_eq1 = sb("r_eq1", [128, 8], F32, SL + 11 * KBo + 704)
    r_eq2 = sb("r_eq2", [128, 8], F32, SL + 11 * KBo + 736)
    r_sc2 = sb("r_sc2", [128, 8], F32, SL + 11 * KBo + 768)
    m2f = sb("m2f_row", [1, T], F32, SL + 12 * KBo)
    rs_col = sb("rs_col", [8, 1], F32, SL + 11 * KBo + 800)
    nrs_row = sb("nrs_row", [1, 8], F32, SL + 11 * KBo + 832)
    wrf_sb = sb("wrf_sb", [128, NK * 8], F32, 3552)
    S1 = 20864
    ln1 = sb("ln1", [128, NK * TE], BF16, S1)
    y_tok = sb("y_tok", [128, NG * D], BF16, S1)
    ln2 = sb("ln2", [128, NK * T], BF16, S1)
    S2 = 53760
    z_sb = sb("z_sb", [128, NK * T], BF16, S2)
    mslab = [sb(f"mslab{i}", [128, 3 * NK * 128], BF16, S2 + i * 12 * KBo)
             for i in range(2)]
    S3 = 86528
    u_bf = sb("u_bf", [128, NK * TE], BF16, S3)
    x_cm = sb("x_cm", [128, NK * T], BF16, S3)
    b_cm = sb("b_cm", [128, NK * T], BF16, S3 + 32 * KBo)
    c_cm = sb("c_cm", [128, NK * T], BF16, S3 + 64 * KBo)
    x1 = sb("x1", [128, NK * T], F32, S3)
    x1bf = sb("x1bf", [128, NK * T], BF16, S3 + 64 * KBo)
    hidden = sb("hidden", [128, E * NJF * T], BF16, S3)
    S4 = 184832
    uring = [sb(f"uring{i}", [128, TE], F32, S4 + i * 4128) for i in range(2)]
    xpre = [sb(f"xpre{i}", [128, TE], BF16, S4 + i * 2080) for i in range(2)]
    ctmp = [sb(f"ctmp{i}", [128, T], F32, S4 + 4160 + i * 4096)
            for i in range(2)]
    x_tok = sb("x_tok", [128, 2048], BF16, S4)
    b_tok = sb("b_tok", [128, 2048], BF16, S4 + 4 * KBo)
    xd = sb("xd", [128, 2048], BF16, S4 + 8 * KBo)
    x0r = [sb(f"x0r{i}", [128, T], F32, S4 + i * 4096) for i in range(2)]
    sgr = [sb(f"sgr{i}", [128, 512], F32, S4 + i * 2048) for i in range(2)]
    h1r = [sb(f"h1r{i}", [128, 512], F32, S4 + 4096 + i * 2048)
           for i in range(2)]
    tout = [sb(f"tout{i}", [128, 512], F32, S4 + i * 2048) for i in range(2)]
    ostg = [sb(f"ostg{i}", [128, 512], F32, S4 + 4096 + i * 2048)
            for i in range(2)]
    S5 = 197184
    sqr = [sb(f"sqr{i}", [128, 512], BF16, S5 + i * KBo) for i in range(2)]
    mrow = sb("mrow", [1, 512], F32, S5 + 2 * KBo)
    srow = sb("srow", [1, 512], F32, S5 + 4 * KBo)
    scrow = sb("scrow", [1, 512], F32, S5 + 6 * KBo)
    mbc = sb("mbc", [128, TE], BF16, S5 + 8 * KBo)
    sbc = sb("sbc", [128, TE], BF16, S5 + 8 * KBo + 2080)
    wa_sb = sb("wa_sb", [128, NK * 32], BF16, S5 + 12 * KBo)
    a_sb = sb("a_sb", [32, T], F32, S5)
    zeros32 = sb("zeros32", [32, T], F32, S5 + 4 * KBo)
    g_sb = sb("g_sb", [32, T], F32, S5 + 8 * KBo)
    gt_sb = sb("gt_sb", [128, 256], F32, S5 + 12 * KBo)
    gmrow = sb("gmrow", [1, 256], F32, S5 + 4 * KBo)
    gerow = sb("gerow", [1, 256], F32, S5 + 5 * KBo)
    gmnrow = sb("gmnrow", [1, 256], F32, S5 + 6 * KBo)
    arow = sb("arow", [1, 256], F32, S5 + 7 * KBo)
    expabc = sb("expabc", [128, 256], F32, S5)
    gbc = [sb(f"gbc{i}", [128, 32], F32, S5 + 1 * KBo + i * 128)
           for i in range(2)]
    mtr = [sb(f"mtr{i}", [128, 128], BF16, S5 + 8 * KBo + i * 256)
           for i in range(16)]
    s_bf = sb("s_bf", [128, NC * 64], BF16, S5 + 1 * KBo + 512)
    ybr = [sb(f"ybr{i}", [128, 64], F32, S5 + 3 * KBo + 512 + i * 256)
           for i in range(4)]
    yfr = [sb(f"yfr{i}", [128, 128], F32, S5 + 4 * KBo + 512 + i * 512)
           for i in range(2)]
    mbc2 = sb("mbc2", [128, T], BF16, S5)
    sbc2 = sb("sbc2", [128, T], BF16, S5 + 2 * KBo)
    sq2 = [sb(f"sq2_{i}", [128, 512], BF16, S5 + 4 * KBo + i * KBo)
           for i in range(2)]
    m512 = sb("m512", [1, 512], F32, S5 + 6 * KBo)
    s2row = sb("s2row", [1, 512], F32, S5 + 8 * KBo)
    scr2 = sb("scr2", [1, 512], F32, S5 + 10 * KBo)

    ctx = ExitStack()
    pf = [ctx.enter_context(nc.psum_tensor(f"pf{i}", [128, 512], F32))
          for i in range(6)]
    pbt = [ctx.enter_context(nc.psum_tensor(f"pbt{i}", [128, 128], BF16))
           for i in range(2)]

    sem_names = ["pe", "dve", "act", "pool", "sp",
                 "c_dma", "u0", "u1", "ws0", "ws1", "ms0", "ms1",
                 "ms00", "ms01", "ms02", "ms10", "ms11", "ms12",
                 "m800", "m801", "m802", "m810", "m811", "m812",
                 "x00", "x01", "o0", "o1", "cc", "ccd", "ge_dma"]
    if debug:
        sem_names += ["dbg_dma", "dbg_gdma"]
    sems = {n: nc.alloc_semaphore(n) for n in sem_names}

    R = {}

    def res(key):
        if key not in R:
            R[key] = Res()
        return R[key]

    cload = []
    prevld = None
    for dst, src in [(wr_sb, d_wr), (wr_sb, d_wr), (cw_sb, d_cw),
                     (cb_sb, d_cb), (zb_sb, d_zb), (drep_sb, d_drep),
                     (smask_sb, d_smask)]:
        prevld = s.ev("pool", lambda g, d=dst, sr=src: g.dma_start(
            out=d[:], in_=sr), deps=[prevld], sem="c_dma", incv=16,
            tag="cload")
        cload.append(prevld)
    e_ms = []
    e_ms.append(s.ev("pool", lambda g: g.memset(identb[:], 1.0), tag="ms"))
    e_ms.append(s.ev("pool", lambda g: g.memset(identf[:], 1.0), tag="ms"))
    e_ms.append(s.ev("pool", lambda g: g.memset(ones1f[:], 1.0), tag="ms"))
    e_ms.append(s.ev("pool", lambda g: g.memset(ones1b[:], 1.0), tag="ms"))
    e_ms.append(s.ev("pool", lambda g: g.memset(onescol[:], 1.0), tag="ms"))
    e_ms.append(s.ev("pool", lambda g: g.memset(u01[:], 1.0), tag="ms"))
    e_ms.append(s.ev("pool", lambda g: g.nop(), tag="ms"))
    e_ms.append(s.ev("pool", lambda g: g.memset(c_eps[:], EPS), tag="ms"))
    e_ms.append(s.ev("pool", lambda g: g.memset(c_nhalf[:], -0.5), tag="ms"))
    e_ms.append(s.ev("pool", lambda g: g.memset(onescolf[:], 1.0), tag="ms"))
    e_cph = s.ev("pool", lambda g: g.memset(c_phalf[:], 0.5), tag="ms")
    e_ms.append(e_cph)
    e_ones1f, e_ones1b, e_onescol = e_ms[2], e_ms[3], e_ms[4]
    e_ceps, e_cnh, e_onescolf = e_ms[7], e_ms[8], e_ms[9]
    e_xx, e_wr, e_cw, e_cb, e_zbld, e_drepld, e_smld = cload
    e_wrf = s.ev("pool", lambda g: g.dma_start(out=wrf_sb[:], in_=d_wr),
                 deps=[e_smld], sem="c_dma", incv=16, tag="wrfld")
    e_idb = s.ev("pool", lambda g: g.affine_select(
        identb[:], identb[:], pattern=[[1, 128]], compare_op=ALU.is_equal,
        fill=0.0, base=0, channel_multiplier=-1), deps=e_ms, tag="idb")
    e_idf = s.ev("pool", lambda g: g.affine_select(
        identf[:], identf[:], pattern=[[1, 128]], compare_op=ALU.is_equal,
        fill=0.0, base=0, channel_multiplier=-1), deps=[e_idb], tag="idf")
    e_u01 = s.ev("pool", lambda g: g.affine_select(
        u01[:], u01[:], pattern=[[1, 128]], compare_op=ALU.is_ge,
        fill=0.0, base=0, channel_multiplier=-1), deps=[e_idf], tag="u01")
    e_const = cload + [e_u01, e_idf, e_idb]

    e_ucast = []
    for k in range(NK):
        sl = k % 2
        e_ld = s.ev("sp", lambda sp, k=k, sl=sl: sp.dma_start(
            out=uring[sl][:], in_=d_u[k * 128:(k + 1) * 128, :]),
            writes=[res(("uring", sl))], sem=f"u{sl}", incv=16, tag=f"uld{k}")
        e_c = s.ev("act", lambda a, k=k, sl=sl: a.activation(
            u_bf[:, k * TE:(k + 1) * TE], uring[sl][:], AF.Copy),
            deps=[e_ld], reads=[res(("uring", sl))],
            writes=[res(("ubf", k))], tag=f"ucast{k}")
        e_ucast.append(e_c)
    CGS = [(0, 512), (512, 1024), (1024, 1027)]
    e_nrm_all = []
    for cg, (c0, c1) in enumerate(CGS):
        n = c1 - c0
        prev_mm = None
        for k in range(NK):
            sl = k % 2
            e_sq = s.ev("act", lambda a, k=k, c0=c0, c1=c1, sl=sl: a.activation(
                sqr[sl][:, 0:c1 - c0], u_bf[:, k * TE + c0:k * TE + c1],
                AF.Square), deps=[e_ucast[k]], writes=[res(("sqr", sl))],
                tag=f"sq{cg}_{k}")
            e_m1 = s.ev("pe", lambda t, k=k, c0=c0, c1=c1, st=(k == 0), sp=(k == NK - 1): t.matmul(
                pf[0][0:1, 0:c1 - c0], onescol[:], u_bf[:, k * TE + c0:k * TE + c1],
                start=st, stop=sp), deps=[e_ucast[k], e_onescol],
                reads=[], writes=([res(("pf", 0))] if k == 0 else []),
                tag=f"mmM{cg}_{k}")
            e_m2 = s.ev("pe", lambda t, k=k, n=n, sl=sl, st=(k == 0), sp=(k == NK - 1): t.matmul(
                pf[1][0:1, 0:n], onescol[:], sqr[sl][:, 0:n],
                start=st, stop=sp), deps=[e_sq],
                reads=[res(("sqr", sl))],
                writes=([res(("pf", 1))] if k == 0 else []),
                tag=f"mmS{cg}_{k}")
            prev_mm = (e_m1, e_m2)
        e_em = s.ev("dve", lambda v, n=n: v.tensor_scalar_mul(
            mrow[:, 0:n], pf[0][0:1, 0:n], INV_D), deps=[prev_mm[0]],
            reads=[res(("pf", 0))], writes=[res("mrow")], tag=f"evm{cg}")
        e_es = s.ev("dve", lambda v, n=n: v.tensor_scalar_mul(
            srow[:, 0:n], pf[1][0:1, 0:n], INV_D), deps=[prev_mm[1]],
            reads=[res(("pf", 1))], writes=[res("srow")], tag=f"evs{cg}")
        e_msq = s.ev("act", lambda a, n=n: a.activation(
            scrow[:, 0:n], mrow[:, 0:n], AF.Square), deps=[e_em],
            writes=[res("scrow")], tag=f"msq{cg}")
        e_var = s.ev("dve", lambda v, n=n: v.tensor_sub(
            srow[:, 0:n], srow[:, 0:n], scrow[:, 0:n]), deps=[e_es, e_msq],
            writes=[res("srow")], tag=f"var{cg}")
        e_ln = s.ev("act", lambda a, n=n: a.activation(
            srow[:, 0:n], srow[:, 0:n], AF.Ln, bias=c_eps[:]),
            deps=[e_var, e_ceps], writes=[res("srow")], tag=f"vln{cg}")
        e_rs = s.ev("act", lambda a, n=n: a.activation(
            srow[:, 0:n], srow[:, 0:n], AF.Exp, scale=c_nhalf[:]),
            deps=[e_ln, e_cnh], writes=[res("srow")], tag=f"rstd{cg}")
        e_bm = s.ev("pe", lambda t, n=n: t.matmul(
            pf[2][:, 0:n], ones1f[:], mrow[:, 0:n], start=True, stop=True),
            deps=[e_em, e_ones1f], writes=[res(("pf", 2))], tag=f"bcm{cg}")
        e_vm = s.ev("dve", lambda v, n=n, c0=c0: v.tensor_copy(
            mbc[:, c0:c0 + n], pf[2][:, 0:n]), deps=[e_bm],
            reads=[res(("pf", 2))], writes=[res("mbc")], tag=f"bmc{cg}")
        e_bs = s.ev("pe", lambda t, n=n: t.matmul(
            pf[3][:, 0:n], ones1f[:], srow[:, 0:n], start=True, stop=True),
            deps=[e_rs], writes=[res(("pf", 3))], tag=f"bcs{cg}")
        e_vs = s.ev("dve", lambda v, n=n, c0=c0: v.tensor_copy(
            sbc[:, c0:c0 + n], pf[3][:, 0:n]), deps=[e_bs],
            reads=[res(("pf", 3))], writes=[res("sbc")], tag=f"bsc{cg}")
        e_nrm_all += [e_vm, e_vs]
    e_ln1 = []
    for k in range(NK):
        sl = k % 2
        e_t = s.ev("dve", lambda v, k=k, sl=sl: v.tensor_sub(
            uring[sl][:], u_bf[:, k * TE:(k + 1) * TE], mbc[:]),
            deps=e_nrm_all + [e_ucast[k]], writes=[res(("uring", sl))],
            reads=[res("mbc")], tag=f"nsub{k}")
        e_n = s.ev("dve", lambda v, k=k, sl=sl: v.tensor_mul(
            ln1[:, k * TE:(k + 1) * TE], uring[sl][:], sbc[:]),
            deps=[e_t], reads=[res(("uring", sl)), res("sbc")],
            writes=[res(("ln1", k))], tag=f"nmul{k}")
        e_ln1.append(e_n)
    if debug >= 2:
        dbg_outs["ln1"] = None

    s.fence()

    e_wa = s.ev("pool", lambda g: g.dma_start(out=wa_sb[:], in_=d_wa),
                deps=[e_smld], sem="c_dma", incv=16, tag="wald")
    e_apsum = []
    for cg in range(2):
        c0, c1 = 3 + cg * 512, 3 + (cg + 1) * 512
        last = None
        for k in range(NK):
            last = s.ev("pe", lambda t, k=k, c0=c0, c1=c1, st=(k == 0), sp=(k == NK - 1), cg=cg: t.matmul(
                pf[4][0:32, 0:512],
                wa_sb[:, k * 32:(k + 1) * 32],
                ln1[:, k * TE + c0:k * TE + c1], start=st, stop=sp),
                deps=[e_ln1[k], e_wa],
                writes=([res(("pf", 4))] if k == 0 else []), tag=f"Amm{cg}_{k}")
        e_ev = s.ev("dve", lambda v, cg=cg: v.tensor_copy(
            a_sb[:, cg * 512:(cg + 1) * 512], pf[4][0:32, 0:512]),
            deps=[last], reads=[res(("pf", 4))], writes=[res("araw")],
            tag=f"Aev{cg}")
        e_apsum.append(e_ev)
    e_abs = s.ev("act", lambda a: a.activation(
        zeros32[:], a_sb[:], AF.Abs), deps=e_apsum,
        reads=[res("araw")], writes=[res("z32")], tag="abs")
    e_exn = s.ev("act", lambda a: a.activation(
        zeros32[:], zeros32[:], AF.Exp, scale=-1.0), deps=[e_abs],
        writes=[res("z32")], tag="exn")
    e_lnp = s.ev("act", lambda a: a.activation(
        zeros32[:], zeros32[:], AF.Ln, bias=1.0), deps=[e_exn],
        writes=[res("z32")], tag="lnp")
    e_rel = s.ev("dve", lambda v: v.tensor_single_scalar(
        g_sb[:], a_sb[:], 0.0, ALU.max), deps=e_apsum, reads=[res("araw")],
        writes=[res("g32")], tag="rel")
    e_a = s.ev("dve", lambda v: v.scalar_tensor_tensor(
        a_sb[:], g_sb[:], -1.0, zeros32[:], op0=ALU.mult, op1=ALU.subtract),
        deps=[e_lnp, e_rel], reads=[res("z32"), res("g32")],
        writes=[res("araw")], tag="mka")
    e_z32 = s.ev("pool", lambda g: g.memset(zeros32[:], 0.0), deps=[e_a],
                 writes=[res("z32")], tag="z32")
    e_scan = s.ev("dve", lambda v: v.tensor_tensor_scan(
        g_sb[:], a_sb[:], zeros32[:], 0.0, ALU.add, ALU.add),
        deps=[e_z32, e_a], reads=[res("araw"), res("z32")],
        writes=[res("g32")], tag="scan")
    e_gt = []
    for gg in range(8):
        e_tr = s.ev("pe", lambda t, gg=gg: t.transpose(
            pf[5][:, 0:32],
            g_sb[:, gg * 128:(gg + 1) * 128], identf[0:32, 0:32]),
            deps=[e_scan, e_idf], writes=[res(("pf", 5))],
            tag=f"gtr{gg}")
        e_gv = s.ev("dve", lambda v, gg=gg: v.tensor_copy(
            gt_sb[:, gg * 32:(gg + 1) * 32], pf[5][:, 0:32]),
            deps=[e_tr], reads=[res(("pf", 5))], writes=[res("gt")],
            tag=f"gte{gg}")
        e_gt.append(e_gv)
    e_gm = s.ev("sp", lambda sp: sp.dma_start(
        out=gmrow[:], in_=gt_sb[63:64, :]),
        deps=e_gt, sem="ge_dma", incv=16, tag="gm")
    e_ge = s.ev("sp", lambda sp: sp.dma_start(
        out=gerow[:], in_=gt_sb[127:128, :]),
        deps=[e_gm], sem="ge_dma", incv=16, tag="ge")
    gmn_r = res("gmnrow")
    e_gn0 = s.ev("dve", lambda v: v.tensor_copy(
        gmnrow[:, 0:224], gmrow[:, 32:256]), deps=[e_gm, e_ge],
        writes=[gmn_r], tag="gn0")
    e_gn1 = s.ev("dve", lambda v: v.tensor_copy(
        gmnrow[:, 224:256], gerow[:, 224:256]), deps=[e_ge],
        writes=[gmn_r], tag="gn1")
    ar = res("arow")
    e_ar0 = s.ev("dve", lambda v: v.tensor_copy(arow[:, 0:32], gmnrow[:, 0:32]),
                 deps=[e_gn0, e_gn1], reads=[gmn_r], writes=[ar], tag="ar0")
    e_ar1 = s.ev("dve", lambda v: v.tensor_sub(
        arow[:, 32:256], gmnrow[:, 32:256], gmnrow[:, 0:224]),
        deps=[e_ar0], reads=[gmn_r], writes=[ar], tag="ar1")
    e_expa = s.ev("act", lambda a: a.activation(arow[:], arow[:], AF.Exp),
                  deps=[e_ar1], writes=[ar], tag="expa")
    e_eab = s.ev("pe", lambda t: t.matmul(
        pf[5][:, 0:256], ones1f[:], arow[:], start=True, stop=True),
        deps=[e_expa], writes=[res(("pf", 5))], tag="eabm")
    e_eabv = s.ev("dve", lambda v: v.tensor_copy(expabc[:], pf[5][:, 0:256]),
                  deps=[e_eab], reads=[res(("pf", 5))], writes=[res("expabc")],
                  tag="eabv")
    e_zdone = []
    nslab_z = NJZ // 2
    slab_consumed = [None, None]
    slab_loaded = [None, None]
    for sidx in range(nslab_z):
        buf = sidx % 2
        e_ld = s.ev("pool", lambda g, sidx=sidx, buf=buf: g.dma_start(
            out=wslab[buf][:], in_=d_wz[:, sidx * 2 * NK * 128:(sidx + 1) * 2 * NK * 128]),
            deps=[slab_consumed[buf]], sem=f"ws{buf}", incv=16,
            tag=f"zsl{sidx}")
        slab_loaded[buf] = e_ld
        for jloc in range(2):
            jz = sidx * 2 + jloc
            for cg in range(2):
                c0 = 3 + cg * 512
                bank = (jz * 2 + cg) % 2
                last = None
                for k in range(NK):
                    last = s.ev("pe", lambda t, jloc=jloc, k=k, c0=c0, bank=bank, buf=buf, st=(k == 0), sp=(k == NK - 1): t.matmul(
                        pf[bank][:, 0:512],
                        wslab[buf][:, (jloc * NK + k) * 128:(jloc * NK + k + 1) * 128],
                        ln1[:, k * TE + c0:k * TE + c0 + 512],
                        start=st, stop=sp),
                        deps=[e_ld, e_ln1[k]],
                        writes=([res(("pf", bank))] if k == 0 else []),
                        tag=f"zmm{jz}_{cg}_{k}")
                slab_consumed[buf] = last
                e_ev = s.ev("act", lambda a, jz=jz, cg=cg, bank=bank: a.activation(
                    z_sb[:, jz * T + cg * 512:jz * T + (cg + 1) * 512],
                    pf[bank][:, 0:512], AF.Silu, bias=zb_sb[:, jz:jz + 1]),
                    deps=[last, e_zbld], reads=[res(("pf", bank))],
                    writes=[res(("z", jz))], tag=f"zev{jz}_{cg}")
                e_zdone.append(e_ev)

    for sidx in range(NJX // 2):
        buf = sidx % 2
        e_ld = s.ev("pool", lambda g, sidx=sidx, buf=buf: g.dma_start(
            out=wslab[buf][:], in_=d_wxbc[:, sidx * 2 * NK * 128:(sidx + 1) * 2 * NK * 128]),
            deps=[slab_consumed[buf]], sem=f"ws{buf}", incv=16,
            tag=f"xsl{sidx}")
        slab_loaded[buf] = e_ld
        for jloc in range(2):
            jc = sidx * 2 + jloc
            pslot = jc % 2
            e_evs = []
            for cg, (c0, c1) in enumerate(CGS):
                bank = cg % 2
                n = c1 - c0
                last = None
                for k in range(NK):
                    last = s.ev("pe", lambda t, jloc=jloc, k=k, c0=c0, n=n, bank=bank, buf=buf, st=(k == 0), sp=(k == NK - 1): t.matmul(
                        pf[bank][:, 0:n],
                        wslab[buf][:, (jloc * NK + k) * 128:(jloc * NK + k + 1) * 128],
                        ln1[:, k * TE + c0:k * TE + c0 + n],
                        start=st, stop=sp),
                        deps=[e_ld, e_ln1[k]],
                        writes=([res(("pf", bank))] if k == 0 else []),
                        tag=f"xmm{jc}_{cg}_{k}")
                slab_consumed[buf] = last
                e_ev = s.ev("act", lambda a, c0=c0, n=n, bank=bank, pslot=pslot: a.activation(
                    xpre[pslot][:, c0:c0 + n], pf[bank][:, 0:n], AF.Copy),
                    deps=[last], reads=[res(("pf", bank))],
                    writes=[res(("xpre", pslot))], tag=f"xev{jc}_{cg}")
                e_evs.append(e_ev)
            tgt = [x_cm, b_cm, c_cm][jc // 16]
            jj = jc % 16
            e_c1 = s.ev("dve", lambda g, jc=jc, pslot=pslot: g.tensor_scalar(
                ctmp[pslot][:], xpre[pslot][:, 0:T], cw_sb[:, jc * 4:jc * 4 + 1],
                cb_sb[:, jc:jc + 1], ALU.mult, ALU.add),
                deps=e_evs + [e_cw, e_cb], reads=[res(("xpre", pslot))],
                writes=[res(("ctmp", pslot))], tag=f"cv0_{jc}")
            e_c2 = s.ev("dve", lambda g, jc=jc, pslot=pslot: g.scalar_tensor_tensor(
                ctmp[pslot][:], xpre[pslot][:, 1:T + 1],
                cw_sb[:, jc * 4 + 1:jc * 4 + 2], ctmp[pslot][:],
                op0=ALU.mult, op1=ALU.add), deps=[e_c1],
                reads=[res(("xpre", pslot))], writes=[res(("ctmp", pslot))],
                tag=f"cv1_{jc}")
            e_c3 = s.ev("dve", lambda g, jc=jc, pslot=pslot: g.scalar_tensor_tensor(
                ctmp[pslot][:], xpre[pslot][:, 2:T + 2],
                cw_sb[:, jc * 4 + 2:jc * 4 + 3], ctmp[pslot][:],
                op0=ALU.mult, op1=ALU.add), deps=[e_c2],
                reads=[res(("xpre", pslot))], writes=[res(("ctmp", pslot))],
                tag=f"cv2_{jc}")
            e_c4 = s.ev("dve", lambda g, jc=jc, pslot=pslot, tgt=tgt, jj=jj: g.scalar_tensor_tensor(
                tgt[:, jj * T:(jj + 1) * T], xpre[pslot][:, 3:T + 3],
                cw_sb[:, jc * 4 + 3:jc * 4 + 4], ctmp[pslot][:],
                op0=ALU.mult, op1=ALU.add), deps=[e_c3],
                reads=[res(("xpre", pslot)), res(("ctmp", pslot))],
                writes=[res(("cm", jc))], tag=f"cv3_{jc}")

    e_dbn = s.ev("act", lambda a: a.activation(dbnd[:], gt_sb[:], AF.Exp),
                 deps=e_gt + [slab_consumed[0], slab_consumed[1]],
                 reads=[res("gt")], writes=[res("dbnd")], tag="dbnd")
    e_decay = [e_dbn]
    ecps = [res(("ecp", c)) for c in range(8)]
    e_b = s.ev("pe", lambda t: t.matmul(
        pf[5][:, 0:256], ones1f[:], gmrow[:, 0:256],
        start=True, stop=True), deps=[e_gm, e_ones1f],
        writes=[res(("pf", 5))], tag="gb")
    e_dp = s.ev("dve", lambda v: v.tensor_sub(
        ecol[:, 0:256], gt_sb[:, 0:256], pf[5][:, 0:256]),
        deps=[e_b], reads=[res(("pf", 5)), res("gt")],
        writes=ecps, tag="dp")
    e_di = s.ev("act", lambda a: a.activation(
        din[:, 0:256], ecol[:, 0:256], AF.Exp),
        deps=[e_dp], reads=ecps, writes=[res("din")], tag="di")
    e_rc = s.ev("act", lambda a: a.activation(
        rcol[:, 0:256], ecol[:, 0:256], AF.Exp, scale=-1.0),
        deps=[e_dp], reads=ecps, writes=[res("rcol")], tag="rc")
    e_b2 = s.ev("pe", lambda t: t.matmul(
        pf[4][:, 0:256], ones1f[:], gmnrow[:, 0:256],
        start=True, stop=True), deps=[e_gn0, e_gn1, e_b],
        reads=[gmn_r], writes=[res(("pf", 4))], tag="ge")
    e_ep = s.ev("dve", lambda v: v.tensor_sub(
        ecol[:, 0:256], pf[4][:, 0:256], gt_sb[:, 0:256]),
        deps=[e_b2, e_di, e_rc], reads=[res(("pf", 4)), res("gt")],
        writes=ecps, tag="ep")
    e_ec = s.ev("act", lambda a: a.activation(
        ecol[:, 0:256], ecol[:, 0:256], AF.Exp),
        deps=[e_ep], writes=ecps, tag="ec")
    for c in range(8):
        res(("ecp", c)).w = e_ec
    e_decay += [e_di, e_rc, e_ec]

    e_szb = s.ev("pool", lambda g_: g_.memset(s_bf[:], 0.0),
                 deps=[e_decay[-1], e_eabv, e_scan] + e_apsum, tag="szb")
    for h in range(NH):
        res(("sbf", h)).w = e_szb

    wos_consumed = [slab_consumed[0], slab_consumed[0]]
    e_wosld = {}
    e_x0ld = {}
    for jf in range(2):
        buf = jf % 2
        e_wosld[jf] = s.ev("pool", lambda g_, jf=jf, buf=buf: g_.dma_start(
            out=wos[buf][:],
            in_=d_wout[:, jf * NK * 128:(jf + 1) * NK * 128]),
            deps=[wos_consumed[buf]], sem=f"ws{buf}", incv=16,
            tag=f"osl{jf}")

    s.fence()
    if phases <= 2:
        return locals()

    e_supd_last = {}
    bank_rd = {i: [] for i in range(8)}
    bank_wr = {i: None for i in range(8)}

    def epoch_dep(bank):
        d = bank_rd[bank] + ([bank_wr[bank]] if bank_wr[bank] else [])
        bank_rd[bank] = []
        return d

    for c in range(NG):
        e_xtok_cp = {}
        for hp in range(16):
            pb = hp % 2
            e_t = s.ev("pe", lambda t, hp=hp, c=c, pb=pb: t.transpose(
                pbt[pb][:], x_cm[:, hp * T + c * 128:hp * T + c * 128 + 128],
                identb[:]), deps=[e_idb] + epoch_dep(6 + pb),
                reads=[res(("cm", hp))], tag=f"trx{c}_{hp}")
            e_cp = s.ev("act", lambda a, hp=hp, pb=pb: a.activation(
                x_tok[:, hp * 128:(hp + 1) * 128], pbt[pb][:], AF.Copy),
                deps=[e_t], writes=[res(("xtok", c, hp))], tag=f"trxe{c}_{hp}")
            bank_rd[6 + pb].append(e_cp)
            e_xtok_cp[hp] = e_cp
            e_tb = s.ev("pe", lambda t, hp=hp, c=c, pb=pb: t.transpose(
                pbt[pb][:], b_cm[:, hp * T + c * 128:hp * T + c * 128 + 128],
                identb[:]), deps=[e_idb] + epoch_dep(6 + pb),
                reads=[res(("cm", 16 + hp))], tag=f"trb{c}_{hp}")
            e_cb2 = s.ev("dve", lambda v, hp=hp, pb=pb: v.tensor_copy(
                b_tok[:, hp * 128:(hp + 1) * 128], pbt[pb][:]),
                deps=[e_tb], writes=[res(("btok", c, hp))], tag=f"trbe{c}_{hp}")
            bank_rd[6 + pb].append(e_cb2)
        for h in range(NH):
            hp, hh = h // 2, h % 2
            e_xd = s.ev("dve", lambda v, h=h, hp=hp, hh=hh, c=c: v.tensor_scalar_mul(
                xd[:, hp * 128 + hh * 64:hp * 128 + (hh + 1) * 64],
                x_tok[:, hp * 128 + hh * 64:hp * 128 + (hh + 1) * 64],
                ecol[:, c * 32 + h:c * 32 + h + 1]),
                deps=[e_xtok_cp[hp], R[("ecp", c)].w],
                reads=[res(("xtok", c, hp))],
                writes=[res(("xd", c, h))], tag=f"xd{c}_{h}")
        for half in range(2):
            hs = list(range(half * 16, half * 16 + 16))
            e_m1 = {}
            e_mte = {}
            e_y2 = {}
            e_s1 = {}
            LAG = 4
            mtbank = [pf[0], pf[1]]
            for i in range(16 + LAG):
                if i < 16:
                    h = hs[i]
                    hh = h % 2
                    bk = h % 2
                    mts = h % 16
                    ccol = (h // 2) * T + c * 128
                    e_m1[h] = s.ev("pe", lambda t, h=h, hh=hh, bk=bk, ccol=ccol: t.matmul(
                        mtbank[bk][:, 0:128],
                        b_cm[hh * 64:(hh + 1) * 64, ccol:ccol + 128],
                        c_cm[hh * 64:(hh + 1) * 64, ccol:ccol + 128],
                        start=True, stop=True),
                        deps=epoch_dep(bk) +
                        [R[("cm", 16 + h // 2)].w, R[("cm", 32 + h // 2)].w],
                        tag=f"m1_{c}_{h}")
                    e_mte[h] = s.ev("dve", lambda v, h=h, c=c, bk=bk, mts=mts: v.scalar_tensor_tensor(
                        mtr[mts][:], mtbank[bk][:, 0:128],
                        rcol[:, c * 32 + h:c * 32 + h + 1], u01[:],
                        op0=ALU.mult, op1=ALU.mult),
                        deps=[e_m1[h], e_u01, R["rcol"].w],
                        writes=[res(("mtr", mts))], tag=f"mte{c}_{h}")
                    bank_rd[bk].append(e_mte[h])
                if i >= LAG:
                    h = hs[i - LAG]
                    hp, hh = h // 2, h % 2
                    yb = 2 + h % 2
                    sbk = 4 + h % 2
                    qc = ((h // 2) % 8) * 64
                    mts = h % 16
                    ccol = (h // 2) * T + c * 128
                    ddy = epoch_dep(yb) if (h - half * 16) < 2 else []
                    dds = epoch_dep(sbk) if (h - half * 16) < 2 else []
                    e_ya = s.ev("pe", lambda t, h=h, hp=hp, hh=hh, yb=yb, qc=qc, mts=mts: t.matmul(
                        pf[yb][:, qc:qc + 64], mtr[mts][:],
                        x_tok[:, hp * 128 + hh * 64:hp * 128 + (hh + 1) * 64],
                        start=True, stop=False),
                        deps=ddy + [e_mte[h], e_xtok_cp[hp]],
                        reads=[res(("mtr", mts))], tag=f"y1_{c}_{h}")
                    e_y2[h] = s.ev("pe", lambda t, h=h, hh=hh, yb=yb, qc=qc, ccol=ccol: t.matmul(
                        pf[yb][:, qc:qc + 64],
                        c_cm[hh * 64:(hh + 1) * 64, ccol:ccol + 128],
                        s_bf[hh * 64:(hh + 1) * 64, (h // 2) * 64:(h // 2 + 1) * 64],
                        start=False, stop=True),
                        deps=[e_ya], reads=[res(("sbf", h))],
                        tag=f"y2_{c}_{h}")
                    e_s1[h] = s.ev("pe", lambda t, h=h, hp=hp, hh=hh, sbk=sbk, qc=qc: t.matmul(
                        pf[sbk][hh * 64:(hh + 1) * 64, qc:qc + 64],
                        b_tok[:, hp * 128 + hh * 64:hp * 128 + (hh + 1) * 64],
                        xd[:, hp * 128 + hh * 64:hp * 128 + (hh + 1) * 64],
                        start=True, stop=True),
                        deps=dds + [R[("btok", c, hp)].w, R[("xd", c, h)].w],
                        tag=f"s1_{c}_{h}")
            last_y = {2: e_y2[half * 16 + 14], 3: e_y2[half * 16 + 15]}
            last_s = {4: e_s1[half * 16 + 14], 5: e_s1[half * 16 + 15]}
            for h in hs:
                yb = 2 + h % 2
                sbk = 4 + h % 2
                hh = h % 2
                qc = ((h // 2) % 8) * 64
                e_ye = s.ev("act", lambda a, h=h, c=c, yb=yb, qc=qc: a.activation(
                    y_tok[:, c * D + h * 64:c * D + (h + 1) * 64],
                    pf[yb][:, qc:qc + 64], AF.Copy,
                    scale=din[:, c * 32 + h:c * 32 + h + 1]),
                    deps=[last_y[yb], R["din"].w],
                    writes=[res(("ytk", c, h))], tag=f"ye{c}_{h}")
                bank_rd[yb].append(e_ye)
                e_su = s.ev("dve", lambda v, h=h, hh=hh, c=c, sbk=sbk, qc=qc: v.scalar_tensor_tensor(
                    s_bf[hh * 64:(hh + 1) * 64, (h // 2) * 64:(h // 2 + 1) * 64],
                    s_bf[hh * 64:(hh + 1) * 64, (h // 2) * 64:(h // 2 + 1) * 64],
                    expabc[hh * 64:(hh + 1) * 64, c * 32 + h:c * 32 + h + 1],
                    pf[sbk][hh * 64:(hh + 1) * 64, qc:qc + 64],
                    op0=ALU.mult, op1=ALU.add),
                    deps=[last_s[sbk], e_eabv, e_y2[h]],
                    writes=[res(("sbf", h))], tag=f"su{c}_{h}")
                bank_rd[sbk].append(e_su)
                e_supd_last[h] = e_su
    if phases <= 3:
        return locals()
    e_ccw = s.ev("sp", lambda sp: sp.dma_start(out=d_ccin, in_=s_bf[:]),
                 deps=[e_supd_last[h] for h in range(NH)],
                 sem="ge_dma", incv=16, tag="ccw")
    e_cc = s.ev("pool", lambda g_: g_.collective_compute(
        "AllGather", ALU.bypass, [[0, 1], [2, 3], [4, 5], [6, 7]],
        [d_ccin.opt()], [d_ccout.opt()]), deps=[e_ccw], sem="cc", incv=1,
        tag="cc")
    e_sld = s.ev("pool", lambda g_: g_.dma_start(
        out=s_inb[:], in_=d_ccout[0:128, :]),
        deps=[e_cc, slab_consumed[1]], sem="ccd", incv=16, tag="sld")
    e_smul = s.ev("dve", lambda v: v.tensor_scalar_mul(
        s_inb[:], s_inb[:], smask_sb[:]), deps=[e_sld, e_smld],
        writes=[res("sinb")], tag="smul")
    def emit_p4b(c):
        for hp in range(16):
            pb = hp % 2
            yfs = hp % 2
            e_t = s.ev("pe", lambda t, c=c, hp=hp, pb=pb: t.transpose(
                pbt[pb][:], y_tok[:, c * D + hp * 128:c * D + (hp + 1) * 128],
                identb[:]),
                deps=[R[("ytk", c, 2 * hp)].w, R[("ytk", c, 2 * hp + 1)].w] +
                     epoch_dep(6 + pb),
                tag=f"ytr{c}_{hp}")
            e_o1 = s.ev("dve", lambda v, c=c, hp=hp, pb=pb, yfs=yfs: v.scalar_tensor_tensor(
                yfr[yfs][:], x_cm[:, hp * T + c * 128:hp * T + c * 128 + 128],
                drep_sb[:, hp:hp + 1], pbt[pb][:],
                op0=ALU.mult, op1=ALU.add),
                deps=[e_t, e_drepld], reads=[res(("cm", hp))],
                writes=[res(("yfr", yfs))], tag=f"yo1{c}_{hp}")
            bank_rd[6 + pb].append(e_o1)
            e_o2 = s.ev("dve", lambda v, c=c, hp=hp, yfs=yfs: v.tensor_mul(
                z_sb[:, hp * T + c * 128:hp * T + c * 128 + 128],
                yfr[yfs][:],
                z_sb[:, hp * T + c * 128:hp * T + c * 128 + 128]),
                deps=[e_o1], reads=[res(("yfr", yfs))],
                writes=[res(("z", hp))], tag=f"yo2{c}_{hp}")
    for c in range(NG):
        e_b = {}
        for h in range(NH):
            hh = h % 2
            half = h // 16
            yb = 2 * half + (h % 2)
            qc = ((h % 16) // 2) * 64
            ccol = (h // 2) * T + c * 128
            dd = epoch_dep(yb) if (h % 16) < 2 else []
            e_b[h] = s.ev("pe", lambda t, h=h, hh=hh, yb=yb, qc=qc, ccol=ccol: t.matmul(
                pf[yb][:, qc:qc + 64],
                c_cm[hh * 64:(hh + 1) * 64, ccol:ccol + 128],
                s_inb[hh * 64:(hh + 1) * 64, (h // 2) * 64:(h // 2 + 1) * 64],
                start=True, stop=True),
                deps=dd + [e_smul, R[("cm", 32 + h // 2)].w],
                tag=f"bnd{c}_{h}")
        last_b = {0: e_b[14], 1: e_b[15], 2: e_b[30], 3: e_b[31]}
        for h in range(NH):
            half = h // 16
            yb = 2 * half + (h % 2)
            qc = ((h % 16) // 2) * 64
            if h % 2 == 0:
                e_ba = s.ev("dve", lambda v, h=h, c=c, yb=yb, qc=qc: v.scalar_tensor_tensor(
                    y_tok[:, c * D + h * 64:c * D + (h + 1) * 64],
                    pf[yb][:, qc:qc + 64],
                    dbnd[:, c * 32 + h:c * 32 + h + 1],
                    y_tok[:, c * D + h * 64:c * D + (h + 1) * 64],
                    op0=ALU.mult, op1=ALU.add),
                    deps=[last_b[yb], R["dbnd"].w],
                    writes=[res(("ytk", c, h))], tag=f"ba{c}_{h}")
                bank_rd[yb].append(e_ba)
            else:
                ybs = (h // 2) % 4
                e_be = s.ev("act", lambda a, h=h, c=c, yb=yb, qc=qc, ybs=ybs: a.activation(
                    ybr[ybs][:], pf[yb][:, qc:qc + 64], AF.Copy,
                    scale=dbnd[:, c * 32 + h:c * 32 + h + 1]),
                    deps=[last_b[yb], R["dbnd"].w],
                    writes=[res(("ybr", ybs))], tag=f"be{c}_{h}")
                bank_rd[yb].append(e_be)
                e_ba = s.ev("dve", lambda v, h=h, c=c, ybs=ybs: v.tensor_add(
                    y_tok[:, c * D + h * 64:c * D + (h + 1) * 64],
                    y_tok[:, c * D + h * 64:c * D + (h + 1) * 64],
                    ybr[ybs][:]),
                    deps=[e_be], reads=[res(("ybr", ybs))],
                    writes=[res(("ytk", c, h))], tag=f"ba{c}_{h}")
        if c >= 1:
            emit_p4b(c - 1)

    emit_p4b(NG - 1)
    s.fence()
    if phases <= 35:
        return locals()

    s.fence()
    if phases <= 4:
        return locals()

    e_x1 = {}
    for jf in range(NK):
        buf = jf % 2
        sl = jf % 2
        if jf in e_wosld:
            e_ld = e_wosld[jf]
        else:
            e_ld = s.ev("pool", lambda g_, jf=jf, buf=buf: g_.dma_start(
                out=wos[buf][:],
                in_=d_wout[:, jf * NK * 128:(jf + 1) * NK * 128]),
                deps=[wos_consumed[buf]], sem=f"ws{buf}", incv=16,
                tag=f"osl{jf}")
        e_x0 = s.ev("sp", lambda sp, jf=jf, sl=sl: sp.dma_start(
            out=x0r[sl][:], in_=d_u[jf * 128:(jf + 1) * 128, 3:TE]),
            writes=[res(("x0r", sl))], sem=f"x0{sl}", incv=16,
            tag=f"x0{jf}")
        for th in range(2):
            bank = (jf * 2 + th) % 2
            last = None
            for k in range(NK):
                last = s.ev("pe", lambda t, k=k, th=th, bank=bank, buf=buf, st=(k == 0), sp_=(k == NK - 1): t.matmul(
                    pf[bank][:, 0:512],
                    wos[buf][:, k * 128:(k + 1) * 128],
                    z_sb[:, k * T + th * 512:k * T + (th + 1) * 512],
                    start=st, stop=sp_),
                    deps=[e_ld], reads=[res(("z", k))],
                    writes=([res(("pf", bank))] if k == 0 else []),
                    tag=f"wo{jf}_{th}_{k}")
            wos_consumed[buf] = last
            e_e = s.ev("dve", lambda v, jf=jf, th=th, sl=sl, bank=bank: v.tensor_add(
                x1[:, jf * T + th * 512:jf * T + (th + 1) * 512],
                x0r[sl][:, th * 512:(th + 1) * 512], pf[bank][:, 0:512]),
                deps=[last, e_x0], reads=[res(("pf", bank)), res(("x0r", sl))],
                writes=[res(("x1", jf))], tag=f"x1e{jf}_{th}")
            e_x1[(jf, th)] = e_e
    if phases <= 5:
        return locals()
    e_x1c = []
    for k in range(NK):
        e_c = s.ev("act", lambda a, k=k: a.activation(
            x1bf[:, k * T:(k + 1) * T], x1[:, k * T:(k + 1) * T], AF.Copy),
            deps=[e_x1[(k, 0)], e_x1[(k, 1)]], reads=[res(("x1", k))],
            writes=[res(("x1bf", k))], tag=f"x1c{k}")
        e_x1c.append(e_c)
    CG2 = [(0, 512), (512, 1024)]
    e_bc2 = []
    x1bf_readers = {k: [] for k in range(NK)}
    for cg, (c0, c1) in enumerate(CG2):
        n = c1 - c0
        lastm = lasts_ = None
        for k in range(NK):
            sl = k % 2
            e_sq = s.ev("act", lambda a, k=k, c0=c0, n=n, sl=sl: a.activation(
                sq2[sl][:, 0:n], x1bf[:, k * T + c0:k * T + c0 + n],
                AF.Square), deps=[e_x1c[k]], writes=[res(("sq2", sl))],
                tag=f"q2{cg}_{k}")
            x1bf_readers[k].append(e_sq)
            lastm = s.ev("pe", lambda t, k=k, c0=c0, n=n, st=(k == 0), sp_=(k == NK - 1): t.matmul(
                pf[2][0:1, 0:n], onescol[:], x1bf[:, k * T + c0:k * T + c0 + n],
                start=st, stop=sp_), deps=[e_x1c[k], e_onescol],
                writes=([res(("pf", 2))] if k == 0 else []), tag=f"m2m{cg}_{k}")
            x1bf_readers[k].append(lastm)
            lasts_ = s.ev("pe", lambda t, k=k, n=n, sl=sl, st=(k == 0), sp_=(k == NK - 1): t.matmul(
                pf[3][0:1, 0:n], onescol[:], sq2[sl][:, 0:n],
                start=st, stop=sp_), deps=[e_sq], reads=[res(("sq2", sl))],
                writes=([res(("pf", 3))] if k == 0 else []), tag=f"s2m{cg}_{k}")
        e_em2 = s.ev("dve", lambda v, n=n: v.tensor_scalar_mul(
            m512[:, 0:n], pf[2][0:1, 0:n], INV_D), deps=[lastm],
            reads=[res(("pf", 2))], writes=[res("m2f")], tag=f"em2{cg}")
        e_es2 = s.ev("dve", lambda v, n=n: v.tensor_scalar_mul(
            s2row[:, 0:n], pf[3][0:1, 0:n], INV_D), deps=[lasts_],
            reads=[res(("pf", 3))], writes=[res("s2row")], tag=f"es2{cg}")
        e_mq2 = s.ev("act", lambda a, n=n: a.activation(
            scr2[:, 0:n], m512[:, 0:n], AF.Square), deps=[e_em2],
            reads=[res("m2f")], writes=[res("scr2")], tag=f"mq2{cg}")
        e_v2 = s.ev("dve", lambda v, n=n: v.tensor_sub(
            s2row[:, 0:n], s2row[:, 0:n], scr2[:, 0:n]), deps=[e_es2, e_mq2],
            writes=[res("s2row")], tag=f"v2{cg}")
        e_l2 = s.ev("act", lambda a, n=n: a.activation(
            scr2[:, 0:n], s2row[:, 0:n], AF.Ln, bias=c_eps[:]),
            deps=[e_v2, e_ceps, e_mq2], reads=[res("s2row")],
            writes=[res("scr2")], tag=f"l2{cg}")
        e_r2 = s.ev("act", lambda a, n=n: a.activation(
            s2row[:, 0:n], scr2[:, 0:n], AF.Exp, scale=c_nhalf[:]),
            deps=[e_l2, e_cnh], reads=[res("scr2")],
            writes=[res("s2row")], tag=f"r2{cg}")
        e_si = s.ev("act", lambda a, n=n: a.activation(
            scr2[:, 0:n], scr2[:, 0:n], AF.Exp, scale=c_phalf[:]),
            deps=[e_r2, e_cph], writes=[res("scr2")], tag=f"si{cg}")
        e_m2c = s.ev("act", lambda a, n=n, c0=c0: a.activation(
            m2row[0:1, c0:c0 + n], m512[:, 0:n], AF.Copy), deps=[e_em2],
            reads=[res("m2f")], writes=[res("m2row")], tag=f"m2c{cg}")
        e_m2ff = s.ev("dve", lambda v, n=n, c0=c0: v.tensor_copy(
            m2f[0:1, c0:c0 + n], m512[:, 0:n]), deps=[e_em2],
            reads=[res("m2f")], writes=[res("m2ffull")], tag=f"m2ff{cg}")
        for (dstrow, dstbc, rres, dep) in [
                (m512[:, 0:n], mbc2, "m2f", e_em2),
                (s2row[:, 0:n], sbc2, "s2row", e_r2),
                (scr2[:, 0:n], sigbc, "scr2", e_si)]:
            e_b = s.ev("pe", lambda t, dstrow=dstrow, n=n: t.matmul(
                pf[4][:, 0:n], ones1f[:], dstrow, start=True, stop=True),
                deps=[dep, e_ones1f], reads=[res(rres)],
                writes=[res(("pf", 4))], tag=f"bc2{cg}")
            e_v = s.ev("dve", lambda v, dstbc=dstbc, c0=c0, n=n: v.tensor_copy(
                dstbc[:, c0:c0 + n], pf[4][:, 0:n]), deps=[e_b],
                reads=[res(("pf", 4))], writes=[res(id(dstbc))], tag=f"bv2{cg}")
            e_bc2.append(e_v)
    e_ln2 = []
    for k in range(NK):
        sl = k % 2
        e_t = s.ev("dve", lambda v, k=k, sl=sl: v.tensor_sub(
            x0r[sl][:], x1[:, k * T:(k + 1) * T], mbc2[:]),
            deps=e_bc2 + [e_x1[(k, 0)], e_x1[(k, 1)]],
            reads=[res(id(mbc2)), res(("x1", k))],
            writes=[res(("x0r", sl))], tag=f"n2s{k}")
        e_n = s.ev("dve", lambda v, k=k, sl=sl: v.tensor_mul(
            ln2[:, k * T:(k + 1) * T], x0r[sl][:], sbc2[:]),
            deps=[e_t], reads=[res(("x0r", sl)), res(id(sbc2))],
            writes=[res(("ln2", k))], tag=f"n2m{k}")
        e_ln2.append(e_n)
    lastrs = None
    for k in range(NK):
        lastrs = s.ev("pe", lambda t, k=k, st=(k == 0), sp_=(k == NK - 1): t.matmul(
            pf[3][0:8, 0:1], wrf_sb[:, k * 8:(k + 1) * 8], onescolf[:],
            start=st, stop=sp_), deps=[e_wrf, e_onescolf],
            writes=([res(("pf", 3))] if k == 0 else []), tag=f"rs{k}")
    e_rsv = s.ev("dve", lambda v: v.tensor_copy(rs_col[:], pf[3][0:8, 0:1]),
                 deps=[lastrs], reads=[res(("pf", 3))], writes=[res("rscol")],
                 tag="rsv")
    e_rst = s.ev("pe", lambda t: t.transpose(
        pf[3][0:1, 0:8], rs_col[:], identf[0:8, 0:8]), deps=[e_rsv, e_idf],
        reads=[res("rscol")], writes=[res(("pf", 3))], tag="rst")
    e_nrs = s.ev("dve", lambda v: v.tensor_scalar_mul(
        nrs_row[:], pf[3][0:1, 0:8], -1.0), deps=[e_rst],
        reads=[res(("pf", 3))], writes=[res("nrs")], tag="nrs")
    e_sc_ev = []
    for cg in range(2):
        c0 = cg * 512
        last = None
        for k in range(NK):
            last = s.ev("pe", lambda t, k=k, c0=c0, st=(k == 0): t.matmul(
                pf[2][0:8, 0:512], wrf_sb[:, k * 8:(k + 1) * 8],
                x1[:, k * T + c0:k * T + c0 + 512], start=st, stop=False),
                deps=[e_wrf, e_x1[(k, 0)], e_x1[(k, 1)]],
                reads=[res(("x1", k))],
                writes=([res(("pf", 2))] if k == 0 else []), tag=f"rt{cg}_{k}")
        last = s.ev("pe", lambda t, c0=c0: t.matmul(
            pf[2][0:8, 0:512], nrs_row[:], m2f[0:1, c0:c0 + 512],
            start=False, stop=True),
            deps=[last, e_nrs, R["m2ffull"].w], reads=[res("nrs"), res("m2ffull")],
            tag=f"rtm{cg}")
        e_ev = s.ev("dve", lambda v, c0=c0: v.tensor_mul(
            scores[:, c0:c0 + 512], pf[2][0:8, 0:512], sbc2[0:8, c0:c0 + 512]),
            deps=[last] + e_bc2, reads=[res(("pf", 2)), res(id(sbc2))],
            writes=[res("scores")], tag=f"rte{cg}")
        e_sc_ev.append(e_ev)
    def emit_top2():
        for gg in range(8):
            e_t = s.ev("pe", lambda t, gg=gg: t.transpose(
                pf[3][:, 0:8], scores[0:8, gg * 128:(gg + 1) * 128],
                identf[0:8, 0:8]), deps=e_sc_ev + [e_idf], reads=[res("scores")],
                writes=[res(("pf", 3))], tag=f"st{gg}")
            e_v = s.ev("dve", lambda v, gg=gg: v.tensor_copy(
                sttok[:, gg * 8:(gg + 1) * 8], pf[3][:, 0:8]), deps=[e_t],
                reads=[res(("pf", 3))], writes=[res("sttok")], tag=f"stv{gg}")
            sl = sttok[:, gg * 8:(gg + 1) * 8]
            e1 = s.ev("dve", lambda v, sl=sl: v.reduce_max(r_m1[:], sl, axis=AX.X),
                      deps=[e_v], writes=[res("rm1")], tag=f"rm1{gg}")
            e2 = s.ev("dve", lambda v: v.tensor_scalar_mul(r_nm1[:], r_m1[:], -1.0),
                      deps=[e1], reads=[res("rm1")], writes=[res("rnm1")],
                      tag=f"rnm{gg}")
            e3 = s.ev("dve", lambda v, sl=sl: v.tensor_single_scalar(
                r_eq1[:], sl, r_m1[:], ALU.is_equal), deps=[e1],
                reads=[res("rm1")], writes=[res("req1")], tag=f"req1{gg}")
            e4 = s.ev("dve", lambda v, sl=sl: v.scalar_tensor_tensor(
                r_sc2[:], r_eq1[:], -1e30, sl, op0=ALU.mult, op1=ALU.add),
                deps=[e3], reads=[res("req1")], writes=[res("rsc2")],
                tag=f"rsc2{gg}")
            e5 = s.ev("dve", lambda v: v.reduce_max(r_m2[:], r_sc2[:], axis=AX.X),
                      deps=[e4], reads=[res("rsc2")], writes=[res("rm2")],
                      tag=f"rm2{gg}")
            e6 = s.ev("dve", lambda v, sl=sl: v.tensor_single_scalar(
                r_eq2[:], sl, r_m2[:], ALU.is_equal), deps=[e5],
                reads=[res("rm2")], writes=[res("req2")], tag=f"req2{gg}")
            e7 = s.ev("act", lambda a: a.activation(
                r_em[:], r_m2[:], AF.Exp, bias=r_nm1[:]), deps=[e5, e2],
                reads=[res("rm2"), res("rnm1")], writes=[res("rem")],
                tag=f"rem{gg}")
            e8 = s.ev("dve", lambda v: v.tensor_scalar_add(r_p1[:], r_em[:], 1.0),
                      deps=[e7], reads=[res("rem")], writes=[res("rp1")],
                      tag=f"rp1a{gg}")
            e9 = s.ev("dve", lambda v: v.reciprocal(r_p1[:], r_p1[:]),
                      deps=[e8], writes=[res("rp1")], tag=f"rp1b{gg}")
            e10 = s.ev("dve", lambda v: v.tensor_mul(r_p2[:], r_em[:], r_p1[:]),
                       deps=[e9], reads=[res("rem"), res("rp1")],
                       writes=[res("rp2")], tag=f"rp2{gg}")
            e11 = s.ev("dve", lambda v, gg=gg: v.tensor_scalar_mul(
                wtok[:, gg * 8:(gg + 1) * 8], r_eq1[:], r_p1[:]),
                deps=[e10, e3], reads=[res("req1"), res("rp1")],
                writes=[res(("wtok", gg))], tag=f"w1{gg}")
            e12 = s.ev("dve", lambda v, gg=gg: v.scalar_tensor_tensor(
                wtok[:, gg * 8:(gg + 1) * 8], r_eq2[:], r_p2[:],
                wtok[:, gg * 8:(gg + 1) * 8], op0=ALU.mult, op1=ALU.add),
                deps=[e11, e6], reads=[res("req2"), res("rp2")],
                writes=[res(("wtok", gg))], tag=f"w2{gg}")

    if phases <= 6:
        return locals()
    mslab_consumed = {}
    for m in range(2):
        for sl_ in range(3):
            mslab_consumed[(m, sl_)] = [e_x1[(15, 0)], e_x1[(15, 1)]]
    e_wbv_all = {}

    def emit_wbc(e):
        e_wbv = []
        for gg in range(8):
            ws = gg % 2
            e_wt = s.ev("pe", lambda t, gg=gg, e=e: t.transpose(
                pf[3][0:1, 0:128], wtok[:, gg * 8 + e:gg * 8 + e + 1],
                identf[:]), deps=[R[("wtok", gg)].w, e_idf],
                writes=[res(("pf", 3))], tag=f"wt{e}_{gg}")
            e_wv = s.ev("dve", lambda v, ws=ws: v.tensor_copy(
                wrow[ws][:], pf[3][0:1, 0:128]), deps=[e_wt],
                reads=[res(("pf", 3))], writes=[res(("wrow", ws))],
                tag=f"wv{e}_{gg}")
            e_wb = s.ev("pe", lambda t, gg=gg, ws=ws: t.matmul(
                pf[4][:, (gg % 4) * 128:(gg % 4 + 1) * 128], ones1f[:],
                wrow[ws][:], start=True, stop=True),
                deps=[e_wv, e_ones1f], reads=[res(("wrow", ws))],
                writes=([res(("pf", 4))] if gg % 4 == 0 else []),
                tag=f"wb{e}_{gg}")
            if gg % 4 == 3:
                e_bv = s.ev("dve", lambda v, gg=gg: v.tensor_copy(
                    wbc[:, (gg - 3) * 128:(gg + 1) * 128], pf[4][:, 0:512]),
                    deps=[e_wb], reads=[res(("pf", 4))],
                    writes=[res("wbc")], tag=f"wbv{e}_{gg}")
                e_wbv.append(e_bv)
        e_wbv_all[e] = e_wbv

    for e in range(E):
        if e > 0:
            emit_wbc(e)
        e_wbv = e_wbv_all.get(e, [])
        for jf in range(NJF):
            slot = jf % 3
            e_lg = s.ev("pool", lambda g_, e=e, jf=jf, slot=slot: g_.dma_start(
                out=mslab[0][:, slot * NK * 128:(slot + 1) * NK * 128],
                in_=d_wg[e, :, jf * NK * 128:(jf + 1) * NK * 128]),
                deps=mslab_consumed[(0, slot)], sem=f"ms0{slot}", incv=16,
                tag=f"lg{e}_{jf}")
            e_lu = s.ev("pool", lambda g_, e=e, jf=jf, slot=slot: g_.dma_start(
                out=mslab[1][:, slot * NK * 128:(slot + 1) * NK * 128],
                in_=d_wu[e, :, jf * NK * 128:(jf + 1) * NK * 128]),
                deps=mslab_consumed[(1, slot)], sem=f"ms1{slot}", incv=16,
                tag=f"lu{e}_{jf}")
            for th in range(2):
                sgs = (jf * 2 + th) % 2
                lastg = None
                for k in range(NK):
                    lastg = s.ev("pe", lambda t, slot=slot, k=k, th=th, st=(k == 0), sp_=(k == NK - 1): t.matmul(
                        pf[0][:, 0:512],
                        mslab[0][:, (slot * NK + k) * 128:(slot * NK + k + 1) * 128],
                        ln2[:, k * T + th * 512:k * T + (th + 1) * 512],
                        start=st, stop=sp_),
                        deps=[e_lg, e_ln2[k]],
                        writes=([res(("pf", 0))] if k == 0 else []),
                        tag=f"gm{e}_{jf}_{th}_{k}")
                e_sg = s.ev("act", lambda a_, sgs=sgs: a_.activation(
                    sgr[sgs][:], pf[0][:, 0:512], AF.Silu),
                    deps=[lastg], reads=[res(("pf", 0))],
                    writes=[res(("sgr", sgs))], tag=f"sg{e}_{jf}_{th}")
                lastu = None
                for k in range(NK):
                    lastu = s.ev("pe", lambda t, slot=slot, k=k, th=th, st=(k == 0), sp_=(k == NK - 1): t.matmul(
                        pf[1][:, 0:512],
                        mslab[1][:, (slot * NK + k) * 128:(slot * NK + k + 1) * 128],
                        ln2[:, k * T + th * 512:k * T + (th + 1) * 512],
                        start=st, stop=sp_),
                        deps=[e_lu, e_ln2[k]],
                        writes=([res(("pf", 1))] if k == 0 else []),
                        tag=f"um{e}_{jf}_{th}_{k}")
                if e == 0:
                    e_h1 = s.ev("dve", lambda v, jf=jf, th=th, sgs=sgs: v.tensor_mul(
                        hidden[:, jf * T + th * 512:jf * T + (th + 1) * 512],
                        sgr[sgs][:], pf[1][:, 0:512]),
                        deps=[lastu, e_sg], reads=[res(("pf", 1)), res(("sgr", sgs))],
                        writes=[res(("hid", e, jf))], tag=f"h1{e}_{jf}_{th}")
                else:
                    e_h1 = s.ev("dve", lambda v, sgs=sgs: v.tensor_mul(
                        h1r[sgs][:], sgr[sgs][:], pf[1][:, 0:512]),
                        deps=[lastu, e_sg], reads=[res(("pf", 1)), res(("sgr", sgs))],
                        writes=[res(("h1r", sgs))], tag=f"h1{e}_{jf}_{th}")
                    e_h2 = s.ev("dve", lambda v, e=e, jf=jf, th=th, sgs=sgs: v.tensor_mul(
                        hidden[:, (e * NJF + jf) * T + th * 512:(e * NJF + jf) * T + (th + 1) * 512],
                        h1r[sgs][:], wbc[:, th * 512:(th + 1) * 512]),
                        deps=[e_h1] + e_wbv, reads=[res(("h1r", sgs)), res("wbc")],
                        writes=[res(("hid", e, jf))], tag=f"h2{e}_{jf}_{th}")
                mslab_consumed[(0, slot)] = [lastg]
                mslab_consumed[(1, slot)] = [lastu]
        if e == 0:
            emit_top2()
            emit_wbc(0)
            e_wbv = e_wbv_all[0]
            for jf in range(NJF):
                for th in range(2):
                    s.ev("dve", lambda v, jf=jf, th=th: v.tensor_mul(
                        hidden[:, jf * T + th * 512:jf * T + (th + 1) * 512],
                        hidden[:, jf * T + th * 512:jf * T + (th + 1) * 512],
                        wbc[:, th * 512:(th + 1) * 512]),
                        deps=e_wbv, reads=[res("wbc")],
                        writes=[res(("hid", 0, jf))], tag=f"h2p{jf}_{th}")
    s.fence()

    if phases <= 7:
        return locals()
    e_sts = []
    dslab_consumed = [mslab_consumed[(0, 2)][0], mslab_consumed[(1, 2)][0]]
    dslab_first = [mslab_consumed[(0, 0)][0], mslab_consumed[(0, 1)][0],
                   mslab_consumed[(1, 0)][0], mslab_consumed[(1, 1)][0]]
    for j in range(NK):
        buf = j % 2
        e_ldp = []
        for p in range(3):
            e_ldp.append(s.ev("pool", lambda g_, j=j, buf=buf, p=p: g_.dma_start(
                out=mslab[buf][:, p * 2048:(p + 1) * 2048],
                in_=d_wd[:, j * E * NJF * 128 + p * 2048:
                         j * E * NJF * 128 + (p + 1) * 2048]),
                deps=[dslab_consumed[buf]] + (dslab_first if j < 2 else []),
                sem=f"m8{buf}{p}", incv=16, tag=f"dsl{j}_{p}"))
        for th in range(2):
            bank = (j * 2 + th) % 2
            last = None
            first = True
            for e in range(E):
                for kk in range(NJF):
                    last = s.ev("pe", lambda t, e=e, kk=kk, th=th, bank=bank, buf=buf, st=first: t.matmul(
                        pf[bank][:, 0:512],
                        mslab[buf][:, (e * NJF + kk) * 128:(e * NJF + kk + 1) * 128],
                        hidden[:, (e * NJF + kk) * T + th * 512:(e * NJF + kk) * T + (th + 1) * 512],
                        start=st, stop=False),
                        deps=[e_ldp[(e * NJF + kk) // 16]],
                        reads=[res(("hid", e, kk))],
                        writes=([res(("pf", bank))] if first else []),
                        tag=f"dm{j}_{th}_{e}_{kk}")
                    first = False
            last = s.ev("pe", lambda t, j=j, th=th, bank=bank: t.matmul(
                pf[bank][:, 0:512], ones1b[:],
                m2row[0:1, th * 512:(th + 1) * 512], start=False, stop=True),
                deps=[last, R["m2row"].w, e_ms[3]], reads=[res("m2row")],
                tag=f"dmm{j}_{th}")
            dslab_consumed[buf] = last
            ts_ = (j * 2 + th) % 2
            e_t1 = s.ev("dve", lambda v, j=j, th=th, ts_=ts_: v.tensor_mul(
                tout[ts_][:], ln2[:, j * T + th * 512:j * T + (th + 1) * 512],
                sigbc[:, th * 512:(th + 1) * 512]),
                deps=[R[("ln2", j)].w, R[id(sigbc)].w],
                reads=[res(("ln2", j)), res(id(sigbc))],
                writes=[res(("tout", ts_))], tag=f"t1_{j}_{th}")
            e_t2 = s.ev("dve", lambda v, ts_=ts_, bank=bank: v.tensor_add(
                ostg[ts_][:], tout[ts_][:], pf[bank][:, 0:512]),
                deps=[last, e_t1], reads=[res(("pf", bank)), res(("tout", ts_))],
                writes=[res(("ostg", ts_))], tag=f"t2_{j}_{th}")
            e_st = s.ev("sp", lambda sp, j=j, th=th, ts_=ts_: sp.dma_start(
                out=d_o[j * 128:(j + 1) * 128, th * 512:(th + 1) * 512],
                in_=ostg[ts_][:]), deps=[e_t2], reads=[res(("ostg", ts_))],
                sem=f"o{ts_}", incv=16, tag=f"st{j}_{th}")
            e_sts.append(e_st)
    e_end = s.ev("sp", lambda sp: sp.nop(), deps=e_sts, tag="end")

    return locals()


def finish(env):
    nc, s, sems = env["nc"], env["s"], env["sems"]
    s.finalize_plan()
    with nc.Block() as blk:
        @blk.tensor
        def _(t):
            s.emit("pe", t, sems)

        @blk.vector
        def _(v):
            s.emit("dve", v, sems)

        @blk.scalar
        def _(a):
            s.emit("act", a, sems)

        @blk.gpsimd
        def _(g):
            s.emit("pool", g, sems)

        @blk.sync
        def _(sp):
            s.emit("sp", sp, sems)
    nc.finalize()
    return nc


def stage(inputs):
    W_in = inputs["L0_attn_W_in"]
    W_out = inputs["L0_attn_W_out"]
    conv_w = inputs["L0_attn_conv_w"]
    conv_b = inputs["L0_attn_conv_b"]
    z_bias = inputs["L0_attn_z_bias"]
    Dd = inputs["L0_attn_D"]
    Wr = inputs["L0_ffn_w_gate_router"]
    Wg = inputs["L0_ffn_w_gate_proj"]
    Wu = inputs["L0_ffn_w_up_proj"]
    Wd = inputs["L0_ffn_w_down_proj"]
    u = inputs["L0_attn_u"]
    C = np.ascontiguousarray

    wxbc = C(np.asarray(W_in[:CONVD]).reshape(NJX, 128, NK, 128)
             .transpose(3, 0, 2, 1).reshape(128, -1))
    wz = C(np.asarray(W_in[CONVD:AOFF]).reshape(NJZ, 128, NK, 128)
           .transpose(3, 0, 2, 1).reshape(128, -1))
    wa = C(np.asarray(W_in[AOFF:]).reshape(32, NK, 128)
           .transpose(2, 1, 0).reshape(128, -1))
    wout = C(np.asarray(W_out).reshape(NK, 128, NK, 128)
             .transpose(3, 0, 2, 1).reshape(128, -1))
    wr = C(np.asarray(Wr).T.reshape(NK, 128, 8).transpose(1, 0, 2)
           .reshape(128, -1))
    wg = C(np.asarray(Wg).reshape(E, NJF, 128, NK, 128)
           .transpose(4, 0, 1, 3, 2).reshape(128, E, -1).transpose(1, 0, 2))
    wu = C(np.asarray(Wu).reshape(E, NJF, 128, NK, 128)
           .transpose(4, 0, 1, 3, 2).reshape(128, E, -1).transpose(1, 0, 2))
    wd = C(np.asarray(Wd).reshape(E, NK, 128, NJF, 128)
           .transpose(4, 1, 0, 3, 2).reshape(128, -1))
    cw = C(np.asarray(conv_w)[:, 0, :].reshape(NJX, 128, 4)
           .transpose(1, 0, 2).reshape(128, -1))
    cb = C(np.asarray(conv_b).reshape(NJX, 128).T)
    zb = C(np.asarray(z_bias).reshape(NJZ, 128).T)
    drep = C(np.repeat(np.asarray(Dd), HD).reshape(NK, 128).T)
    shared = dict(wxbc=wxbc, wz=wz, wa=wa, wout=wout, wr=wr, wg=wg, wu=wu,
                  wd=wd, cw=cw, cb=cb, zb=zb, drep=drep)
    un = np.asarray(u)
    per_core = []
    for core in range(8):
        b, h = core // 2, core % 2
        ue = np.zeros((D, TE), np.float32)
        if h == 0:
            ue[:, 3:] = un[b, 0:T, :].T
        else:
            ue[:, :] = un[b, T - 3:L, :].T
        m = dict(shared)
        m["u"] = C(ue)
        m["smask"] = np.full((128, 1), float(h), np.float32)
        per_core.append(m)
    return per_core


def kernel(**inputs):
    per_core = stage(inputs)
    env = build_nc(debug=0)
    nc = finish(env)
    res = run_bass_kernel_spmd(nc, per_core, core_ids=list(range(8)))
    u = np.asarray(inputs["L0_attn_u"])
    out = np.empty((B, L, D), np.float32)
    for core in range(8):
        b, h = core // 2, core % 2
        out[b, h * T:(h + 1) * T, :] = np.asarray(res.results[core]["o"]).T
    return out




# version 18
# speedup vs baseline: 1.0762x; 1.0762x over previous
import numpy as np
import ml_dtypes
import concourse.bass as bass
import concourse.mybir as mybir
from concourse.bass_utils import run_bass_kernel_spmd
from contextlib import ExitStack

F32 = mybir.dt.float32
BF16 = mybir.dt.bfloat16
AF = mybir.ActivationFunctionType
ALU = mybir.AluOpType
AX = mybir.AxisListType

B, L, D = 4, 2048, 2048
T = 1024
TE = T + 3
NH, HD, DS = 32, 64, 64
NK = 16
E, FF = 8, 768
NJZ, NJX, NJF = 16, 48, 6
CONVD, AOFF = 6144, 8192
HD = 64
Q = 64
NG = 8
NC = 16
EPS = 1e-5
INV_D = 1.0 / D


class Ev:
    __slots__ = ("eng", "fn", "deps", "sem", "incv", "needed", "val", "tag")

    def __init__(self, eng, fn, deps, sem, incv, tag):
        self.eng, self.fn, self.deps = eng, fn, deps
        self.sem, self.incv = sem, incv
        self.needed = False
        self.val = None
        self.tag = tag


class Res:
    __slots__ = ("w", "r")

    def __init__(self):
        self.w = None
        self.r = []


def succeed(new, *old):
    rs = []
    for o in old:
        rs.extend(o.r)
        if o.w is not None:
            rs.append(o.w)
    new.r = rs
    new.w = None


class Sched:
    def __init__(self):
        self.events = []
        self.last_per_eng = {}
        self.pending_fence = {}

    def ev(self, eng, fn, deps=(), reads=(), writes=(), sem=None, incv=1,
           tag=""):
        dl = [d for d in deps if d is not None]
        if eng in self.pending_fence:
            dl.extend(self.pending_fence.pop(eng))
        for r in reads:
            if r.w is not None:
                dl.append(r.w)
        e = Ev(eng, fn, dl, sem if sem is not None else eng, incv, tag)
        for r in reads:
            r.r.append(e)
        for w in writes:
            e.deps.extend(w.r)
            if w.w is not None:
                e.deps.append(w.w)
            w.w = e
            w.r = []
        self.events.append(e)
        self.last_per_eng[eng] = e
        return e

    def fence(self):
        lasts = [e for e in self.last_per_eng.values()]
        for eng in ("pe", "dve", "act", "pool", "sp"):
            self.pending_fence[eng] = list(lasts)

    def finalize_plan(self):
        for e in self.events:
            e.deps = [d for d in e.deps if d is not e]
            for d in e.deps:
                d.needed = True
            if e.sem != e.eng:
                e.needed = True
        counts = {}
        for e in self.events:
            if e.needed:
                counts[e.sem] = counts.get(e.sem, 0) + e.incv
                e.val = counts[e.sem]

    def emit(self, eng_name, eng, sems):
        hw = {}
        for e in self.events:
            if e.eng != eng_name:
                continue
            for d in e.deps:
                assert d.needed, (e.tag, d.tag)
                if hw.get(d.sem, -1) < d.val:
                    eng.wait_ge(sems[d.sem], d.val)
                    hw[d.sem] = d.val
            inst = e.fn(eng)
            if e.needed:
                inst.then_inc(sems[e.sem], e.incv)
                if eng_name == "pool" and e.sem == "pool":
                    eng.wait_ge(sems["pool"], e.val)
                    hw["pool"] = e.val


class Arena:
    def __init__(self, lo, hi):
        self.lo, self.hi, self.cur = lo, hi, lo

    def take(self, nbytes):
        off = (self.cur + 31) & ~31
        assert off + nbytes <= self.hi, (off, nbytes, self.hi)
        self.cur = off + nbytes
        return off

    def reset(self):
        self.cur = self.lo


def build_nc(debug=0, phases=99, p3sub=9):
    p3sub = 9 if p3sub == 9 else p3sub
    nc = bass.Bass("TRN2", target_bir_lowering=False, debug=False)
    s = Sched()
    KB = 1024
    dbg_outs = {}

    dram = lambda n, shp, dt=F32, kind="ExternalInput": nc.dram_tensor(
        n, shp, dt, kind=kind).ap()
    d_u = dram("u", [D, TE])
    d_wxbc = dram("wxbc", [128, NJX * NK * 128])
    d_wz = dram("wz", [128, NJZ * NK * 128])
    d_wa = dram("wa", [128, NK * 32])
    d_wout = dram("wout", [128, NK * NK * 128])
    d_wr = dram("wr", [128, NK * 8])
    d_wg = dram("wg", [E, 128, NJF * NK * 128])
    d_wu = dram("wu", [E, 128, NJF * NK * 128])
    d_wd = dram("wd", [128, NK * E * NJF * 128])
    d_cw = dram("cw", [128, NJX * 4])
    d_cb = dram("cb", [128, NJX])
    d_zb = dram("zb", [128, NJZ])
    d_drep = dram("drep", [128, NK])
    d_smask = dram("smask", [128, 1])
    d_o = dram("o", [D, T], kind="ExternalOutput")
    d_ccin = dram("cc_in", [128, NC * 64], BF16, kind="Internal")
    d_ccout = dram("cc_out", [256, NC * 64], BF16, kind="Internal")

    KBo = 1024

    SBUF_BASE = 16512
    def sb(name, shape, dt, off):
        nb = int(np.prod(shape[1:])) * (2 if dt == BF16 else 4)
        assert SBUF_BASE + off + nb <= 229344, (name, off, nb)
        return nc.alloc_sbuf_tensor_at(name, shape, dt, offset=SBUF_BASE + off)

    identb = sb("identb", [128, 128], BF16, 0)
    identf = sb("identf", [128, 128], F32, 256)
    ones1f = sb("ones1f", [1, 128], F32, 768)
    ones1b = sb("ones1b", [1, 128], BF16, 1280)
    onescol = sb("onescol", [128, 1], BF16, 1536)
    u01 = sb("u01", [128, 128], F32, 1568)
    c_eps = sb("c_eps", [1, 1], F32, 2080)
    c_nhalf = sb("c_nhalf", [1, 1], F32, 2112)
    c_phalf = sb("c_phalf", [1, 1], F32, 4064)
    onescolf = sb("onescolf", [128, 1], F32, 2144)
    cw_sb = sb("cw_sb", [128, NJX * 4], F32, 2176)
    cb_sb = sb("cb_sb", [128, NJX], F32, 2944)
    zb_sb = sb("zb_sb", [128, NJZ], F32, 3136)
    drep_sb = sb("drep_sb", [128, NK], F32, 3200)
    smask_sb = sb("smask_sb", [128, 1], F32, 3264)
    wr_sb = sb("wr_sb", [128, NK * 8], BF16, 3296)
    SL = 4480
    wslab = [sb(f"wslab{i}", [128, 2 * NK * 128], BF16, SL + i * 8 * KBo)
             for i in range(2)]
    wos = [sb(f"wos{i}", [128, NK * 128], BF16, SL + i * 4 * KBo)
           for i in range(2)]
    din = sb("din", [128, 256], F32, SL + 8 * KBo)
    rcol = sb("rcol", [128, 256], F32, SL + 9 * KBo)
    ecol = sb("ecol", [128, 256], F32, SL + 10 * KBo)
    dbnd = sb("dbnd", [128, 256], F32, SL + 11 * KBo)
    s_inb = sb("s_inb", [128, NC * 64], BF16, SL + 12 * KBo)
    scores = sb("scores", [8, T], F32, SL + 0 * KBo)
    sigbc = sb("sigbc", [128, T], BF16, SL + 4 * KBo)
    m2row = sb("m2row", [1, T], BF16, SL + 6 * KBo)
    wbc = sb("wbc", [128, T], BF16, SL + 8 * KBo)
    sttok = sb("sttok", [128, 64], F32, SL + 10 * KBo)
    wtok = sb("wtok", [128, 64], F32, SL + 10 * KBo + 256)
    wrow = [sb(f"wrow{i}", [1, 128], F32, SL + 10 * KBo + 512 + i * 512)
            for i in range(2)]
    r_m1 = sb("r_m1", [128, 1], F32, SL + 11 * KBo + 512)
    r_nm1 = sb("r_nm1", [128, 1], F32, SL + 11 * KBo + 544)
    r_m2 = sb("r_m2", [128, 1], F32, SL + 11 * KBo + 576)
    r_em = sb("r_em", [128, 1], F32, SL + 11 * KBo + 608)
    r_p1 = sb("r_p1", [128, 1], F32, SL + 11 * KBo + 640)
    r_p2 = sb("r_p2", [128, 1], F32, SL + 11 * KBo + 672)
    r_eq1 = sb("r_eq1", [128, 8], F32, SL + 11 * KBo + 704)
    r_eq2 = sb("r_eq2", [128, 8], F32, SL + 11 * KBo + 736)
    r_sc2 = sb("r_sc2", [128, 8], F32, SL + 11 * KBo + 768)
    m2f = sb("m2f_row", [1, T], F32, SL + 12 * KBo)
    rs_col = sb("rs_col", [8, 1], F32, SL + 11 * KBo + 800)
    nrs_row = sb("nrs_row", [1, 8], F32, SL + 11 * KBo + 832)
    wrf_sb = sb("wrf_sb", [128, NK * 8], F32, 3552)
    S1 = 20864
    ln1 = sb("ln1", [128, NK * TE], BF16, S1)
    y_tok = sb("y_tok", [128, NG * D], BF16, S1)
    ln2 = sb("ln2", [128, NK * T], BF16, S1)
    S2 = 53760
    z_sb = sb("z_sb", [128, NK * T], BF16, S2)
    mslab = [sb(f"mslab{i}", [128, 3 * NK * 128], BF16, S2 + i * 12 * KBo)
             for i in range(2)]
    S3 = 86528
    u_bf = sb("u_bf", [128, NK * TE], BF16, S3)
    x_cm = sb("x_cm", [128, NK * T], BF16, S3)
    b_cm = sb("b_cm", [128, NK * T], BF16, S3 + 32 * KBo)
    c_cm = sb("c_cm", [128, NK * T], BF16, S3 + 64 * KBo)
    x1 = sb("x1", [128, NK * T], F32, S3)
    x1bf = sb("x1bf", [128, NK * T], BF16, S3 + 64 * KBo)
    hidden = sb("hidden", [128, E * NJF * T], BF16, S3)
    S4 = 184832
    uring = [sb(f"uring{i}", [128, TE], F32, S4 + i * 4128) for i in range(2)]
    xpre = [sb(f"xpre{i}", [128, TE], BF16, S4 + i * 2080) for i in range(2)]
    ctmp = [sb(f"ctmp{i}", [128, T], F32, S4 + 4160 + i * 4096)
            for i in range(2)]
    x_tok = sb("x_tok", [128, 2048], BF16, S4)
    b_tok = sb("b_tok", [128, 2048], BF16, S4 + 4 * KBo)
    xd = sb("xd", [128, 2048], BF16, S4 + 8 * KBo)
    x0r = [sb(f"x0r{i}", [128, T], F32, S4 + i * 4096) for i in range(2)]
    sgr = [sb(f"sgr{i}", [128, 512], F32, S4 + i * 2048) for i in range(2)]
    h1r = [sb(f"h1r{i}", [128, 512], F32, S4 + 4096 + i * 2048)
           for i in range(2)]
    tout = [sb(f"tout{i}", [128, 512], F32, S4 + i * 2048) for i in range(2)]
    ostg = [sb(f"ostg{i}", [128, 512], F32, S4 + 4096 + i * 2048)
            for i in range(2)]
    S5 = 197184
    sqr = [sb(f"sqr{i}", [128, 512], BF16, S5 + i * KBo) for i in range(2)]
    mrow = sb("mrow", [1, 512], F32, S5 + 2 * KBo)
    srow = sb("srow", [1, 512], F32, S5 + 4 * KBo)
    scrow = sb("scrow", [1, 512], F32, S5 + 6 * KBo)
    mbc = sb("mbc", [128, TE], BF16, S5 + 8 * KBo)
    sbc = sb("sbc", [128, TE], BF16, S5 + 8 * KBo + 2080)
    wa_sb = sb("wa_sb", [128, NK * 32], BF16, S5 + 12 * KBo)
    a_sb = sb("a_sb", [32, T], F32, S5)
    zeros32 = sb("zeros32", [32, T], F32, S5 + 4 * KBo)
    g_sb = sb("g_sb", [32, T], F32, S5 + 8 * KBo)
    gt_sb = sb("gt_sb", [128, 256], F32, S5 + 12 * KBo)
    gmrow = sb("gmrow", [1, 256], F32, S5 + 4 * KBo)
    gerow = sb("gerow", [1, 256], F32, S5 + 5 * KBo)
    gmnrow = sb("gmnrow", [1, 256], F32, S5 + 6 * KBo)
    arow = sb("arow", [1, 256], F32, S5 + 7 * KBo)
    expabc = sb("expabc", [128, 256], F32, S5)
    gbc = [sb(f"gbc{i}", [128, 32], F32, S5 + 1 * KBo + i * 128)
           for i in range(2)]
    mtr = [sb(f"mtr{i}", [128, 128], BF16, S5 + 8 * KBo + i * 256)
           for i in range(16)]
    s_bf = sb("s_bf", [128, NC * 64], BF16, S5 + 1 * KBo + 512)
    ybr = [sb(f"ybr{i}", [128, 64], F32, S5 + 3 * KBo + 512 + i * 256)
           for i in range(4)]
    yfr = [sb(f"yfr{i}", [128, 128], F32, S5 + 4 * KBo + 512 + i * 512)
           for i in range(2)]
    mbc2 = sb("mbc2", [128, T], BF16, S5)
    sbc2 = sb("sbc2", [128, T], BF16, S5 + 2 * KBo)
    sq2 = [sb(f"sq2_{i}", [128, 512], BF16, S5 + 4 * KBo + i * KBo)
           for i in range(2)]
    m512 = sb("m512", [1, 512], F32, S5 + 6 * KBo)
    s2row = sb("s2row", [1, 512], F32, S5 + 8 * KBo)
    scr2 = sb("scr2", [1, 512], F32, S5 + 10 * KBo)

    ctx = ExitStack()
    pf = [ctx.enter_context(nc.psum_tensor(f"pf{i}", [128, 512], F32))
          for i in range(6)]
    pbt = [ctx.enter_context(nc.psum_tensor(f"pbt{i}", [128, 128], BF16))
           for i in range(2)]

    sem_names = ["pe", "dve", "act", "pool", "sp",
                 "c_dma", "u0", "u1", "ws0", "ws1", "ms0", "ms1",
                 "ms00", "ms01", "ms02", "ms10", "ms11", "ms12",
                 "m800", "m801", "m802", "m810", "m811", "m812",
                 "x00", "x01", "o0", "o1", "cc", "ccd", "ge_dma"]
    if debug:
        sem_names += ["dbg_dma", "dbg_gdma"]
    sems = {n: nc.alloc_semaphore(n) for n in sem_names}

    R = {}

    def res(key):
        if key not in R:
            R[key] = Res()
        return R[key]

    cload = []
    prevld = None
    for dst, src in [(wr_sb, d_wr), (wr_sb, d_wr), (cw_sb, d_cw),
                     (cb_sb, d_cb), (zb_sb, d_zb), (drep_sb, d_drep),
                     (smask_sb, d_smask)]:
        prevld = s.ev("pool", lambda g, d=dst, sr=src: g.dma_start(
            out=d[:], in_=sr), deps=[prevld], sem="c_dma", incv=16,
            tag="cload")
        cload.append(prevld)
    e_ms = []
    e_ms.append(s.ev("pool", lambda g: g.memset(identb[:], 1.0), tag="ms"))
    e_ms.append(s.ev("pool", lambda g: g.memset(identf[:], 1.0), tag="ms"))
    e_ms.append(s.ev("pool", lambda g: g.memset(ones1f[:], 1.0), tag="ms"))
    e_ms.append(s.ev("pool", lambda g: g.memset(ones1b[:], 1.0), tag="ms"))
    e_ms.append(s.ev("pool", lambda g: g.memset(onescol[:], 1.0), tag="ms"))
    e_ms.append(s.ev("pool", lambda g: g.memset(u01[:], 1.0), tag="ms"))
    e_ms.append(s.ev("pool", lambda g: g.nop(), tag="ms"))
    e_ms.append(s.ev("pool", lambda g: g.memset(c_eps[:], EPS), tag="ms"))
    e_ms.append(s.ev("pool", lambda g: g.memset(c_nhalf[:], -0.5), tag="ms"))
    e_ms.append(s.ev("pool", lambda g: g.memset(onescolf[:], 1.0), tag="ms"))
    e_cph = s.ev("pool", lambda g: g.memset(c_phalf[:], 0.5), tag="ms")
    e_ms.append(e_cph)
    e_ones1f, e_ones1b, e_onescol = e_ms[2], e_ms[3], e_ms[4]
    e_ceps, e_cnh, e_onescolf = e_ms[7], e_ms[8], e_ms[9]
    e_xx, e_wr, e_cw, e_cb, e_zbld, e_drepld, e_smld = cload
    e_wrf = s.ev("pool", lambda g: g.dma_start(out=wrf_sb[:], in_=d_wr),
                 deps=[e_smld], sem="c_dma", incv=16, tag="wrfld")
    e_idb = s.ev("pool", lambda g: g.affine_select(
        identb[:], identb[:], pattern=[[1, 128]], compare_op=ALU.is_equal,
        fill=0.0, base=0, channel_multiplier=-1), deps=e_ms, tag="idb")
    e_idf = s.ev("pool", lambda g: g.affine_select(
        identf[:], identf[:], pattern=[[1, 128]], compare_op=ALU.is_equal,
        fill=0.0, base=0, channel_multiplier=-1), deps=[e_idb], tag="idf")
    e_u01 = s.ev("pool", lambda g: g.affine_select(
        u01[:], u01[:], pattern=[[1, 128]], compare_op=ALU.is_ge,
        fill=0.0, base=0, channel_multiplier=-1), deps=[e_idf], tag="u01")
    e_const = cload + [e_u01, e_idf, e_idb]

    e_ucast = []
    for k in range(NK):
        sl = k % 2
        eng = "sp" if sl == 0 else "act"
        e_ld = s.ev(eng, lambda q, k=k, sl=sl: q.dma_start(
            out=uring[sl][:], in_=d_u[k * 128:(k + 1) * 128, :]),
            writes=[res(("uring", sl))], sem=f"u{sl}", incv=16, tag=f"uld{k}")
        e_c = s.ev("act", lambda a, k=k, sl=sl: a.activation(
            u_bf[:, k * TE:(k + 1) * TE], uring[sl][:], AF.Copy),
            deps=[e_ld], reads=[res(("uring", sl))],
            writes=[res(("ubf", k))], tag=f"ucast{k}")
        e_ucast.append(e_c)
    CGS = [(0, 512), (512, 1024), (1024, 1027)]
    e_nrm_all = []
    for cg, (c0, c1) in enumerate(CGS):
        n = c1 - c0
        prev_mm = None
        for k in range(NK):
            sl = k % 2
            e_sq = s.ev("act", lambda a, k=k, c0=c0, c1=c1, sl=sl: a.activation(
                sqr[sl][:, 0:c1 - c0], u_bf[:, k * TE + c0:k * TE + c1],
                AF.Square), deps=[e_ucast[k]], writes=[res(("sqr", sl))],
                tag=f"sq{cg}_{k}")
            e_m1 = s.ev("pe", lambda t, k=k, c0=c0, c1=c1, st=(k == 0), sp=(k == NK - 1): t.matmul(
                pf[0][0:1, 0:c1 - c0], onescol[:], u_bf[:, k * TE + c0:k * TE + c1],
                start=st, stop=sp), deps=[e_ucast[k], e_onescol],
                reads=[], writes=([res(("pf", 0))] if k == 0 else []),
                tag=f"mmM{cg}_{k}")
            e_m2 = s.ev("pe", lambda t, k=k, n=n, sl=sl, st=(k == 0), sp=(k == NK - 1): t.matmul(
                pf[1][0:1, 0:n], onescol[:], sqr[sl][:, 0:n],
                start=st, stop=sp), deps=[e_sq],
                reads=[res(("sqr", sl))],
                writes=([res(("pf", 1))] if k == 0 else []),
                tag=f"mmS{cg}_{k}")
            prev_mm = (e_m1, e_m2)
        e_em = s.ev("dve", lambda v, n=n: v.tensor_scalar_mul(
            mrow[:, 0:n], pf[0][0:1, 0:n], INV_D), deps=[prev_mm[0]],
            reads=[res(("pf", 0))], writes=[res("mrow")], tag=f"evm{cg}")
        e_es = s.ev("dve", lambda v, n=n: v.tensor_scalar_mul(
            srow[:, 0:n], pf[1][0:1, 0:n], INV_D), deps=[prev_mm[1]],
            reads=[res(("pf", 1))], writes=[res("srow")], tag=f"evs{cg}")
        e_msq = s.ev("act", lambda a, n=n: a.activation(
            scrow[:, 0:n], mrow[:, 0:n], AF.Square), deps=[e_em],
            writes=[res("scrow")], tag=f"msq{cg}")
        e_var = s.ev("dve", lambda v, n=n: v.tensor_sub(
            srow[:, 0:n], srow[:, 0:n], scrow[:, 0:n]), deps=[e_es, e_msq],
            writes=[res("srow")], tag=f"var{cg}")
        e_ln = s.ev("act", lambda a, n=n: a.activation(
            srow[:, 0:n], srow[:, 0:n], AF.Ln, bias=c_eps[:]),
            deps=[e_var, e_ceps], writes=[res("srow")], tag=f"vln{cg}")
        e_rs = s.ev("act", lambda a, n=n: a.activation(
            srow[:, 0:n], srow[:, 0:n], AF.Exp, scale=c_nhalf[:]),
            deps=[e_ln, e_cnh], writes=[res("srow")], tag=f"rstd{cg}")
        e_bm = s.ev("pe", lambda t, n=n: t.matmul(
            pf[2][:, 0:n], ones1f[:], mrow[:, 0:n], start=True, stop=True),
            deps=[e_em, e_ones1f], writes=[res(("pf", 2))], tag=f"bcm{cg}")
        e_vm = s.ev("dve", lambda v, n=n, c0=c0: v.tensor_copy(
            mbc[:, c0:c0 + n], pf[2][:, 0:n]), deps=[e_bm],
            reads=[res(("pf", 2))], writes=[res("mbc")], tag=f"bmc{cg}")
        e_bs = s.ev("pe", lambda t, n=n: t.matmul(
            pf[3][:, 0:n], ones1f[:], srow[:, 0:n], start=True, stop=True),
            deps=[e_rs], writes=[res(("pf", 3))], tag=f"bcs{cg}")
        e_vs = s.ev("dve", lambda v, n=n, c0=c0: v.tensor_copy(
            sbc[:, c0:c0 + n], pf[3][:, 0:n]), deps=[e_bs],
            reads=[res(("pf", 3))], writes=[res("sbc")], tag=f"bsc{cg}")
        e_nrm_all += [e_vm, e_vs]
    e_ln1 = []
    for k in range(NK):
        sl = k % 2
        e_t = s.ev("dve", lambda v, k=k, sl=sl: v.tensor_sub(
            uring[sl][:], u_bf[:, k * TE:(k + 1) * TE], mbc[:]),
            deps=e_nrm_all + [e_ucast[k]], writes=[res(("uring", sl))],
            reads=[res("mbc")], tag=f"nsub{k}")
        e_n = s.ev("dve", lambda v, k=k, sl=sl: v.tensor_mul(
            ln1[:, k * TE:(k + 1) * TE], uring[sl][:], sbc[:]),
            deps=[e_t], reads=[res(("uring", sl)), res("sbc")],
            writes=[res(("ln1", k))], tag=f"nmul{k}")
        e_ln1.append(e_n)
    if debug >= 2:
        dbg_outs["ln1"] = None

    s.fence()

    e_wa = s.ev("pool", lambda g: g.dma_start(out=wa_sb[:], in_=d_wa),
                deps=[e_smld], sem="c_dma", incv=16, tag="wald")
    e_apsum = []
    for cg in range(2):
        c0, c1 = 3 + cg * 512, 3 + (cg + 1) * 512
        last = None
        for k in range(NK):
            last = s.ev("pe", lambda t, k=k, c0=c0, c1=c1, st=(k == 0), sp=(k == NK - 1), cg=cg: t.matmul(
                pf[4][0:32, 0:512],
                wa_sb[:, k * 32:(k + 1) * 32],
                ln1[:, k * TE + c0:k * TE + c1], start=st, stop=sp),
                deps=[e_ln1[k], e_wa],
                writes=([res(("pf", 4))] if k == 0 else []), tag=f"Amm{cg}_{k}")
        e_ev = s.ev("dve", lambda v, cg=cg: v.tensor_copy(
            a_sb[:, cg * 512:(cg + 1) * 512], pf[4][0:32, 0:512]),
            deps=[last], reads=[res(("pf", 4))], writes=[res("araw")],
            tag=f"Aev{cg}")
        e_apsum.append(e_ev)
    e_abs = s.ev("act", lambda a: a.activation(
        zeros32[:], a_sb[:], AF.Abs), deps=e_apsum,
        reads=[res("araw")], writes=[res("z32")], tag="abs")
    e_exn = s.ev("act", lambda a: a.activation(
        zeros32[:], zeros32[:], AF.Exp, scale=-1.0), deps=[e_abs],
        writes=[res("z32")], tag="exn")
    e_lnp = s.ev("act", lambda a: a.activation(
        zeros32[:], zeros32[:], AF.Ln, bias=1.0), deps=[e_exn],
        writes=[res("z32")], tag="lnp")
    e_rel = s.ev("dve", lambda v: v.tensor_single_scalar(
        g_sb[:], a_sb[:], 0.0, ALU.max), deps=e_apsum, reads=[res("araw")],
        writes=[res("g32")], tag="rel")
    e_a = s.ev("dve", lambda v: v.scalar_tensor_tensor(
        a_sb[:], g_sb[:], -1.0, zeros32[:], op0=ALU.mult, op1=ALU.subtract),
        deps=[e_lnp, e_rel], reads=[res("z32"), res("g32")],
        writes=[res("araw")], tag="mka")
    e_z32 = s.ev("pool", lambda g: g.memset(zeros32[:], 0.0), deps=[e_a],
                 writes=[res("z32")], tag="z32")
    e_scan = s.ev("dve", lambda v: v.tensor_tensor_scan(
        g_sb[:], a_sb[:], zeros32[:], 0.0, ALU.add, ALU.add),
        deps=[e_z32, e_a], reads=[res("araw"), res("z32")],
        writes=[res("g32")], tag="scan")
    e_gt = []
    for gg in range(8):
        e_tr = s.ev("pe", lambda t, gg=gg: t.transpose(
            pf[5][:, 0:32],
            g_sb[:, gg * 128:(gg + 1) * 128], identf[0:32, 0:32]),
            deps=[e_scan, e_idf], writes=[res(("pf", 5))],
            tag=f"gtr{gg}")
        e_gv = s.ev("dve", lambda v, gg=gg: v.tensor_copy(
            gt_sb[:, gg * 32:(gg + 1) * 32], pf[5][:, 0:32]),
            deps=[e_tr], reads=[res(("pf", 5))], writes=[res("gt")],
            tag=f"gte{gg}")
        e_gt.append(e_gv)
    e_gm = s.ev("sp", lambda sp: sp.dma_start(
        out=gmrow[:], in_=gt_sb[63:64, :]),
        deps=e_gt, sem="ge_dma", incv=16, tag="gm")
    e_ge = s.ev("sp", lambda sp: sp.dma_start(
        out=gerow[:], in_=gt_sb[127:128, :]),
        deps=[e_gm], sem="ge_dma", incv=16, tag="ge")
    gmn_r = res("gmnrow")
    e_gn0 = s.ev("dve", lambda v: v.tensor_copy(
        gmnrow[:, 0:224], gmrow[:, 32:256]), deps=[e_gm, e_ge],
        writes=[gmn_r], tag="gn0")
    e_gn1 = s.ev("dve", lambda v: v.tensor_copy(
        gmnrow[:, 224:256], gerow[:, 224:256]), deps=[e_ge],
        writes=[gmn_r], tag="gn1")
    ar = res("arow")
    e_ar0 = s.ev("dve", lambda v: v.tensor_copy(arow[:, 0:32], gmnrow[:, 0:32]),
                 deps=[e_gn0, e_gn1], reads=[gmn_r], writes=[ar], tag="ar0")
    e_ar1 = s.ev("dve", lambda v: v.tensor_sub(
        arow[:, 32:256], gmnrow[:, 32:256], gmnrow[:, 0:224]),
        deps=[e_ar0], reads=[gmn_r], writes=[ar], tag="ar1")
    e_expa = s.ev("act", lambda a: a.activation(arow[:], arow[:], AF.Exp),
                  deps=[e_ar1], writes=[ar], tag="expa")
    e_eab = s.ev("pe", lambda t: t.matmul(
        pf[5][:, 0:256], ones1f[:], arow[:], start=True, stop=True),
        deps=[e_expa], writes=[res(("pf", 5))], tag="eabm")
    e_eabv = s.ev("dve", lambda v: v.tensor_copy(expabc[:], pf[5][:, 0:256]),
                  deps=[e_eab], reads=[res(("pf", 5))], writes=[res("expabc")],
                  tag="eabv")
    e_zdone = []
    nslab_z = NJZ // 2
    slab_consumed = [None, None]
    slab_loaded = [None, None]
    for sidx in range(nslab_z):
        buf = sidx % 2
        e_ld = s.ev("pool", lambda g, sidx=sidx, buf=buf: g.dma_start(
            out=wslab[buf][:], in_=d_wz[:, sidx * 2 * NK * 128:(sidx + 1) * 2 * NK * 128]),
            deps=[slab_consumed[buf]], sem=f"ws{buf}", incv=16,
            tag=f"zsl{sidx}")
        slab_loaded[buf] = e_ld
        for jloc in range(2):
            jz = sidx * 2 + jloc
            for cg in range(2):
                c0 = 3 + cg * 512
                bank = (jz * 2 + cg) % 2
                last = None
                for k in range(NK):
                    last = s.ev("pe", lambda t, jloc=jloc, k=k, c0=c0, bank=bank, buf=buf, st=(k == 0), sp=(k == NK - 1): t.matmul(
                        pf[bank][:, 0:512],
                        wslab[buf][:, (jloc * NK + k) * 128:(jloc * NK + k + 1) * 128],
                        ln1[:, k * TE + c0:k * TE + c0 + 512],
                        start=st, stop=sp),
                        deps=[e_ld, e_ln1[k]],
                        writes=([res(("pf", bank))] if k == 0 else []),
                        tag=f"zmm{jz}_{cg}_{k}")
                slab_consumed[buf] = last
                e_ev = s.ev("act", lambda a, jz=jz, cg=cg, bank=bank: a.activation(
                    z_sb[:, jz * T + cg * 512:jz * T + (cg + 1) * 512],
                    pf[bank][:, 0:512], AF.Silu, bias=zb_sb[:, jz:jz + 1]),
                    deps=[last, e_zbld], reads=[res(("pf", bank))],
                    writes=[res(("z", jz))], tag=f"zev{jz}_{cg}")
                e_zdone.append(e_ev)

    for sidx in range(NJX // 2):
        buf = sidx % 2
        e_ld = s.ev("pool", lambda g, sidx=sidx, buf=buf: g.dma_start(
            out=wslab[buf][:], in_=d_wxbc[:, sidx * 2 * NK * 128:(sidx + 1) * 2 * NK * 128]),
            deps=[slab_consumed[buf]], sem=f"ws{buf}", incv=16,
            tag=f"xsl{sidx}")
        slab_loaded[buf] = e_ld
        for jloc in range(2):
            jc = sidx * 2 + jloc
            pslot = jc % 2
            e_evs = []
            for cg, (c0, c1) in enumerate(CGS):
                bank = cg % 2
                n = c1 - c0
                last = None
                for k in range(NK):
                    last = s.ev("pe", lambda t, jloc=jloc, k=k, c0=c0, n=n, bank=bank, buf=buf, st=(k == 0), sp=(k == NK - 1): t.matmul(
                        pf[bank][:, 0:n],
                        wslab[buf][:, (jloc * NK + k) * 128:(jloc * NK + k + 1) * 128],
                        ln1[:, k * TE + c0:k * TE + c0 + n],
                        start=st, stop=sp),
                        deps=[e_ld, e_ln1[k]],
                        writes=([res(("pf", bank))] if k == 0 else []),
                        tag=f"xmm{jc}_{cg}_{k}")
                slab_consumed[buf] = last
                e_ev = s.ev("act", lambda a, c0=c0, n=n, bank=bank, pslot=pslot: a.activation(
                    xpre[pslot][:, c0:c0 + n], pf[bank][:, 0:n], AF.Copy),
                    deps=[last], reads=[res(("pf", bank))],
                    writes=[res(("xpre", pslot))], tag=f"xev{jc}_{cg}")
                e_evs.append(e_ev)
            tgt = [x_cm, b_cm, c_cm][jc // 16]
            jj = jc % 16
            e_c1 = s.ev("dve", lambda g, jc=jc, pslot=pslot: g.tensor_scalar(
                ctmp[pslot][:], xpre[pslot][:, 0:T], cw_sb[:, jc * 4:jc * 4 + 1],
                cb_sb[:, jc:jc + 1], ALU.mult, ALU.add),
                deps=e_evs + [e_cw, e_cb], reads=[res(("xpre", pslot))],
                writes=[res(("ctmp", pslot))], tag=f"cv0_{jc}")
            e_c2 = s.ev("dve", lambda g, jc=jc, pslot=pslot: g.scalar_tensor_tensor(
                ctmp[pslot][:], xpre[pslot][:, 1:T + 1],
                cw_sb[:, jc * 4 + 1:jc * 4 + 2], ctmp[pslot][:],
                op0=ALU.mult, op1=ALU.add), deps=[e_c1],
                reads=[res(("xpre", pslot))], writes=[res(("ctmp", pslot))],
                tag=f"cv1_{jc}")
            e_c3 = s.ev("dve", lambda g, jc=jc, pslot=pslot: g.scalar_tensor_tensor(
                ctmp[pslot][:], xpre[pslot][:, 2:T + 2],
                cw_sb[:, jc * 4 + 2:jc * 4 + 3], ctmp[pslot][:],
                op0=ALU.mult, op1=ALU.add), deps=[e_c2],
                reads=[res(("xpre", pslot))], writes=[res(("ctmp", pslot))],
                tag=f"cv2_{jc}")
            e_c4 = s.ev("dve", lambda g, jc=jc, pslot=pslot, tgt=tgt, jj=jj: g.scalar_tensor_tensor(
                tgt[:, jj * T:(jj + 1) * T], xpre[pslot][:, 3:T + 3],
                cw_sb[:, jc * 4 + 3:jc * 4 + 4], ctmp[pslot][:],
                op0=ALU.mult, op1=ALU.add), deps=[e_c3],
                reads=[res(("xpre", pslot)), res(("ctmp", pslot))],
                writes=[res(("cm", jc))], tag=f"cv3_{jc}")

    e_dbn = s.ev("act", lambda a: a.activation(dbnd[:], gt_sb[:], AF.Exp),
                 deps=e_gt + [slab_consumed[0], slab_consumed[1]],
                 reads=[res("gt")], writes=[res("dbnd")], tag="dbnd")
    e_decay = [e_dbn]
    ecps = [res(("ecp", c)) for c in range(8)]
    e_b = s.ev("pe", lambda t: t.matmul(
        pf[5][:, 0:256], ones1f[:], gmrow[:, 0:256],
        start=True, stop=True), deps=[e_gm, e_ones1f],
        writes=[res(("pf", 5))], tag="gb")
    e_dp = s.ev("dve", lambda v: v.tensor_sub(
        ecol[:, 0:256], gt_sb[:, 0:256], pf[5][:, 0:256]),
        deps=[e_b], reads=[res(("pf", 5)), res("gt")],
        writes=ecps, tag="dp")
    e_di = s.ev("act", lambda a: a.activation(
        din[:, 0:256], ecol[:, 0:256], AF.Exp),
        deps=[e_dp], reads=ecps, writes=[res("din")], tag="di")
    e_rc = s.ev("act", lambda a: a.activation(
        rcol[:, 0:256], ecol[:, 0:256], AF.Exp, scale=-1.0),
        deps=[e_dp], reads=ecps, writes=[res("rcol")], tag="rc")
    e_b2 = s.ev("pe", lambda t: t.matmul(
        pf[4][:, 0:256], ones1f[:], gmnrow[:, 0:256],
        start=True, stop=True), deps=[e_gn0, e_gn1, e_b],
        reads=[gmn_r], writes=[res(("pf", 4))], tag="ge")
    e_ep = s.ev("dve", lambda v: v.tensor_sub(
        ecol[:, 0:256], pf[4][:, 0:256], gt_sb[:, 0:256]),
        deps=[e_b2, e_di, e_rc], reads=[res(("pf", 4)), res("gt")],
        writes=ecps, tag="ep")
    e_ec = s.ev("act", lambda a: a.activation(
        ecol[:, 0:256], ecol[:, 0:256], AF.Exp),
        deps=[e_ep], writes=ecps, tag="ec")
    for c in range(8):
        res(("ecp", c)).w = e_ec
    e_decay += [e_di, e_rc, e_ec]

    e_szb = s.ev("pool", lambda g_: g_.memset(s_bf[:], 0.0),
                 deps=[e_decay[-1], e_eabv, e_scan] + e_apsum, tag="szb")
    for h in range(NH):
        res(("sbf", h)).w = e_szb

    wos_consumed = [slab_consumed[0], slab_consumed[0]]
    e_wosld = {}
    e_x0ld = {}
    for jf in range(2):
        buf = jf % 2
        e_wosld[jf] = s.ev("pool", lambda g_, jf=jf, buf=buf: g_.dma_start(
            out=wos[buf][:],
            in_=d_wout[:, jf * NK * 128:(jf + 1) * NK * 128]),
            deps=[wos_consumed[buf]], sem=f"ws{buf}", incv=16,
            tag=f"osl{jf}")

    s.fence()
    if phases <= 2:
        return locals()

    e_supd_last = {}
    bank_rd = {i: [] for i in range(8)}
    bank_wr = {i: None for i in range(8)}

    def epoch_dep(bank):
        d = bank_rd[bank] + ([bank_wr[bank]] if bank_wr[bank] else [])
        bank_rd[bank] = []
        return d

    pending_su = []

    def emit_pending_su():
        for fn in pending_su:
            fn()
        del pending_su[:]

    for c in range(NG):
        e_xtok_cp = {}
        for hp in range(16):
            pb = hp % 2
            e_t = s.ev("pe", lambda t, hp=hp, c=c, pb=pb: t.transpose(
                pbt[pb][:], x_cm[:, hp * T + c * 128:hp * T + c * 128 + 128],
                identb[:]), deps=[e_idb] + epoch_dep(6 + pb),
                reads=[res(("cm", hp))], tag=f"trx{c}_{hp}")
            e_cp = s.ev("act", lambda a, hp=hp, pb=pb: a.activation(
                x_tok[:, hp * 128:(hp + 1) * 128], pbt[pb][:], AF.Copy),
                deps=[e_t], writes=[res(("xtok", c, hp))], tag=f"trxe{c}_{hp}")
            bank_rd[6 + pb].append(e_cp)
            e_xtok_cp[hp] = e_cp
            e_tb = s.ev("pe", lambda t, hp=hp, c=c, pb=pb: t.transpose(
                pbt[pb][:], b_cm[:, hp * T + c * 128:hp * T + c * 128 + 128],
                identb[:]), deps=[e_idb] + epoch_dep(6 + pb),
                reads=[res(("cm", 16 + hp))], tag=f"trb{c}_{hp}")
            e_cb2 = s.ev("dve", lambda v, hp=hp, pb=pb: v.tensor_copy(
                b_tok[:, hp * 128:(hp + 1) * 128], pbt[pb][:]),
                deps=[e_tb], writes=[res(("btok", c, hp))], tag=f"trbe{c}_{hp}")
            bank_rd[6 + pb].append(e_cb2)
        for h in range(NH):
            hp, hh = h // 2, h % 2
            e_xd = s.ev("act", lambda a, h=h, hp=hp, hh=hh, c=c: a.activation(
                xd[:, hp * 128 + hh * 64:hp * 128 + (hh + 1) * 64],
                x_tok[:, hp * 128 + hh * 64:hp * 128 + (hh + 1) * 64],
                AF.Copy, scale=ecol[:, c * 32 + h:c * 32 + h + 1]),
                deps=[e_xtok_cp[hp], R[("ecp", c)].w],
                reads=[res(("xtok", c, hp))],
                writes=[res(("xd", c, h))], tag=f"xd{c}_{h}")
        emit_pending_su()
        for half in range(2):
            hs = list(range(half * 16, half * 16 + 16))
            e_m1 = {}
            e_mte = {}
            e_y2 = {}
            e_s1 = {}
            LAG = 4
            mtbank = [pf[0], pf[1]]
            for i in range(16 + LAG):
                if i < 16:
                    h = hs[i]
                    hh = h % 2
                    bk = h % 2
                    mts = h % 16
                    ccol = (h // 2) * T + c * 128
                    e_m1[h] = s.ev("pe", lambda t, h=h, hh=hh, bk=bk, ccol=ccol: t.matmul(
                        mtbank[bk][:, 0:128],
                        b_cm[hh * 64:(hh + 1) * 64, ccol:ccol + 128],
                        c_cm[hh * 64:(hh + 1) * 64, ccol:ccol + 128],
                        start=True, stop=True),
                        deps=epoch_dep(bk) +
                        [R[("cm", 16 + h // 2)].w, R[("cm", 32 + h // 2)].w],
                        tag=f"m1_{c}_{h}")
                    e_mte[h] = s.ev("dve", lambda v, h=h, c=c, bk=bk, mts=mts: v.scalar_tensor_tensor(
                        mtr[mts][:], mtbank[bk][:, 0:128],
                        rcol[:, c * 32 + h:c * 32 + h + 1], u01[:],
                        op0=ALU.mult, op1=ALU.mult),
                        deps=[e_m1[h], e_u01, R["rcol"].w],
                        writes=[res(("mtr", mts))], tag=f"mte{c}_{h}")
                    bank_rd[bk].append(e_mte[h])
                if i >= LAG:
                    h = hs[i - LAG]
                    hp, hh = h // 2, h % 2
                    yb = 2 + h % 2
                    sbk = 4 + h % 2
                    qc = ((h // 2) % 8) * 64
                    mts = h % 16
                    ccol = (h // 2) * T + c * 128
                    ddy = epoch_dep(yb) if (h - half * 16) < 2 else []
                    dds = epoch_dep(sbk) if (h - half * 16) < 2 else []
                    e_ya = s.ev("pe", lambda t, h=h, hp=hp, hh=hh, yb=yb, qc=qc, mts=mts: t.matmul(
                        pf[yb][:, qc:qc + 64], mtr[mts][:],
                        x_tok[:, hp * 128 + hh * 64:hp * 128 + (hh + 1) * 64],
                        start=True, stop=False),
                        deps=ddy + [e_mte[h], e_xtok_cp[hp]],
                        reads=[res(("mtr", mts))], tag=f"y1_{c}_{h}")
                    e_y2[h] = s.ev("pe", lambda t, h=h, hh=hh, yb=yb, qc=qc, ccol=ccol: t.matmul(
                        pf[yb][:, qc:qc + 64],
                        c_cm[hh * 64:(hh + 1) * 64, ccol:ccol + 128],
                        s_bf[hh * 64:(hh + 1) * 64, (h // 2) * 64:(h // 2 + 1) * 64],
                        start=False, stop=True),
                        deps=[e_ya], reads=[res(("sbf", h))],
                        tag=f"y2_{c}_{h}")
                    e_s1[h] = s.ev("pe", lambda t, h=h, hp=hp, hh=hh, sbk=sbk, qc=qc: t.matmul(
                        pf[sbk][hh * 64:(hh + 1) * 64, qc:qc + 64],
                        b_tok[:, hp * 128 + hh * 64:hp * 128 + (hh + 1) * 64],
                        xd[:, hp * 128 + hh * 64:hp * 128 + (hh + 1) * 64],
                        start=True, stop=True),
                        deps=dds + [R[("btok", c, hp)].w, R[("xd", c, h)].w],
                        tag=f"s1_{c}_{h}")
            last_y = {2: e_y2[half * 16 + 14], 3: e_y2[half * 16 + 15]}
            last_s = {4: e_s1[half * 16 + 14], 5: e_s1[half * 16 + 15]}
            for h in hs:
                yb = 2 + h % 2
                sbk = 4 + h % 2
                hh = h % 2
                qc = ((h // 2) % 8) * 64
                e_ye = s.ev("act", lambda a, h=h, c=c, yb=yb, qc=qc: a.activation(
                    y_tok[:, c * D + h * 64:c * D + (h + 1) * 64],
                    pf[yb][:, qc:qc + 64], AF.Copy,
                    scale=din[:, c * 32 + h:c * 32 + h + 1]),
                    deps=[last_y[yb], R["din"].w],
                    writes=[res(("ytk", c, h))], tag=f"ye{c}_{h}")
                bank_rd[yb].append(e_ye)
                def mk_su(h=h, hh=hh, c=c, sbk=sbk, qc=qc,
                          ls=last_s[sbk], ey2=e_y2[h]):
                    e_su = s.ev("dve", lambda v: v.scalar_tensor_tensor(
                        s_bf[hh * 64:(hh + 1) * 64, (h // 2) * 64:(h // 2 + 1) * 64],
                        s_bf[hh * 64:(hh + 1) * 64, (h // 2) * 64:(h // 2 + 1) * 64],
                        expabc[hh * 64:(hh + 1) * 64, c * 32 + h:c * 32 + h + 1],
                        pf[sbk][hh * 64:(hh + 1) * 64, qc:qc + 64],
                        op0=ALU.mult, op1=ALU.add),
                        deps=[ls, e_eabv, ey2],
                        writes=[res(("sbf", h))], tag=f"su{c}_{h}")
                    bank_rd[sbk].append(e_su)
                    e_supd_last[h] = e_su
                if half == 0:
                    mk_su()
                else:
                    pending_su.append(mk_su)
    emit_pending_su()
    if phases <= 3:
        return locals()
    e_ccw = s.ev("sp", lambda sp: sp.dma_start(out=d_ccin, in_=s_bf[:]),
                 deps=[e_supd_last[h] for h in range(NH)],
                 sem="ge_dma", incv=16, tag="ccw")
    e_cc = s.ev("pool", lambda g_: g_.collective_compute(
        "AllGather", ALU.bypass, [[0, 1], [2, 3], [4, 5], [6, 7]],
        [d_ccin.opt()], [d_ccout.opt()]), deps=[e_ccw], sem="cc", incv=1,
        tag="cc")
    e_sld = s.ev("pool", lambda g_: g_.dma_start(
        out=s_inb[:], in_=d_ccout[0:128, :]),
        deps=[e_cc, slab_consumed[1]], sem="ccd", incv=16, tag="sld")
    e_smul = s.ev("dve", lambda v: v.tensor_scalar_mul(
        s_inb[:], s_inb[:], smask_sb[:]), deps=[e_sld, e_smld],
        writes=[res("sinb")], tag="smul")
    def emit_p4b(c):
        for hp in range(16):
            pb = hp % 2
            yfs = hp % 2
            e_t = s.ev("pe", lambda t, c=c, hp=hp, pb=pb: t.transpose(
                pbt[pb][:], y_tok[:, c * D + hp * 128:c * D + (hp + 1) * 128],
                identb[:]),
                deps=[R[("ytk", c, 2 * hp)].w, R[("ytk", c, 2 * hp + 1)].w] +
                     epoch_dep(6 + pb),
                tag=f"ytr{c}_{hp}")
            e_o1 = s.ev("dve", lambda v, c=c, hp=hp, pb=pb, yfs=yfs: v.scalar_tensor_tensor(
                yfr[yfs][:], x_cm[:, hp * T + c * 128:hp * T + c * 128 + 128],
                drep_sb[:, hp:hp + 1], pbt[pb][:],
                op0=ALU.mult, op1=ALU.add),
                deps=[e_t, e_drepld], reads=[res(("cm", hp))],
                writes=[res(("yfr", yfs))], tag=f"yo1{c}_{hp}")
            bank_rd[6 + pb].append(e_o1)
            e_o2 = s.ev("dve", lambda v, c=c, hp=hp, yfs=yfs: v.tensor_mul(
                z_sb[:, hp * T + c * 128:hp * T + c * 128 + 128],
                yfr[yfs][:],
                z_sb[:, hp * T + c * 128:hp * T + c * 128 + 128]),
                deps=[e_o1], reads=[res(("yfr", yfs))],
                writes=[res(("z", hp))], tag=f"yo2{c}_{hp}")
    for c in range(NG):
        e_b = {}
        for h in range(NH):
            hh = h % 2
            half = h // 16
            yb = 2 * half + (h % 2)
            qc = ((h % 16) // 2) * 64
            ccol = (h // 2) * T + c * 128
            dd = epoch_dep(yb) if (h % 16) < 2 else []
            e_b[h] = s.ev("pe", lambda t, h=h, hh=hh, yb=yb, qc=qc, ccol=ccol: t.matmul(
                pf[yb][:, qc:qc + 64],
                c_cm[hh * 64:(hh + 1) * 64, ccol:ccol + 128],
                s_inb[hh * 64:(hh + 1) * 64, (h // 2) * 64:(h // 2 + 1) * 64],
                start=True, stop=True),
                deps=dd + [e_smul, R[("cm", 32 + h // 2)].w],
                tag=f"bnd{c}_{h}")
        last_b = {0: e_b[14], 1: e_b[15], 2: e_b[30], 3: e_b[31]}
        for h in range(NH):
            half = h // 16
            yb = 2 * half + (h % 2)
            qc = ((h % 16) // 2) * 64
            if h % 2 == 0:
                e_ba = s.ev("dve", lambda v, h=h, c=c, yb=yb, qc=qc: v.scalar_tensor_tensor(
                    y_tok[:, c * D + h * 64:c * D + (h + 1) * 64],
                    pf[yb][:, qc:qc + 64],
                    dbnd[:, c * 32 + h:c * 32 + h + 1],
                    y_tok[:, c * D + h * 64:c * D + (h + 1) * 64],
                    op0=ALU.mult, op1=ALU.add),
                    deps=[last_b[yb], R["dbnd"].w],
                    writes=[res(("ytk", c, h))], tag=f"ba{c}_{h}")
                bank_rd[yb].append(e_ba)
            else:
                ybs = (h // 2) % 4
                e_be = s.ev("act", lambda a, h=h, c=c, yb=yb, qc=qc, ybs=ybs: a.activation(
                    ybr[ybs][:], pf[yb][:, qc:qc + 64], AF.Copy,
                    scale=dbnd[:, c * 32 + h:c * 32 + h + 1]),
                    deps=[last_b[yb], R["dbnd"].w],
                    writes=[res(("ybr", ybs))], tag=f"be{c}_{h}")
                bank_rd[yb].append(e_be)
                e_ba = s.ev("dve", lambda v, h=h, c=c, ybs=ybs: v.tensor_add(
                    y_tok[:, c * D + h * 64:c * D + (h + 1) * 64],
                    y_tok[:, c * D + h * 64:c * D + (h + 1) * 64],
                    ybr[ybs][:]),
                    deps=[e_be], reads=[res(("ybr", ybs))],
                    writes=[res(("ytk", c, h))], tag=f"ba{c}_{h}")
        if c >= 1:
            emit_p4b(c - 1)

    emit_p4b(NG - 1)
    s.fence()
    if phases <= 35:
        return locals()

    s.fence()
    if phases <= 4:
        return locals()

    e_x1 = {}
    for jf in range(NK):
        buf = jf % 2
        sl = jf % 2
        if jf in e_wosld:
            e_ld = e_wosld[jf]
        else:
            e_ld = s.ev("pool", lambda g_, jf=jf, buf=buf: g_.dma_start(
                out=wos[buf][:],
                in_=d_wout[:, jf * NK * 128:(jf + 1) * NK * 128]),
                deps=[wos_consumed[buf]], sem=f"ws{buf}", incv=16,
                tag=f"osl{jf}")
        e_x0 = s.ev("sp", lambda sp, jf=jf, sl=sl: sp.dma_start(
            out=x0r[sl][:], in_=d_u[jf * 128:(jf + 1) * 128, 3:TE]),
            writes=[res(("x0r", sl))], sem=f"x0{sl}", incv=16,
            tag=f"x0{jf}")
        for th in range(2):
            bank = (jf * 2 + th) % 2
            last = None
            for k in range(NK):
                last = s.ev("pe", lambda t, k=k, th=th, bank=bank, buf=buf, st=(k == 0), sp_=(k == NK - 1): t.matmul(
                    pf[bank][:, 0:512],
                    wos[buf][:, k * 128:(k + 1) * 128],
                    z_sb[:, k * T + th * 512:k * T + (th + 1) * 512],
                    start=st, stop=sp_),
                    deps=[e_ld], reads=[res(("z", k))],
                    writes=([res(("pf", bank))] if k == 0 else []),
                    tag=f"wo{jf}_{th}_{k}")
            wos_consumed[buf] = last
            e_e = s.ev("dve", lambda v, jf=jf, th=th, sl=sl, bank=bank: v.tensor_add(
                x1[:, jf * T + th * 512:jf * T + (th + 1) * 512],
                x0r[sl][:, th * 512:(th + 1) * 512], pf[bank][:, 0:512]),
                deps=[last, e_x0], reads=[res(("pf", bank)), res(("x0r", sl))],
                writes=[res(("x1", jf))], tag=f"x1e{jf}_{th}")
            e_x1[(jf, th)] = e_e
    if phases <= 5:
        return locals()
    e_x1c = []
    for k in range(NK):
        e_c = s.ev("act", lambda a, k=k: a.activation(
            x1bf[:, k * T:(k + 1) * T], x1[:, k * T:(k + 1) * T], AF.Copy),
            deps=[e_x1[(k, 0)], e_x1[(k, 1)]], reads=[res(("x1", k))],
            writes=[res(("x1bf", k))], tag=f"x1c{k}")
        e_x1c.append(e_c)
    CG2 = [(0, 512), (512, 1024)]
    e_bc2 = []
    x1bf_readers = {k: [] for k in range(NK)}
    for cg, (c0, c1) in enumerate(CG2):
        n = c1 - c0
        lastm = lasts_ = None
        for k in range(NK):
            sl = k % 2
            e_sq = s.ev("act", lambda a, k=k, c0=c0, n=n, sl=sl: a.activation(
                sq2[sl][:, 0:n], x1bf[:, k * T + c0:k * T + c0 + n],
                AF.Square), deps=[e_x1c[k]], writes=[res(("sq2", sl))],
                tag=f"q2{cg}_{k}")
            x1bf_readers[k].append(e_sq)
            lastm = s.ev("pe", lambda t, k=k, c0=c0, n=n, st=(k == 0), sp_=(k == NK - 1): t.matmul(
                pf[2][0:1, 0:n], onescol[:], x1bf[:, k * T + c0:k * T + c0 + n],
                start=st, stop=sp_), deps=[e_x1c[k], e_onescol],
                writes=([res(("pf", 2))] if k == 0 else []), tag=f"m2m{cg}_{k}")
            x1bf_readers[k].append(lastm)
            lasts_ = s.ev("pe", lambda t, k=k, n=n, sl=sl, st=(k == 0), sp_=(k == NK - 1): t.matmul(
                pf[3][0:1, 0:n], onescol[:], sq2[sl][:, 0:n],
                start=st, stop=sp_), deps=[e_sq], reads=[res(("sq2", sl))],
                writes=([res(("pf", 3))] if k == 0 else []), tag=f"s2m{cg}_{k}")
        e_em2 = s.ev("dve", lambda v, n=n: v.tensor_scalar_mul(
            m512[:, 0:n], pf[2][0:1, 0:n], INV_D), deps=[lastm],
            reads=[res(("pf", 2))], writes=[res("m2f")], tag=f"em2{cg}")
        e_es2 = s.ev("dve", lambda v, n=n: v.tensor_scalar_mul(
            s2row[:, 0:n], pf[3][0:1, 0:n], INV_D), deps=[lasts_],
            reads=[res(("pf", 3))], writes=[res("s2row")], tag=f"es2{cg}")
        e_mq2 = s.ev("act", lambda a, n=n: a.activation(
            scr2[:, 0:n], m512[:, 0:n], AF.Square), deps=[e_em2],
            reads=[res("m2f")], writes=[res("scr2")], tag=f"mq2{cg}")
        e_v2 = s.ev("dve", lambda v, n=n: v.tensor_sub(
            s2row[:, 0:n], s2row[:, 0:n], scr2[:, 0:n]), deps=[e_es2, e_mq2],
            writes=[res("s2row")], tag=f"v2{cg}")
        e_l2 = s.ev("act", lambda a, n=n: a.activation(
            scr2[:, 0:n], s2row[:, 0:n], AF.Ln, bias=c_eps[:]),
            deps=[e_v2, e_ceps, e_mq2], reads=[res("s2row")],
            writes=[res("scr2")], tag=f"l2{cg}")
        e_r2 = s.ev("act", lambda a, n=n: a.activation(
            s2row[:, 0:n], scr2[:, 0:n], AF.Exp, scale=c_nhalf[:]),
            deps=[e_l2, e_cnh], reads=[res("scr2")],
            writes=[res("s2row")], tag=f"r2{cg}")
        e_si = s.ev("act", lambda a, n=n: a.activation(
            scr2[:, 0:n], scr2[:, 0:n], AF.Exp, scale=c_phalf[:]),
            deps=[e_r2, e_cph], writes=[res("scr2")], tag=f"si{cg}")
        e_m2c = s.ev("act", lambda a, n=n, c0=c0: a.activation(
            m2row[0:1, c0:c0 + n], m512[:, 0:n], AF.Copy), deps=[e_em2],
            reads=[res("m2f")], writes=[res("m2row")], tag=f"m2c{cg}")
        e_m2ff = s.ev("dve", lambda v, n=n, c0=c0: v.tensor_copy(
            m2f[0:1, c0:c0 + n], m512[:, 0:n]), deps=[e_em2],
            reads=[res("m2f")], writes=[res("m2ffull")], tag=f"m2ff{cg}")
        for (dstrow, dstbc, rres, dep) in [
                (m512[:, 0:n], mbc2, "m2f", e_em2),
                (s2row[:, 0:n], sbc2, "s2row", e_r2),
                (scr2[:, 0:n], sigbc, "scr2", e_si)]:
            e_b = s.ev("pe", lambda t, dstrow=dstrow, n=n: t.matmul(
                pf[4][:, 0:n], ones1f[:], dstrow, start=True, stop=True),
                deps=[dep, e_ones1f], reads=[res(rres)],
                writes=[res(("pf", 4))], tag=f"bc2{cg}")
            e_v = s.ev("dve", lambda v, dstbc=dstbc, c0=c0, n=n: v.tensor_copy(
                dstbc[:, c0:c0 + n], pf[4][:, 0:n]), deps=[e_b],
                reads=[res(("pf", 4))], writes=[res(id(dstbc))], tag=f"bv2{cg}")
            e_bc2.append(e_v)
    e_ln2 = []
    for k in range(NK):
        sl = k % 2
        e_t = s.ev("dve", lambda v, k=k, sl=sl: v.tensor_sub(
            x0r[sl][:], x1[:, k * T:(k + 1) * T], mbc2[:]),
            deps=e_bc2 + [e_x1[(k, 0)], e_x1[(k, 1)]],
            reads=[res(id(mbc2)), res(("x1", k))],
            writes=[res(("x0r", sl))], tag=f"n2s{k}")
        e_n = s.ev("dve", lambda v, k=k, sl=sl: v.tensor_mul(
            ln2[:, k * T:(k + 1) * T], x0r[sl][:], sbc2[:]),
            deps=[e_t], reads=[res(("x0r", sl)), res(id(sbc2))],
            writes=[res(("ln2", k))], tag=f"n2m{k}")
        e_ln2.append(e_n)
    lastrs = None
    for k in range(NK):
        lastrs = s.ev("pe", lambda t, k=k, st=(k == 0), sp_=(k == NK - 1): t.matmul(
            pf[3][0:8, 0:1], wrf_sb[:, k * 8:(k + 1) * 8], onescolf[:],
            start=st, stop=sp_), deps=[e_wrf, e_onescolf],
            writes=([res(("pf", 3))] if k == 0 else []), tag=f"rs{k}")
    e_rsv = s.ev("dve", lambda v: v.tensor_copy(rs_col[:], pf[3][0:8, 0:1]),
                 deps=[lastrs], reads=[res(("pf", 3))], writes=[res("rscol")],
                 tag="rsv")
    e_rst = s.ev("pe", lambda t: t.transpose(
        pf[3][0:1, 0:8], rs_col[:], identf[0:8, 0:8]), deps=[e_rsv, e_idf],
        reads=[res("rscol")], writes=[res(("pf", 3))], tag="rst")
    e_nrs = s.ev("dve", lambda v: v.tensor_scalar_mul(
        nrs_row[:], pf[3][0:1, 0:8], -1.0), deps=[e_rst],
        reads=[res(("pf", 3))], writes=[res("nrs")], tag="nrs")
    e_sc_ev = []
    for cg in range(2):
        c0 = cg * 512
        last = None
        for k in range(NK):
            last = s.ev("pe", lambda t, k=k, c0=c0, st=(k == 0): t.matmul(
                pf[2][0:8, 0:512], wrf_sb[:, k * 8:(k + 1) * 8],
                x1[:, k * T + c0:k * T + c0 + 512], start=st, stop=False),
                deps=[e_wrf, e_x1[(k, 0)], e_x1[(k, 1)]],
                reads=[res(("x1", k))],
                writes=([res(("pf", 2))] if k == 0 else []), tag=f"rt{cg}_{k}")
        last = s.ev("pe", lambda t, c0=c0: t.matmul(
            pf[2][0:8, 0:512], nrs_row[:], m2f[0:1, c0:c0 + 512],
            start=False, stop=True),
            deps=[last, e_nrs, R["m2ffull"].w], reads=[res("nrs"), res("m2ffull")],
            tag=f"rtm{cg}")
        e_ev = s.ev("dve", lambda v, c0=c0: v.tensor_mul(
            scores[:, c0:c0 + 512], pf[2][0:8, 0:512], sbc2[0:8, c0:c0 + 512]),
            deps=[last] + e_bc2, reads=[res(("pf", 2)), res(id(sbc2))],
            writes=[res("scores")], tag=f"rte{cg}")
        e_sc_ev.append(e_ev)
    def emit_top2():
        for gg in range(8):
            e_t = s.ev("pe", lambda t, gg=gg: t.transpose(
                pf[3][:, 0:8], scores[0:8, gg * 128:(gg + 1) * 128],
                identf[0:8, 0:8]), deps=e_sc_ev + [e_idf], reads=[res("scores")],
                writes=[res(("pf", 3))], tag=f"st{gg}")
            e_v = s.ev("dve", lambda v, gg=gg: v.tensor_copy(
                sttok[:, gg * 8:(gg + 1) * 8], pf[3][:, 0:8]), deps=[e_t],
                reads=[res(("pf", 3))], writes=[res("sttok")], tag=f"stv{gg}")
            sl = sttok[:, gg * 8:(gg + 1) * 8]
            e1 = s.ev("dve", lambda v, sl=sl: v.reduce_max(r_m1[:], sl, axis=AX.X),
                      deps=[e_v], writes=[res("rm1")], tag=f"rm1{gg}")
            e2 = s.ev("dve", lambda v: v.tensor_scalar_mul(r_nm1[:], r_m1[:], -1.0),
                      deps=[e1], reads=[res("rm1")], writes=[res("rnm1")],
                      tag=f"rnm{gg}")
            e3 = s.ev("dve", lambda v, sl=sl: v.tensor_single_scalar(
                r_eq1[:], sl, r_m1[:], ALU.is_equal), deps=[e1],
                reads=[res("rm1")], writes=[res("req1")], tag=f"req1{gg}")
            e4 = s.ev("dve", lambda v, sl=sl: v.scalar_tensor_tensor(
                r_sc2[:], r_eq1[:], -1e30, sl, op0=ALU.mult, op1=ALU.add),
                deps=[e3], reads=[res("req1")], writes=[res("rsc2")],
                tag=f"rsc2{gg}")
            e5 = s.ev("dve", lambda v: v.reduce_max(r_m2[:], r_sc2[:], axis=AX.X),
                      deps=[e4], reads=[res("rsc2")], writes=[res("rm2")],
                      tag=f"rm2{gg}")
            e6 = s.ev("dve", lambda v, sl=sl: v.tensor_single_scalar(
                r_eq2[:], sl, r_m2[:], ALU.is_equal), deps=[e5],
                reads=[res("rm2")], writes=[res("req2")], tag=f"req2{gg}")
            e7 = s.ev("act", lambda a: a.activation(
                r_em[:], r_m2[:], AF.Exp, bias=r_nm1[:]), deps=[e5, e2],
                reads=[res("rm2"), res("rnm1")], writes=[res("rem")],
                tag=f"rem{gg}")
            e8 = s.ev("dve", lambda v: v.tensor_scalar_add(r_p1[:], r_em[:], 1.0),
                      deps=[e7], reads=[res("rem")], writes=[res("rp1")],
                      tag=f"rp1a{gg}")
            e9 = s.ev("dve", lambda v: v.reciprocal(r_p1[:], r_p1[:]),
                      deps=[e8], writes=[res("rp1")], tag=f"rp1b{gg}")
            e10 = s.ev("dve", lambda v: v.tensor_mul(r_p2[:], r_em[:], r_p1[:]),
                       deps=[e9], reads=[res("rem"), res("rp1")],
                       writes=[res("rp2")], tag=f"rp2{gg}")
            e11 = s.ev("dve", lambda v, gg=gg: v.tensor_scalar_mul(
                wtok[:, gg * 8:(gg + 1) * 8], r_eq1[:], r_p1[:]),
                deps=[e10, e3], reads=[res("req1"), res("rp1")],
                writes=[res(("wtok", gg))], tag=f"w1{gg}")
            e12 = s.ev("dve", lambda v, gg=gg: v.scalar_tensor_tensor(
                wtok[:, gg * 8:(gg + 1) * 8], r_eq2[:], r_p2[:],
                wtok[:, gg * 8:(gg + 1) * 8], op0=ALU.mult, op1=ALU.add),
                deps=[e11, e6], reads=[res("req2"), res("rp2")],
                writes=[res(("wtok", gg))], tag=f"w2{gg}")

    if phases <= 6:
        return locals()
    mslab_consumed = {}
    for m in range(2):
        for sl_ in range(3):
            mslab_consumed[(m, sl_)] = [e_x1[(15, 0)], e_x1[(15, 1)]]
    e_wbv_all = {}

    def emit_wbc(e):
        e_wbv = []
        for gg in range(8):
            ws = gg % 2
            e_wt = s.ev("pe", lambda t, gg=gg, e=e: t.transpose(
                pf[3][0:1, 0:128], wtok[:, gg * 8 + e:gg * 8 + e + 1],
                identf[:]), deps=[R[("wtok", gg)].w, e_idf],
                writes=[res(("pf", 3))], tag=f"wt{e}_{gg}")
            e_wv = s.ev("dve", lambda v, ws=ws: v.tensor_copy(
                wrow[ws][:], pf[3][0:1, 0:128]), deps=[e_wt],
                reads=[res(("pf", 3))], writes=[res(("wrow", ws))],
                tag=f"wv{e}_{gg}")
            e_wb = s.ev("pe", lambda t, gg=gg, ws=ws: t.matmul(
                pf[4][:, (gg % 4) * 128:(gg % 4 + 1) * 128], ones1f[:],
                wrow[ws][:], start=True, stop=True),
                deps=[e_wv, e_ones1f], reads=[res(("wrow", ws))],
                writes=([res(("pf", 4))] if gg % 4 == 0 else []),
                tag=f"wb{e}_{gg}")
            if gg % 4 == 3:
                e_bv = s.ev("dve", lambda v, gg=gg: v.tensor_copy(
                    wbc[:, (gg - 3) * 128:(gg + 1) * 128], pf[4][:, 0:512]),
                    deps=[e_wb], reads=[res(("pf", 4))],
                    writes=[res("wbc")], tag=f"wbv{e}_{gg}")
                e_wbv.append(e_bv)
        e_wbv_all[e] = e_wbv

    for e in range(E):
        if e > 0:
            emit_wbc(e)
        e_wbv = e_wbv_all.get(e, [])
        for jf in range(NJF):
            slot = jf % 3
            e_lg = s.ev("pool", lambda g_, e=e, jf=jf, slot=slot: g_.dma_start(
                out=mslab[0][:, slot * NK * 128:(slot + 1) * NK * 128],
                in_=d_wg[e, :, jf * NK * 128:(jf + 1) * NK * 128]),
                deps=mslab_consumed[(0, slot)], sem=f"ms0{slot}", incv=16,
                tag=f"lg{e}_{jf}")
            e_lu = s.ev("pool", lambda g_, e=e, jf=jf, slot=slot: g_.dma_start(
                out=mslab[1][:, slot * NK * 128:(slot + 1) * NK * 128],
                in_=d_wu[e, :, jf * NK * 128:(jf + 1) * NK * 128]),
                deps=mslab_consumed[(1, slot)], sem=f"ms1{slot}", incv=16,
                tag=f"lu{e}_{jf}")
            for th in range(2):
                sgs = (jf * 2 + th) % 2
                lastg = None
                for k in range(NK):
                    lastg = s.ev("pe", lambda t, slot=slot, k=k, th=th, st=(k == 0), sp_=(k == NK - 1): t.matmul(
                        pf[0][:, 0:512],
                        mslab[0][:, (slot * NK + k) * 128:(slot * NK + k + 1) * 128],
                        ln2[:, k * T + th * 512:k * T + (th + 1) * 512],
                        start=st, stop=sp_),
                        deps=[e_lg, e_ln2[k]],
                        writes=([res(("pf", 0))] if k == 0 else []),
                        tag=f"gm{e}_{jf}_{th}_{k}")
                e_sg = s.ev("act", lambda a_, sgs=sgs: a_.activation(
                    sgr[sgs][:], pf[0][:, 0:512], AF.Silu),
                    deps=[lastg], reads=[res(("pf", 0))],
                    writes=[res(("sgr", sgs))], tag=f"sg{e}_{jf}_{th}")
                lastu = None
                for k in range(NK):
                    lastu = s.ev("pe", lambda t, slot=slot, k=k, th=th, st=(k == 0), sp_=(k == NK - 1): t.matmul(
                        pf[1][:, 0:512],
                        mslab[1][:, (slot * NK + k) * 128:(slot * NK + k + 1) * 128],
                        ln2[:, k * T + th * 512:k * T + (th + 1) * 512],
                        start=st, stop=sp_),
                        deps=[e_lu, e_ln2[k]],
                        writes=([res(("pf", 1))] if k == 0 else []),
                        tag=f"um{e}_{jf}_{th}_{k}")
                if e == 0:
                    e_h1 = s.ev("dve", lambda v, jf=jf, th=th, sgs=sgs: v.tensor_mul(
                        hidden[:, jf * T + th * 512:jf * T + (th + 1) * 512],
                        sgr[sgs][:], pf[1][:, 0:512]),
                        deps=[lastu, e_sg], reads=[res(("pf", 1)), res(("sgr", sgs))],
                        writes=[res(("hid", e, jf))], tag=f"h1{e}_{jf}_{th}")
                else:
                    e_h1 = s.ev("dve", lambda v, sgs=sgs: v.tensor_mul(
                        h1r[sgs][:], sgr[sgs][:], pf[1][:, 0:512]),
                        deps=[lastu, e_sg], reads=[res(("pf", 1)), res(("sgr", sgs))],
                        writes=[res(("h1r", sgs))], tag=f"h1{e}_{jf}_{th}")
                    e_h2 = s.ev("dve", lambda v, e=e, jf=jf, th=th, sgs=sgs: v.tensor_mul(
                        hidden[:, (e * NJF + jf) * T + th * 512:(e * NJF + jf) * T + (th + 1) * 512],
                        h1r[sgs][:], wbc[:, th * 512:(th + 1) * 512]),
                        deps=[e_h1] + e_wbv, reads=[res(("h1r", sgs)), res("wbc")],
                        writes=[res(("hid", e, jf))], tag=f"h2{e}_{jf}_{th}")
                mslab_consumed[(0, slot)] = [lastg]
                mslab_consumed[(1, slot)] = [lastu]
        if e == 0:
            emit_top2()
            emit_wbc(0)
            e_wbv = e_wbv_all[0]
            for jf in range(NJF):
                for th in range(2):
                    s.ev("dve", lambda v, jf=jf, th=th: v.tensor_mul(
                        hidden[:, jf * T + th * 512:jf * T + (th + 1) * 512],
                        hidden[:, jf * T + th * 512:jf * T + (th + 1) * 512],
                        wbc[:, th * 512:(th + 1) * 512]),
                        deps=e_wbv, reads=[res("wbc")],
                        writes=[res(("hid", 0, jf))], tag=f"h2p{jf}_{th}")
    s.fence()

    if phases <= 7:
        return locals()
    e_sts = []
    dslab_consumed = [mslab_consumed[(0, 2)][0], mslab_consumed[(1, 2)][0]]
    dslab_first = [mslab_consumed[(0, 0)][0], mslab_consumed[(0, 1)][0],
                   mslab_consumed[(1, 0)][0], mslab_consumed[(1, 1)][0]]
    for j in range(NK):
        buf = j % 2
        e_ldp = []
        for p in range(3):
            e_ldp.append(s.ev("pool", lambda g_, j=j, buf=buf, p=p: g_.dma_start(
                out=mslab[buf][:, p * 2048:(p + 1) * 2048],
                in_=d_wd[:, j * E * NJF * 128 + p * 2048:
                         j * E * NJF * 128 + (p + 1) * 2048]),
                deps=[dslab_consumed[buf]] + (dslab_first if j < 2 else []),
                sem=f"m8{buf}{p}", incv=16, tag=f"dsl{j}_{p}"))
        for th in range(2):
            bank = (j * 2 + th) % 2
            last = None
            first = True
            for e in range(E):
                for kk in range(NJF):
                    last = s.ev("pe", lambda t, e=e, kk=kk, th=th, bank=bank, buf=buf, st=first: t.matmul(
                        pf[bank][:, 0:512],
                        mslab[buf][:, (e * NJF + kk) * 128:(e * NJF + kk + 1) * 128],
                        hidden[:, (e * NJF + kk) * T + th * 512:(e * NJF + kk) * T + (th + 1) * 512],
                        start=st, stop=False),
                        deps=[e_ldp[(e * NJF + kk) // 16]],
                        reads=[res(("hid", e, kk))],
                        writes=([res(("pf", bank))] if first else []),
                        tag=f"dm{j}_{th}_{e}_{kk}")
                    first = False
            last = s.ev("pe", lambda t, j=j, th=th, bank=bank: t.matmul(
                pf[bank][:, 0:512], ones1b[:],
                m2row[0:1, th * 512:(th + 1) * 512], start=False, stop=True),
                deps=[last, R["m2row"].w, e_ms[3]], reads=[res("m2row")],
                tag=f"dmm{j}_{th}")
            dslab_consumed[buf] = last
            ts_ = (j * 2 + th) % 2
            e_t1 = s.ev("dve", lambda v, j=j, th=th, ts_=ts_: v.tensor_mul(
                tout[ts_][:], ln2[:, j * T + th * 512:j * T + (th + 1) * 512],
                sigbc[:, th * 512:(th + 1) * 512]),
                deps=[R[("ln2", j)].w, R[id(sigbc)].w],
                reads=[res(("ln2", j)), res(id(sigbc))],
                writes=[res(("tout", ts_))], tag=f"t1_{j}_{th}")
            e_t2 = s.ev("dve", lambda v, ts_=ts_, bank=bank: v.tensor_add(
                ostg[ts_][:], tout[ts_][:], pf[bank][:, 0:512]),
                deps=[last, e_t1], reads=[res(("pf", bank)), res(("tout", ts_))],
                writes=[res(("ostg", ts_))], tag=f"t2_{j}_{th}")
            e_st = s.ev("sp", lambda sp, j=j, th=th, ts_=ts_: sp.dma_start(
                out=d_o[j * 128:(j + 1) * 128, th * 512:(th + 1) * 512],
                in_=ostg[ts_][:]), deps=[e_t2], reads=[res(("ostg", ts_))],
                sem=f"o{ts_}", incv=16, tag=f"st{j}_{th}")
            e_sts.append(e_st)
    e_end = s.ev("sp", lambda sp: sp.nop(), deps=e_sts, tag="end")

    return locals()


def finish(env):
    nc, s, sems = env["nc"], env["s"], env["sems"]
    s.finalize_plan()
    with nc.Block() as blk:
        @blk.tensor
        def _(t):
            s.emit("pe", t, sems)

        @blk.vector
        def _(v):
            s.emit("dve", v, sems)

        @blk.scalar
        def _(a):
            s.emit("act", a, sems)

        @blk.gpsimd
        def _(g):
            s.emit("pool", g, sems)

        @blk.sync
        def _(sp):
            s.emit("sp", sp, sems)
    nc.finalize()
    return nc


def stage(inputs):
    W_in = inputs["L0_attn_W_in"]
    W_out = inputs["L0_attn_W_out"]
    conv_w = inputs["L0_attn_conv_w"]
    conv_b = inputs["L0_attn_conv_b"]
    z_bias = inputs["L0_attn_z_bias"]
    Dd = inputs["L0_attn_D"]
    Wr = inputs["L0_ffn_w_gate_router"]
    Wg = inputs["L0_ffn_w_gate_proj"]
    Wu = inputs["L0_ffn_w_up_proj"]
    Wd = inputs["L0_ffn_w_down_proj"]
    u = inputs["L0_attn_u"]
    C = np.ascontiguousarray

    wxbc = C(np.asarray(W_in[:CONVD]).reshape(NJX, 128, NK, 128)
             .transpose(3, 0, 2, 1).reshape(128, -1))
    wz = C(np.asarray(W_in[CONVD:AOFF]).reshape(NJZ, 128, NK, 128)
           .transpose(3, 0, 2, 1).reshape(128, -1))
    wa = C(np.asarray(W_in[AOFF:]).reshape(32, NK, 128)
           .transpose(2, 1, 0).reshape(128, -1))
    wout = C(np.asarray(W_out).reshape(NK, 128, NK, 128)
             .transpose(3, 0, 2, 1).reshape(128, -1))
    wr = C(np.asarray(Wr).T.reshape(NK, 128, 8).transpose(1, 0, 2)
           .reshape(128, -1))
    wg = C(np.asarray(Wg).reshape(E, NJF, 128, NK, 128)
           .transpose(4, 0, 1, 3, 2).reshape(128, E, -1).transpose(1, 0, 2))
    wu = C(np.asarray(Wu).reshape(E, NJF, 128, NK, 128)
           .transpose(4, 0, 1, 3, 2).reshape(128, E, -1).transpose(1, 0, 2))
    wd = C(np.asarray(Wd).reshape(E, NK, 128, NJF, 128)
           .transpose(4, 1, 0, 3, 2).reshape(128, -1))
    cw = C(np.asarray(conv_w)[:, 0, :].reshape(NJX, 128, 4)
           .transpose(1, 0, 2).reshape(128, -1))
    cb = C(np.asarray(conv_b).reshape(NJX, 128).T)
    zb = C(np.asarray(z_bias).reshape(NJZ, 128).T)
    drep = C(np.repeat(np.asarray(Dd), HD).reshape(NK, 128).T)
    shared = dict(wxbc=wxbc, wz=wz, wa=wa, wout=wout, wr=wr, wg=wg, wu=wu,
                  wd=wd, cw=cw, cb=cb, zb=zb, drep=drep)
    un = np.asarray(u)
    per_core = []
    for core in range(8):
        b, h = core // 2, core % 2
        ue = np.zeros((D, TE), np.float32)
        if h == 0:
            ue[:, 3:] = un[b, 0:T, :].T
        else:
            ue[:, :] = un[b, T - 3:L, :].T
        m = dict(shared)
        m["u"] = C(ue)
        m["smask"] = np.full((128, 1), float(h), np.float32)
        per_core.append(m)
    return per_core


def kernel(**inputs):
    per_core = stage(inputs)
    env = build_nc(debug=0)
    nc = finish(env)
    res = run_bass_kernel_spmd(nc, per_core, core_ids=list(range(8)))
    u = np.asarray(inputs["L0_attn_u"])
    out = np.empty((B, L, D), np.float32)
    for core in range(8):
        b, h = core // 2, core % 2
        out[b, h * T:(h + 1) * T, :] = np.asarray(res.results[core]["o"]).T
    return out




# version 19 (final)
# speedup vs baseline: 1.0795x; 1.0031x over previous
import numpy as np
import ml_dtypes
import concourse.bass as bass
import concourse.mybir as mybir
from concourse.bass_utils import run_bass_kernel_spmd
from contextlib import ExitStack

F32 = mybir.dt.float32
BF16 = mybir.dt.bfloat16
AF = mybir.ActivationFunctionType
ALU = mybir.AluOpType
AX = mybir.AxisListType

B, L, D = 4, 2048, 2048
T = 1024
TE = T + 3
NH, HD, DS = 32, 64, 64
NK = 16
E, FF = 8, 768
NJZ, NJX, NJF = 16, 48, 6
CONVD, AOFF = 6144, 8192
HD = 64
Q = 64
NG = 8
NC = 16
EPS = 1e-5
INV_D = 1.0 / D


class Ev:
    __slots__ = ("eng", "fn", "deps", "sem", "incv", "needed", "val", "tag")

    def __init__(self, eng, fn, deps, sem, incv, tag):
        self.eng, self.fn, self.deps = eng, fn, deps
        self.sem, self.incv = sem, incv
        self.needed = False
        self.val = None
        self.tag = tag


class Res:
    __slots__ = ("w", "r")

    def __init__(self):
        self.w = None
        self.r = []


def succeed(new, *old):
    rs = []
    for o in old:
        rs.extend(o.r)
        if o.w is not None:
            rs.append(o.w)
    new.r = rs
    new.w = None


class Sched:
    def __init__(self):
        self.events = []
        self.last_per_eng = {}
        self.pending_fence = {}

    def ev(self, eng, fn, deps=(), reads=(), writes=(), sem=None, incv=1,
           tag=""):
        dl = [d for d in deps if d is not None]
        if eng in self.pending_fence:
            dl.extend(self.pending_fence.pop(eng))
        for r in reads:
            if r.w is not None:
                dl.append(r.w)
        e = Ev(eng, fn, dl, sem if sem is not None else eng, incv, tag)
        for r in reads:
            r.r.append(e)
        for w in writes:
            e.deps.extend(w.r)
            if w.w is not None:
                e.deps.append(w.w)
            w.w = e
            w.r = []
        self.events.append(e)
        self.last_per_eng[eng] = e
        return e

    def fence(self):
        lasts = [e for e in self.last_per_eng.values()]
        for eng in ("pe", "dve", "act", "pool", "sp"):
            self.pending_fence[eng] = list(lasts)

    def finalize_plan(self):
        for e in self.events:
            e.deps = [d for d in e.deps if d is not e]
            for d in e.deps:
                d.needed = True
            if e.sem != e.eng:
                e.needed = True
        counts = {}
        for e in self.events:
            if e.needed:
                counts[e.sem] = counts.get(e.sem, 0) + e.incv
                e.val = counts[e.sem]

    def emit(self, eng_name, eng, sems):
        hw = {}
        for e in self.events:
            if e.eng != eng_name:
                continue
            for d in e.deps:
                assert d.needed, (e.tag, d.tag)
                if hw.get(d.sem, -1) < d.val:
                    eng.wait_ge(sems[d.sem], d.val)
                    hw[d.sem] = d.val
            inst = e.fn(eng)
            if e.needed:
                inst.then_inc(sems[e.sem], e.incv)
                if eng_name == "pool" and e.sem == "pool":
                    eng.wait_ge(sems["pool"], e.val)
                    hw["pool"] = e.val


class Arena:
    def __init__(self, lo, hi):
        self.lo, self.hi, self.cur = lo, hi, lo

    def take(self, nbytes):
        off = (self.cur + 31) & ~31
        assert off + nbytes <= self.hi, (off, nbytes, self.hi)
        self.cur = off + nbytes
        return off

    def reset(self):
        self.cur = self.lo


def build_nc(debug=0, phases=99, p3sub=9):
    p3sub = 9 if p3sub == 9 else p3sub
    nc = bass.Bass("TRN2", target_bir_lowering=False, debug=False)
    s = Sched()
    KB = 1024
    dbg_outs = {}

    dram = lambda n, shp, dt=F32, kind="ExternalInput": nc.dram_tensor(
        n, shp, dt, kind=kind).ap()
    d_u = dram("u", [D, TE])
    d_wxbc = dram("wxbc", [128, NJX * NK * 128])
    d_wz = dram("wz", [128, NJZ * NK * 128])
    d_wa = dram("wa", [128, NK * 32])
    d_wout = dram("wout", [128, NK * NK * 128])
    d_wr = dram("wr", [128, NK * 8])
    d_wg = dram("wg", [E, 128, NJF * NK * 128])
    d_wu = dram("wu", [E, 128, NJF * NK * 128])
    d_wd = dram("wd", [128, NK * E * NJF * 128])
    d_cw = dram("cw", [128, NJX * 4])
    d_cb = dram("cb", [128, NJX])
    d_zb = dram("zb", [128, NJZ])
    d_drep = dram("drep", [128, NK])
    d_smask = dram("smask", [128, 1])
    d_o = dram("o", [D, T], kind="ExternalOutput")
    d_ccin = dram("cc_in", [128, NC * 64], BF16, kind="Internal")
    d_ccout = dram("cc_out", [256, NC * 64], BF16, kind="Internal")

    KBo = 1024

    SBUF_BASE = 16512
    def sb(name, shape, dt, off):
        nb = int(np.prod(shape[1:])) * (2 if dt == BF16 else 4)
        assert SBUF_BASE + off + nb <= 229344, (name, off, nb)
        return nc.alloc_sbuf_tensor_at(name, shape, dt, offset=SBUF_BASE + off)

    identb = sb("identb", [128, 128], BF16, 0)
    identf = sb("identf", [128, 128], F32, 256)
    ones1f = sb("ones1f", [1, 128], F32, 768)
    ones1b = sb("ones1b", [1, 128], BF16, 1280)
    onescol = sb("onescol", [128, 1], BF16, 1536)
    u01 = sb("u01", [128, 128], F32, 1568)
    c_eps = sb("c_eps", [1, 1], F32, 2080)
    c_nhalf = sb("c_nhalf", [1, 1], F32, 2112)
    c_phalf = sb("c_phalf", [1, 1], F32, 4064)
    onescolf = sb("onescolf", [128, 1], F32, 2144)
    cw_sb = sb("cw_sb", [128, NJX * 4], F32, 2176)
    cb_sb = sb("cb_sb", [128, NJX], F32, 2944)
    zb_sb = sb("zb_sb", [128, NJZ], F32, 3136)
    drep_sb = sb("drep_sb", [128, NK], F32, 3200)
    smask_sb = sb("smask_sb", [128, 1], F32, 3264)
    wr_sb = sb("wr_sb", [128, NK * 8], BF16, 3296)
    SL = 4480
    wslab = [sb(f"wslab{i}", [128, 2 * NK * 128], BF16, SL + i * 8 * KBo)
             for i in range(2)]
    wos = [sb(f"wos{i}", [128, NK * 128], BF16, SL + i * 4 * KBo)
           for i in range(2)]
    din = sb("din", [128, 256], F32, SL + 8 * KBo)
    rcol = sb("rcol", [128, 256], F32, SL + 9 * KBo)
    ecol = sb("ecol", [128, 256], F32, SL + 10 * KBo)
    dbnd = sb("dbnd", [128, 256], F32, SL + 11 * KBo)
    s_inb = sb("s_inb", [128, NC * 64], BF16, SL + 12 * KBo)
    scores = sb("scores", [8, T], F32, SL + 0 * KBo)
    sigbc = sb("sigbc", [128, T], BF16, SL + 4 * KBo)
    m2row = sb("m2row", [1, T], BF16, SL + 6 * KBo)
    wbc = sb("wbc", [128, T], BF16, SL + 8 * KBo)
    sttok = sb("sttok", [128, 64], F32, SL + 10 * KBo)
    wtok = sb("wtok", [128, 64], F32, SL + 10 * KBo + 256)
    wrow = [sb(f"wrow{i}", [1, 128], F32, SL + 10 * KBo + 512 + i * 512)
            for i in range(2)]
    r_m1 = sb("r_m1", [128, 1], F32, SL + 11 * KBo + 512)
    r_nm1 = sb("r_nm1", [128, 1], F32, SL + 11 * KBo + 544)
    r_m2 = sb("r_m2", [128, 1], F32, SL + 11 * KBo + 576)
    r_em = sb("r_em", [128, 1], F32, SL + 11 * KBo + 608)
    r_p1 = sb("r_p1", [128, 1], F32, SL + 11 * KBo + 640)
    r_p2 = sb("r_p2", [128, 1], F32, SL + 11 * KBo + 672)
    r_eq1 = sb("r_eq1", [128, 8], F32, SL + 11 * KBo + 704)
    r_eq2 = sb("r_eq2", [128, 8], F32, SL + 11 * KBo + 736)
    r_sc2 = sb("r_sc2", [128, 8], F32, SL + 11 * KBo + 768)
    m2f = sb("m2f_row", [1, T], F32, SL + 12 * KBo)
    rs_col = sb("rs_col", [8, 1], F32, SL + 11 * KBo + 800)
    nrs_row = sb("nrs_row", [1, 8], F32, SL + 11 * KBo + 832)
    wrf_sb = sb("wrf_sb", [128, NK * 8], F32, 3552)
    S1 = 20864
    ln1 = sb("ln1", [128, NK * TE], BF16, S1)
    y_tok = sb("y_tok", [128, NG * D], BF16, S1)
    ln2 = sb("ln2", [128, NK * T], BF16, S1)
    S2 = 53760
    z_sb = sb("z_sb", [128, NK * T], BF16, S2)
    mslab = [sb(f"mslab{i}", [128, 3 * NK * 128], BF16, S2 + i * 12 * KBo)
             for i in range(2)]
    S3 = 86528
    u_bf = sb("u_bf", [128, NK * TE], BF16, S3)
    x_cm = sb("x_cm", [128, NK * T], BF16, S3)
    b_cm = sb("b_cm", [128, NK * T], BF16, S3 + 32 * KBo)
    c_cm = sb("c_cm", [128, NK * T], BF16, S3 + 64 * KBo)
    x1 = sb("x1", [128, NK * T], F32, S3)
    x1bf = sb("x1bf", [128, NK * T], BF16, S3 + 64 * KBo)
    hidden = sb("hidden", [128, E * NJF * T], BF16, S3)
    S4 = 184832
    uring = [sb(f"uring{i}", [128, TE], F32, S4 + i * 4128) for i in range(2)]
    xpre = [sb(f"xpre{i}", [128, TE], BF16, S4 + i * 2080) for i in range(2)]
    ctmp = [sb(f"ctmp{i}", [128, T], F32, S4 + 4160 + i * 4096)
            for i in range(2)]
    x_tok = sb("x_tok", [128, 2048], BF16, S4)
    b_tok = sb("b_tok", [128, 2048], BF16, S4 + 4 * KBo)
    xd = sb("xd", [128, 2048], BF16, S4 + 8 * KBo)
    x0r = [sb(f"x0r{i}", [128, T], F32, S4 + i * 4096) for i in range(2)]
    sgr = [sb(f"sgr{i}", [128, 512], F32, S4 + i * 2048) for i in range(2)]
    h1r = [sb(f"h1r{i}", [128, 512], F32, S4 + 4096 + i * 2048)
           for i in range(2)]
    tout = [sb(f"tout{i}", [128, 512], F32, S4 + i * 2048) for i in range(2)]
    ostg = [sb(f"ostg{i}", [128, 512], F32, S4 + 4096 + i * 2048)
            for i in range(2)]
    S5 = 197184
    sqr = [sb(f"sqr{i}", [128, 512], BF16, S5 + i * KBo) for i in range(2)]
    mrow = sb("mrow", [1, 512], F32, S5 + 2 * KBo)
    srow = sb("srow", [1, 512], F32, S5 + 4 * KBo)
    scrow = sb("scrow", [1, 512], F32, S5 + 6 * KBo)
    mbc = sb("mbc", [128, TE], BF16, S5 + 8 * KBo)
    sbc = sb("sbc", [128, TE], BF16, S5 + 8 * KBo + 2080)
    wa_sb = sb("wa_sb", [128, NK * 32], BF16, S5 + 12 * KBo)
    a_sb = sb("a_sb", [32, T], F32, S5)
    zeros32 = sb("zeros32", [32, T], F32, S5 + 4 * KBo)
    g_sb = sb("g_sb", [32, T], F32, S5 + 8 * KBo)
    gt_sb = sb("gt_sb", [128, 256], F32, S5 + 12 * KBo)
    gmrow = sb("gmrow", [1, 256], F32, S5 + 4 * KBo)
    gerow = sb("gerow", [1, 256], F32, S5 + 5 * KBo)
    gmnrow = sb("gmnrow", [1, 256], F32, S5 + 6 * KBo)
    arow = sb("arow", [1, 256], F32, S5 + 7 * KBo)
    expabc = sb("expabc", [128, 256], F32, S5)
    gbc = [sb(f"gbc{i}", [128, 32], F32, S5 + 1 * KBo + i * 128)
           for i in range(2)]
    mtr = [sb(f"mtr{i}", [128, 128], BF16, S5 + 8 * KBo + i * 256)
           for i in range(16)]
    s_bf = sb("s_bf", [128, NC * 64], BF16, S5 + 1 * KBo + 512)
    ybr = [sb(f"ybr{i}", [128, 64], F32, S5 + 3 * KBo + 512 + i * 256)
           for i in range(4)]
    yfr = [sb(f"yfr{i}", [128, 128], F32, S5 + 4 * KBo + 512 + i * 512)
           for i in range(2)]
    mbc2 = sb("mbc2", [128, T], BF16, S5)
    sbc2 = sb("sbc2", [128, T], BF16, S5 + 2 * KBo)
    sq2 = [sb(f"sq2_{i}", [128, 512], BF16, S5 + 4 * KBo + i * KBo)
           for i in range(2)]
    m512 = sb("m512", [1, 512], F32, S5 + 6 * KBo)
    s2row = sb("s2row", [1, 512], F32, S5 + 8 * KBo)
    scr2 = sb("scr2", [1, 512], F32, S5 + 10 * KBo)

    ctx = ExitStack()
    pf = [ctx.enter_context(nc.psum_tensor(f"pf{i}", [128, 512], F32))
          for i in range(6)]
    pbt = [ctx.enter_context(nc.psum_tensor(f"pbt{i}", [128, 128], BF16))
           for i in range(2)]

    sem_names = ["pe", "dve", "act", "pool", "sp",
                 "c_dma", "u0", "u1", "ws0", "ws1", "ms0", "ms1",
                 "ms00", "ms01", "ms02", "ms10", "ms11", "ms12",
                 "m800", "m801", "m802", "m810", "m811", "m812",
                 "x00", "x01", "o0", "o1", "cc", "ccd", "ge_dma"]
    if debug:
        sem_names += ["dbg_dma", "dbg_gdma"]
    sems = {n: nc.alloc_semaphore(n) for n in sem_names}

    R = {}

    def res(key):
        if key not in R:
            R[key] = Res()
        return R[key]

    cload = []
    prevld = None
    for dst, src in [(wr_sb, d_wr), (wr_sb, d_wr), (cw_sb, d_cw),
                     (cb_sb, d_cb), (zb_sb, d_zb), (drep_sb, d_drep),
                     (smask_sb, d_smask)]:
        prevld = s.ev("pool", lambda g, d=dst, sr=src: g.dma_start(
            out=d[:], in_=sr), deps=[prevld], sem="c_dma", incv=16,
            tag="cload")
        cload.append(prevld)
    e_ms = []
    e_ms.append(s.ev("pool", lambda g: g.memset(identb[:], 1.0), tag="ms"))
    e_ms.append(s.ev("pool", lambda g: g.memset(identf[:], 1.0), tag="ms"))
    e_ms.append(s.ev("pool", lambda g: g.memset(ones1f[:], 1.0), tag="ms"))
    e_ms.append(s.ev("pool", lambda g: g.memset(ones1b[:], 1.0), tag="ms"))
    e_ms.append(s.ev("pool", lambda g: g.memset(onescol[:], 1.0), tag="ms"))
    e_ms.append(s.ev("pool", lambda g: g.memset(u01[:], 1.0), tag="ms"))
    e_ms.append(s.ev("pool", lambda g: g.nop(), tag="ms"))
    e_ms.append(s.ev("pool", lambda g: g.memset(c_eps[:], EPS), tag="ms"))
    e_ms.append(s.ev("pool", lambda g: g.memset(c_nhalf[:], -0.5), tag="ms"))
    e_ms.append(s.ev("pool", lambda g: g.memset(onescolf[:], 1.0), tag="ms"))
    e_cph = s.ev("pool", lambda g: g.memset(c_phalf[:], 0.5), tag="ms")
    e_ms.append(e_cph)
    e_ones1f, e_ones1b, e_onescol = e_ms[2], e_ms[3], e_ms[4]
    e_ceps, e_cnh, e_onescolf = e_ms[7], e_ms[8], e_ms[9]
    e_xx, e_wr, e_cw, e_cb, e_zbld, e_drepld, e_smld = cload
    e_wrf = s.ev("pool", lambda g: g.dma_start(out=wrf_sb[:], in_=d_wr),
                 deps=[e_smld], sem="c_dma", incv=16, tag="wrfld")
    e_idb = s.ev("pool", lambda g: g.affine_select(
        identb[:], identb[:], pattern=[[1, 128]], compare_op=ALU.is_equal,
        fill=0.0, base=0, channel_multiplier=-1), deps=e_ms, tag="idb")
    e_idf = s.ev("pool", lambda g: g.affine_select(
        identf[:], identf[:], pattern=[[1, 128]], compare_op=ALU.is_equal,
        fill=0.0, base=0, channel_multiplier=-1), deps=[e_idb], tag="idf")
    e_u01 = s.ev("pool", lambda g: g.affine_select(
        u01[:], u01[:], pattern=[[1, 128]], compare_op=ALU.is_ge,
        fill=0.0, base=0, channel_multiplier=-1), deps=[e_idf], tag="u01")
    e_const = cload + [e_u01, e_idf, e_idb]

    e_ucast = []
    for k in range(NK):
        sl = k % 2
        eng = "sp" if sl == 0 else "act"
        e_ld = s.ev(eng, lambda q, k=k, sl=sl: q.dma_start(
            out=uring[sl][:], in_=d_u[k * 128:(k + 1) * 128, :]),
            writes=[res(("uring", sl))], sem=f"u{sl}", incv=16, tag=f"uld{k}")
        e_c = s.ev("act", lambda a, k=k, sl=sl: a.activation(
            u_bf[:, k * TE:(k + 1) * TE], uring[sl][:], AF.Copy),
            deps=[e_ld], reads=[res(("uring", sl))],
            writes=[res(("ubf", k))], tag=f"ucast{k}")
        e_ucast.append(e_c)
    CGS = [(0, 512), (512, 1024), (1024, 1027)]
    e_nrm_all = []
    for cg, (c0, c1) in enumerate(CGS):
        n = c1 - c0
        prev_mm = None
        for k in range(NK):
            sl = k % 2
            e_sq = s.ev("act", lambda a, k=k, c0=c0, c1=c1, sl=sl: a.activation(
                sqr[sl][:, 0:c1 - c0], u_bf[:, k * TE + c0:k * TE + c1],
                AF.Square), deps=[e_ucast[k]], writes=[res(("sqr", sl))],
                tag=f"sq{cg}_{k}")
            e_m1 = s.ev("pe", lambda t, k=k, c0=c0, c1=c1, st=(k == 0), sp=(k == NK - 1): t.matmul(
                pf[0][0:1, 0:c1 - c0], onescol[:], u_bf[:, k * TE + c0:k * TE + c1],
                start=st, stop=sp), deps=[e_ucast[k], e_onescol],
                reads=[], writes=([res(("pf", 0))] if k == 0 else []),
                tag=f"mmM{cg}_{k}")
            e_m2 = s.ev("pe", lambda t, k=k, n=n, sl=sl, st=(k == 0), sp=(k == NK - 1): t.matmul(
                pf[1][0:1, 0:n], onescol[:], sqr[sl][:, 0:n],
                start=st, stop=sp), deps=[e_sq],
                reads=[res(("sqr", sl))],
                writes=([res(("pf", 1))] if k == 0 else []),
                tag=f"mmS{cg}_{k}")
            prev_mm = (e_m1, e_m2)
        e_em = s.ev("dve", lambda v, n=n: v.tensor_scalar_mul(
            mrow[:, 0:n], pf[0][0:1, 0:n], INV_D), deps=[prev_mm[0]],
            reads=[res(("pf", 0))], writes=[res("mrow")], tag=f"evm{cg}")
        e_es = s.ev("dve", lambda v, n=n: v.tensor_scalar_mul(
            srow[:, 0:n], pf[1][0:1, 0:n], INV_D), deps=[prev_mm[1]],
            reads=[res(("pf", 1))], writes=[res("srow")], tag=f"evs{cg}")
        e_msq = s.ev("act", lambda a, n=n: a.activation(
            scrow[:, 0:n], mrow[:, 0:n], AF.Square), deps=[e_em],
            writes=[res("scrow")], tag=f"msq{cg}")
        e_var = s.ev("dve", lambda v, n=n: v.tensor_sub(
            srow[:, 0:n], srow[:, 0:n], scrow[:, 0:n]), deps=[e_es, e_msq],
            writes=[res("srow")], tag=f"var{cg}")
        e_ln = s.ev("act", lambda a, n=n: a.activation(
            srow[:, 0:n], srow[:, 0:n], AF.Ln, bias=c_eps[:]),
            deps=[e_var, e_ceps], writes=[res("srow")], tag=f"vln{cg}")
        e_rs = s.ev("act", lambda a, n=n: a.activation(
            srow[:, 0:n], srow[:, 0:n], AF.Exp, scale=c_nhalf[:]),
            deps=[e_ln, e_cnh], writes=[res("srow")], tag=f"rstd{cg}")
        e_bm = s.ev("pe", lambda t, n=n: t.matmul(
            pf[2][:, 0:n], ones1f[:], mrow[:, 0:n], start=True, stop=True),
            deps=[e_em, e_ones1f], writes=[res(("pf", 2))], tag=f"bcm{cg}")
        e_vm = s.ev("dve", lambda v, n=n, c0=c0: v.tensor_copy(
            mbc[:, c0:c0 + n], pf[2][:, 0:n]), deps=[e_bm],
            reads=[res(("pf", 2))], writes=[res("mbc")], tag=f"bmc{cg}")
        e_bs = s.ev("pe", lambda t, n=n: t.matmul(
            pf[3][:, 0:n], ones1f[:], srow[:, 0:n], start=True, stop=True),
            deps=[e_rs], writes=[res(("pf", 3))], tag=f"bcs{cg}")
        e_vs = s.ev("dve", lambda v, n=n, c0=c0: v.tensor_copy(
            sbc[:, c0:c0 + n], pf[3][:, 0:n]), deps=[e_bs],
            reads=[res(("pf", 3))], writes=[res("sbc")], tag=f"bsc{cg}")
        e_nrm_all += [e_vm, e_vs]
    e_ln1 = []
    for k in range(NK):
        sl = k % 2
        e_t = s.ev("dve", lambda v, k=k, sl=sl: v.tensor_sub(
            uring[sl][:], u_bf[:, k * TE:(k + 1) * TE], mbc[:]),
            deps=e_nrm_all + [e_ucast[k]], writes=[res(("uring", sl))],
            reads=[res("mbc")], tag=f"nsub{k}")
        e_n = s.ev("dve", lambda v, k=k, sl=sl: v.tensor_mul(
            ln1[:, k * TE:(k + 1) * TE], uring[sl][:], sbc[:]),
            deps=[e_t], reads=[res(("uring", sl)), res("sbc")],
            writes=[res(("ln1", k))], tag=f"nmul{k}")
        e_ln1.append(e_n)
    if debug >= 2:
        dbg_outs["ln1"] = None

    s.fence()

    e_wa = s.ev("pool", lambda g: g.dma_start(out=wa_sb[:], in_=d_wa),
                deps=[e_smld], sem="c_dma", incv=16, tag="wald")
    e_apsum = []
    for cg in range(2):
        c0, c1 = 3 + cg * 512, 3 + (cg + 1) * 512
        last = None
        for k in range(NK):
            last = s.ev("pe", lambda t, k=k, c0=c0, c1=c1, st=(k == 0), sp=(k == NK - 1), cg=cg: t.matmul(
                pf[4][0:32, 0:512],
                wa_sb[:, k * 32:(k + 1) * 32],
                ln1[:, k * TE + c0:k * TE + c1], start=st, stop=sp),
                deps=[e_ln1[k], e_wa],
                writes=([res(("pf", 4))] if k == 0 else []), tag=f"Amm{cg}_{k}")
        e_ev = s.ev("dve", lambda v, cg=cg: v.tensor_copy(
            a_sb[:, cg * 512:(cg + 1) * 512], pf[4][0:32, 0:512]),
            deps=[last], reads=[res(("pf", 4))], writes=[res("araw")],
            tag=f"Aev{cg}")
        e_apsum.append(e_ev)
    e_abs = s.ev("act", lambda a: a.activation(
        zeros32[:], a_sb[:], AF.Abs), deps=e_apsum,
        reads=[res("araw")], writes=[res("z32")], tag="abs")
    e_exn = s.ev("act", lambda a: a.activation(
        zeros32[:], zeros32[:], AF.Exp, scale=-1.0), deps=[e_abs],
        writes=[res("z32")], tag="exn")
    e_lnp = s.ev("act", lambda a: a.activation(
        zeros32[:], zeros32[:], AF.Ln, bias=1.0), deps=[e_exn],
        writes=[res("z32")], tag="lnp")
    e_rel = s.ev("dve", lambda v: v.tensor_single_scalar(
        g_sb[:], a_sb[:], 0.0, ALU.max), deps=e_apsum, reads=[res("araw")],
        writes=[res("g32")], tag="rel")
    e_a = s.ev("dve", lambda v: v.scalar_tensor_tensor(
        a_sb[:], g_sb[:], -1.0, zeros32[:], op0=ALU.mult, op1=ALU.subtract),
        deps=[e_lnp, e_rel], reads=[res("z32"), res("g32")],
        writes=[res("araw")], tag="mka")
    e_z32 = s.ev("pool", lambda g: g.memset(zeros32[:], 0.0), deps=[e_a],
                 writes=[res("z32")], tag="z32")
    e_scan = s.ev("dve", lambda v: v.tensor_tensor_scan(
        g_sb[:], a_sb[:], zeros32[:], 0.0, ALU.add, ALU.add),
        deps=[e_z32, e_a], reads=[res("araw"), res("z32")],
        writes=[res("g32")], tag="scan")
    e_gt = []
    for gg in range(8):
        e_tr = s.ev("pe", lambda t, gg=gg: t.transpose(
            pf[5][:, 0:32],
            g_sb[:, gg * 128:(gg + 1) * 128], identf[0:32, 0:32]),
            deps=[e_scan, e_idf], writes=[res(("pf", 5))],
            tag=f"gtr{gg}")
        e_gv = s.ev("dve", lambda v, gg=gg: v.tensor_copy(
            gt_sb[:, gg * 32:(gg + 1) * 32], pf[5][:, 0:32]),
            deps=[e_tr], reads=[res(("pf", 5))], writes=[res("gt")],
            tag=f"gte{gg}")
        e_gt.append(e_gv)
    e_gm = s.ev("sp", lambda sp: sp.dma_start(
        out=gmrow[:], in_=gt_sb[63:64, :]),
        deps=e_gt, sem="ge_dma", incv=16, tag="gm")
    e_ge = s.ev("sp", lambda sp: sp.dma_start(
        out=gerow[:], in_=gt_sb[127:128, :]),
        deps=[e_gm], sem="ge_dma", incv=16, tag="ge")
    gmn_r = res("gmnrow")
    e_gn0 = s.ev("dve", lambda v: v.tensor_copy(
        gmnrow[:, 0:224], gmrow[:, 32:256]), deps=[e_gm, e_ge],
        writes=[gmn_r], tag="gn0")
    e_gn1 = s.ev("dve", lambda v: v.tensor_copy(
        gmnrow[:, 224:256], gerow[:, 224:256]), deps=[e_ge],
        writes=[gmn_r], tag="gn1")
    ar = res("arow")
    e_ar0 = s.ev("dve", lambda v: v.tensor_copy(arow[:, 0:32], gmnrow[:, 0:32]),
                 deps=[e_gn0, e_gn1], reads=[gmn_r], writes=[ar], tag="ar0")
    e_ar1 = s.ev("dve", lambda v: v.tensor_sub(
        arow[:, 32:256], gmnrow[:, 32:256], gmnrow[:, 0:224]),
        deps=[e_ar0], reads=[gmn_r], writes=[ar], tag="ar1")
    e_expa = s.ev("act", lambda a: a.activation(arow[:], arow[:], AF.Exp),
                  deps=[e_ar1], writes=[ar], tag="expa")
    e_eab = s.ev("pe", lambda t: t.matmul(
        pf[5][:, 0:256], ones1f[:], arow[:], start=True, stop=True),
        deps=[e_expa], writes=[res(("pf", 5))], tag="eabm")
    e_eabv = s.ev("dve", lambda v: v.tensor_copy(expabc[:], pf[5][:, 0:256]),
                  deps=[e_eab], reads=[res(("pf", 5))], writes=[res("expabc")],
                  tag="eabv")
    e_zdone = []
    nslab_z = NJZ // 2
    slab_consumed = [None, None]
    slab_loaded = [None, None]
    for sidx in range(nslab_z):
        buf = sidx % 2
        e_ld = s.ev("pool", lambda g, sidx=sidx, buf=buf: g.dma_start(
            out=wslab[buf][:], in_=d_wz[:, sidx * 2 * NK * 128:(sidx + 1) * 2 * NK * 128]),
            deps=[slab_consumed[buf]], sem=f"ws{buf}", incv=16,
            tag=f"zsl{sidx}")
        slab_loaded[buf] = e_ld
        for jloc in range(2):
            jz = sidx * 2 + jloc
            for cg in range(2):
                c0 = 3 + cg * 512
                bank = (jz * 2 + cg) % 2
                last = None
                for k in range(NK):
                    last = s.ev("pe", lambda t, jloc=jloc, k=k, c0=c0, bank=bank, buf=buf, st=(k == 0), sp=(k == NK - 1): t.matmul(
                        pf[bank][:, 0:512],
                        wslab[buf][:, (jloc * NK + k) * 128:(jloc * NK + k + 1) * 128],
                        ln1[:, k * TE + c0:k * TE + c0 + 512],
                        start=st, stop=sp),
                        deps=[e_ld, e_ln1[k]],
                        writes=([res(("pf", bank))] if k == 0 else []),
                        tag=f"zmm{jz}_{cg}_{k}")
                slab_consumed[buf] = last
                e_ev = s.ev("act", lambda a, jz=jz, cg=cg, bank=bank: a.activation(
                    z_sb[:, jz * T + cg * 512:jz * T + (cg + 1) * 512],
                    pf[bank][:, 0:512], AF.Silu, bias=zb_sb[:, jz:jz + 1]),
                    deps=[last, e_zbld], reads=[res(("pf", bank))],
                    writes=[res(("z", jz))], tag=f"zev{jz}_{cg}")
                e_zdone.append(e_ev)

    for sidx in range(NJX // 2):
        buf = sidx % 2
        e_ld = s.ev("pool", lambda g, sidx=sidx, buf=buf: g.dma_start(
            out=wslab[buf][:], in_=d_wxbc[:, sidx * 2 * NK * 128:(sidx + 1) * 2 * NK * 128]),
            deps=[slab_consumed[buf]], sem=f"ws{buf}", incv=16,
            tag=f"xsl{sidx}")
        slab_loaded[buf] = e_ld
        for jloc in range(2):
            jc = sidx * 2 + jloc
            pslot = jc % 2
            e_evs = []
            for cg, (c0, c1) in enumerate(CGS):
                bank = cg % 2
                n = c1 - c0
                last = None
                for k in range(NK):
                    last = s.ev("pe", lambda t, jloc=jloc, k=k, c0=c0, n=n, bank=bank, buf=buf, st=(k == 0), sp=(k == NK - 1): t.matmul(
                        pf[bank][:, 0:n],
                        wslab[buf][:, (jloc * NK + k) * 128:(jloc * NK + k + 1) * 128],
                        ln1[:, k * TE + c0:k * TE + c0 + n],
                        start=st, stop=sp),
                        deps=[e_ld, e_ln1[k]],
                        writes=([res(("pf", bank))] if k == 0 else []),
                        tag=f"xmm{jc}_{cg}_{k}")
                slab_consumed[buf] = last
                e_ev = s.ev("act", lambda a, c0=c0, n=n, bank=bank, pslot=pslot: a.activation(
                    xpre[pslot][:, c0:c0 + n], pf[bank][:, 0:n], AF.Copy),
                    deps=[last], reads=[res(("pf", bank))],
                    writes=[res(("xpre", pslot))], tag=f"xev{jc}_{cg}")
                e_evs.append(e_ev)
            tgt = [x_cm, b_cm, c_cm][jc // 16]
            jj = jc % 16
            e_c1 = s.ev("dve", lambda g, jc=jc, pslot=pslot: g.tensor_scalar(
                ctmp[pslot][:], xpre[pslot][:, 0:T], cw_sb[:, jc * 4:jc * 4 + 1],
                cb_sb[:, jc:jc + 1], ALU.mult, ALU.add),
                deps=e_evs + [e_cw, e_cb], reads=[res(("xpre", pslot))],
                writes=[res(("ctmp", pslot))], tag=f"cv0_{jc}")
            e_c2 = s.ev("dve", lambda g, jc=jc, pslot=pslot: g.scalar_tensor_tensor(
                ctmp[pslot][:], xpre[pslot][:, 1:T + 1],
                cw_sb[:, jc * 4 + 1:jc * 4 + 2], ctmp[pslot][:],
                op0=ALU.mult, op1=ALU.add), deps=[e_c1],
                reads=[res(("xpre", pslot))], writes=[res(("ctmp", pslot))],
                tag=f"cv1_{jc}")
            e_c3 = s.ev("dve", lambda g, jc=jc, pslot=pslot: g.scalar_tensor_tensor(
                ctmp[pslot][:], xpre[pslot][:, 2:T + 2],
                cw_sb[:, jc * 4 + 2:jc * 4 + 3], ctmp[pslot][:],
                op0=ALU.mult, op1=ALU.add), deps=[e_c2],
                reads=[res(("xpre", pslot))], writes=[res(("ctmp", pslot))],
                tag=f"cv2_{jc}")
            e_c4 = s.ev("dve", lambda g, jc=jc, pslot=pslot, tgt=tgt, jj=jj: g.scalar_tensor_tensor(
                tgt[:, jj * T:(jj + 1) * T], xpre[pslot][:, 3:T + 3],
                cw_sb[:, jc * 4 + 3:jc * 4 + 4], ctmp[pslot][:],
                op0=ALU.mult, op1=ALU.add), deps=[e_c3],
                reads=[res(("xpre", pslot)), res(("ctmp", pslot))],
                writes=[res(("cm", jc))], tag=f"cv3_{jc}")

    e_dbn = s.ev("act", lambda a: a.activation(dbnd[:], gt_sb[:], AF.Exp),
                 deps=e_gt + [slab_consumed[0], slab_consumed[1]],
                 reads=[res("gt")], writes=[res("dbnd")], tag="dbnd")
    e_decay = [e_dbn]
    ecps = [res(("ecp", c)) for c in range(8)]
    e_b = s.ev("pe", lambda t: t.matmul(
        pf[5][:, 0:256], ones1f[:], gmrow[:, 0:256],
        start=True, stop=True), deps=[e_gm, e_ones1f],
        writes=[res(("pf", 5))], tag="gb")
    e_dp = s.ev("dve", lambda v: v.tensor_sub(
        ecol[:, 0:256], gt_sb[:, 0:256], pf[5][:, 0:256]),
        deps=[e_b], reads=[res(("pf", 5)), res("gt")],
        writes=ecps, tag="dp")
    e_di = s.ev("act", lambda a: a.activation(
        din[:, 0:256], ecol[:, 0:256], AF.Exp),
        deps=[e_dp], reads=ecps, writes=[res("din")], tag="di")
    e_rc = s.ev("act", lambda a: a.activation(
        rcol[:, 0:256], ecol[:, 0:256], AF.Exp, scale=-1.0),
        deps=[e_dp], reads=ecps, writes=[res("rcol")], tag="rc")
    e_b2 = s.ev("pe", lambda t: t.matmul(
        pf[4][:, 0:256], ones1f[:], gmnrow[:, 0:256],
        start=True, stop=True), deps=[e_gn0, e_gn1, e_b],
        reads=[gmn_r], writes=[res(("pf", 4))], tag="ge")
    e_ep = s.ev("dve", lambda v: v.tensor_sub(
        ecol[:, 0:256], pf[4][:, 0:256], gt_sb[:, 0:256]),
        deps=[e_b2, e_di, e_rc], reads=[res(("pf", 4)), res("gt")],
        writes=ecps, tag="ep")
    e_ec = s.ev("act", lambda a: a.activation(
        ecol[:, 0:256], ecol[:, 0:256], AF.Exp),
        deps=[e_ep], writes=ecps, tag="ec")
    for c in range(8):
        res(("ecp", c)).w = e_ec
    e_decay += [e_di, e_rc, e_ec]

    e_szb = s.ev("pool", lambda g_: g_.memset(s_bf[:], 0.0),
                 deps=[e_decay[-1], e_eabv, e_scan] + e_apsum, tag="szb")
    for h in range(NH):
        res(("sbf", h)).w = e_szb

    wos_consumed = [slab_consumed[0], slab_consumed[0]]
    e_wosld = {}
    e_x0ld = {}
    for jf in range(2):
        buf = jf % 2
        e_wosld[jf] = s.ev("pool", lambda g_, jf=jf, buf=buf: g_.dma_start(
            out=wos[buf][:],
            in_=d_wout[:, jf * NK * 128:(jf + 1) * NK * 128]),
            deps=[wos_consumed[buf]], sem=f"ws{buf}", incv=16,
            tag=f"osl{jf}")

    s.fence()
    if phases <= 2:
        return locals()

    e_supd_last = {}
    bank_rd = {i: [] for i in range(8)}
    bank_wr = {i: None for i in range(8)}

    def epoch_dep(bank):
        d = bank_rd[bank] + ([bank_wr[bank]] if bank_wr[bank] else [])
        bank_rd[bank] = []
        return d

    pending_su = []

    def emit_pending_su():
        for fn in pending_su:
            fn()
        del pending_su[:]

    for c in range(NG):
        e_xtok_cp = {}
        for hp in range(16):
            pb = hp % 2
            e_t = s.ev("pe", lambda t, hp=hp, c=c, pb=pb: t.transpose(
                pbt[pb][:], x_cm[:, hp * T + c * 128:hp * T + c * 128 + 128],
                identb[:]), deps=[e_idb] + epoch_dep(6 + pb),
                reads=[res(("cm", hp))], tag=f"trx{c}_{hp}")
            e_cp = s.ev("act", lambda a, hp=hp, pb=pb: a.activation(
                x_tok[:, hp * 128:(hp + 1) * 128], pbt[pb][:], AF.Copy),
                deps=[e_t], writes=[res(("xtok", c, hp))], tag=f"trxe{c}_{hp}")
            bank_rd[6 + pb].append(e_cp)
            e_xtok_cp[hp] = e_cp
            e_tb = s.ev("pe", lambda t, hp=hp, c=c, pb=pb: t.transpose(
                pbt[pb][:], b_cm[:, hp * T + c * 128:hp * T + c * 128 + 128],
                identb[:]), deps=[e_idb] + epoch_dep(6 + pb),
                reads=[res(("cm", 16 + hp))], tag=f"trb{c}_{hp}")
            e_cb2 = s.ev("dve", lambda v, hp=hp, pb=pb: v.tensor_copy(
                b_tok[:, hp * 128:(hp + 1) * 128], pbt[pb][:]),
                deps=[e_tb], writes=[res(("btok", c, hp))], tag=f"trbe{c}_{hp}")
            bank_rd[6 + pb].append(e_cb2)
        for h in range(NH):
            hp, hh = h // 2, h % 2
            e_xd = s.ev("act", lambda a, h=h, hp=hp, hh=hh, c=c: a.activation(
                xd[:, hp * 128 + hh * 64:hp * 128 + (hh + 1) * 64],
                x_tok[:, hp * 128 + hh * 64:hp * 128 + (hh + 1) * 64],
                AF.Copy, scale=ecol[:, c * 32 + h:c * 32 + h + 1]),
                deps=[e_xtok_cp[hp], R[("ecp", c)].w],
                reads=[res(("xtok", c, hp))],
                writes=[res(("xd", c, h))], tag=f"xd{c}_{h}")
        emit_pending_su()
        for half in range(2):
            hs = list(range(half * 16, half * 16 + 16))
            e_m1 = {}
            e_mte = {}
            e_y2 = {}
            e_s1 = {}
            LAG = 4
            mtbank = [pf[0], pf[1]]
            for i in range(16 + LAG):
                if i < 16:
                    h = hs[i]
                    hh = h % 2
                    bk = h % 2
                    mts = h % 16
                    ccol = (h // 2) * T + c * 128
                    e_m1[h] = s.ev("pe", lambda t, h=h, hh=hh, bk=bk, ccol=ccol: t.matmul(
                        mtbank[bk][:, 0:128],
                        b_cm[hh * 64:(hh + 1) * 64, ccol:ccol + 128],
                        c_cm[hh * 64:(hh + 1) * 64, ccol:ccol + 128],
                        start=True, stop=True),
                        deps=epoch_dep(bk) +
                        [R[("cm", 16 + h // 2)].w, R[("cm", 32 + h // 2)].w],
                        tag=f"m1_{c}_{h}")
                    e_mte[h] = s.ev("dve", lambda v, h=h, c=c, bk=bk, mts=mts: v.scalar_tensor_tensor(
                        mtr[mts][:], mtbank[bk][:, 0:128],
                        rcol[:, c * 32 + h:c * 32 + h + 1], u01[:],
                        op0=ALU.mult, op1=ALU.mult),
                        deps=[e_m1[h], e_u01, R["rcol"].w],
                        writes=[res(("mtr", mts))], tag=f"mte{c}_{h}")
                    bank_rd[bk].append(e_mte[h])
                if i >= LAG:
                    h = hs[i - LAG]
                    hp, hh = h // 2, h % 2
                    yb = 2 + h % 2
                    sbk = 4 + h % 2
                    qc = ((h // 2) % 8) * 64
                    mts = h % 16
                    ccol = (h // 2) * T + c * 128
                    ddy = epoch_dep(yb) if (h - half * 16) < 2 else []
                    dds = epoch_dep(sbk) if (h - half * 16) < 2 else []
                    e_ya = s.ev("pe", lambda t, h=h, hp=hp, hh=hh, yb=yb, qc=qc, mts=mts: t.matmul(
                        pf[yb][:, qc:qc + 64], mtr[mts][:],
                        x_tok[:, hp * 128 + hh * 64:hp * 128 + (hh + 1) * 64],
                        start=True, stop=False),
                        deps=ddy + [e_mte[h], e_xtok_cp[hp]],
                        reads=[res(("mtr", mts))], tag=f"y1_{c}_{h}")
                    e_y2[h] = s.ev("pe", lambda t, h=h, hh=hh, yb=yb, qc=qc, ccol=ccol: t.matmul(
                        pf[yb][:, qc:qc + 64],
                        c_cm[hh * 64:(hh + 1) * 64, ccol:ccol + 128],
                        s_bf[hh * 64:(hh + 1) * 64, (h // 2) * 64:(h // 2 + 1) * 64],
                        start=False, stop=True),
                        deps=[e_ya], reads=[res(("sbf", h))],
                        tag=f"y2_{c}_{h}")
                    e_s1[h] = s.ev("pe", lambda t, h=h, hp=hp, hh=hh, sbk=sbk, qc=qc: t.matmul(
                        pf[sbk][hh * 64:(hh + 1) * 64, qc:qc + 64],
                        b_tok[:, hp * 128 + hh * 64:hp * 128 + (hh + 1) * 64],
                        xd[:, hp * 128 + hh * 64:hp * 128 + (hh + 1) * 64],
                        start=True, stop=True),
                        deps=dds + [R[("btok", c, hp)].w, R[("xd", c, h)].w],
                        tag=f"s1_{c}_{h}")
            last_y = {2: e_y2[half * 16 + 14], 3: e_y2[half * 16 + 15]}
            last_s = {4: e_s1[half * 16 + 14], 5: e_s1[half * 16 + 15]}
            for h in hs:
                yb = 2 + h % 2
                sbk = 4 + h % 2
                hh = h % 2
                qc = ((h // 2) % 8) * 64
                e_ye = s.ev("act", lambda a, h=h, c=c, yb=yb, qc=qc: a.activation(
                    y_tok[:, c * D + h * 64:c * D + (h + 1) * 64],
                    pf[yb][:, qc:qc + 64], AF.Copy,
                    scale=din[:, c * 32 + h:c * 32 + h + 1]),
                    deps=[last_y[yb], R["din"].w],
                    writes=[res(("ytk", c, h))], tag=f"ye{c}_{h}")
                bank_rd[yb].append(e_ye)
                def mk_su(h=h, hh=hh, c=c, sbk=sbk, qc=qc,
                          ls=last_s[sbk], ey2=e_y2[h]):
                    e_su = s.ev("dve", lambda v: v.scalar_tensor_tensor(
                        s_bf[hh * 64:(hh + 1) * 64, (h // 2) * 64:(h // 2 + 1) * 64],
                        s_bf[hh * 64:(hh + 1) * 64, (h // 2) * 64:(h // 2 + 1) * 64],
                        expabc[hh * 64:(hh + 1) * 64, c * 32 + h:c * 32 + h + 1],
                        pf[sbk][hh * 64:(hh + 1) * 64, qc:qc + 64],
                        op0=ALU.mult, op1=ALU.add),
                        deps=[ls, e_eabv, ey2],
                        writes=[res(("sbf", h))], tag=f"su{c}_{h}")
                    bank_rd[sbk].append(e_su)
                    e_supd_last[h] = e_su
                if half == 0:
                    mk_su()
                else:
                    pending_su.append(mk_su)
    emit_pending_su()
    if phases <= 3:
        return locals()
    e_ccw = s.ev("sp", lambda sp: sp.dma_start(out=d_ccin, in_=s_bf[:]),
                 deps=[e_supd_last[h] for h in range(NH)],
                 sem="ge_dma", incv=16, tag="ccw")
    e_cc = s.ev("pool", lambda g_: g_.collective_compute(
        "AllGather", ALU.bypass, [[0, 1], [2, 3], [4, 5], [6, 7]],
        [d_ccin.opt()], [d_ccout.opt()]), deps=[e_ccw], sem="cc", incv=1,
        tag="cc")
    e_sld = s.ev("pool", lambda g_: g_.dma_start(
        out=s_inb[:], in_=d_ccout[0:128, :]),
        deps=[e_cc, slab_consumed[1]], sem="ccd", incv=16, tag="sld")
    e_smul = s.ev("act", lambda a: a.activation(
        s_inb[:], s_inb[:], AF.Copy, scale=smask_sb[:]),
        deps=[e_sld, e_smld], writes=[res("sinb")], tag="smul")
    def emit_p4b(c):
        for hp in range(16):
            pb = hp % 2
            yfs = hp % 2
            e_t = s.ev("pe", lambda t, c=c, hp=hp, pb=pb: t.transpose(
                pbt[pb][:], y_tok[:, c * D + hp * 128:c * D + (hp + 1) * 128],
                identb[:]),
                deps=[R[("ytk", c, 2 * hp)].w, R[("ytk", c, 2 * hp + 1)].w] +
                     epoch_dep(6 + pb),
                tag=f"ytr{c}_{hp}")
            e_o1 = s.ev("dve", lambda v, c=c, hp=hp, pb=pb, yfs=yfs: v.scalar_tensor_tensor(
                yfr[yfs][:], x_cm[:, hp * T + c * 128:hp * T + c * 128 + 128],
                drep_sb[:, hp:hp + 1], pbt[pb][:],
                op0=ALU.mult, op1=ALU.add),
                deps=[e_t, e_drepld], reads=[res(("cm", hp))],
                writes=[res(("yfr", yfs))], tag=f"yo1{c}_{hp}")
            bank_rd[6 + pb].append(e_o1)
            e_o2 = s.ev("dve", lambda v, c=c, hp=hp, yfs=yfs: v.tensor_mul(
                z_sb[:, hp * T + c * 128:hp * T + c * 128 + 128],
                yfr[yfs][:],
                z_sb[:, hp * T + c * 128:hp * T + c * 128 + 128]),
                deps=[e_o1], reads=[res(("yfr", yfs))],
                writes=[res(("z", hp))], tag=f"yo2{c}_{hp}")
    for c in range(NG):
        e_b = {}
        for h in range(NH):
            hh = h % 2
            half = h // 16
            yb = 2 * half + (h % 2)
            qc = ((h % 16) // 2) * 64
            ccol = (h // 2) * T + c * 128
            dd = epoch_dep(yb) if (h % 16) < 2 else []
            e_b[h] = s.ev("pe", lambda t, h=h, hh=hh, yb=yb, qc=qc, ccol=ccol: t.matmul(
                pf[yb][:, qc:qc + 64],
                c_cm[hh * 64:(hh + 1) * 64, ccol:ccol + 128],
                s_inb[hh * 64:(hh + 1) * 64, (h // 2) * 64:(h // 2 + 1) * 64],
                start=True, stop=True),
                deps=dd + [e_smul, R[("cm", 32 + h // 2)].w],
                tag=f"bnd{c}_{h}")
        last_b = {0: e_b[14], 1: e_b[15], 2: e_b[30], 3: e_b[31]}
        for h in range(NH):
            half = h // 16
            yb = 2 * half + (h % 2)
            qc = ((h % 16) // 2) * 64
            if h % 2 == 0:
                e_ba = s.ev("dve", lambda v, h=h, c=c, yb=yb, qc=qc: v.scalar_tensor_tensor(
                    y_tok[:, c * D + h * 64:c * D + (h + 1) * 64],
                    pf[yb][:, qc:qc + 64],
                    dbnd[:, c * 32 + h:c * 32 + h + 1],
                    y_tok[:, c * D + h * 64:c * D + (h + 1) * 64],
                    op0=ALU.mult, op1=ALU.add),
                    deps=[last_b[yb], R["dbnd"].w],
                    writes=[res(("ytk", c, h))], tag=f"ba{c}_{h}")
                bank_rd[yb].append(e_ba)
            else:
                ybs = (h // 2) % 4
                e_be = s.ev("act", lambda a, h=h, c=c, yb=yb, qc=qc, ybs=ybs: a.activation(
                    ybr[ybs][:], pf[yb][:, qc:qc + 64], AF.Copy,
                    scale=dbnd[:, c * 32 + h:c * 32 + h + 1]),
                    deps=[last_b[yb], R["dbnd"].w],
                    writes=[res(("ybr", ybs))], tag=f"be{c}_{h}")
                bank_rd[yb].append(e_be)
                e_ba = s.ev("dve", lambda v, h=h, c=c, ybs=ybs: v.tensor_add(
                    y_tok[:, c * D + h * 64:c * D + (h + 1) * 64],
                    y_tok[:, c * D + h * 64:c * D + (h + 1) * 64],
                    ybr[ybs][:]),
                    deps=[e_be], reads=[res(("ybr", ybs))],
                    writes=[res(("ytk", c, h))], tag=f"ba{c}_{h}")
        if c >= 1:
            emit_p4b(c - 1)

    emit_p4b(NG - 1)
    s.fence()
    if phases <= 35:
        return locals()

    s.fence()
    if phases <= 4:
        return locals()

    e_x1 = {}
    for jf in range(NK):
        buf = jf % 2
        sl = jf % 2
        if jf in e_wosld:
            e_ld = e_wosld[jf]
        else:
            e_ld = s.ev("pool", lambda g_, jf=jf, buf=buf: g_.dma_start(
                out=wos[buf][:],
                in_=d_wout[:, jf * NK * 128:(jf + 1) * NK * 128]),
                deps=[wos_consumed[buf]], sem=f"ws{buf}", incv=16,
                tag=f"osl{jf}")
        e_x0 = s.ev("sp", lambda sp, jf=jf, sl=sl: sp.dma_start(
            out=x0r[sl][:], in_=d_u[jf * 128:(jf + 1) * 128, 3:TE]),
            writes=[res(("x0r", sl))], sem=f"x0{sl}", incv=16,
            tag=f"x0{jf}")
        for th in range(2):
            bank = (jf * 2 + th) % 2
            last = None
            for k in range(NK):
                last = s.ev("pe", lambda t, k=k, th=th, bank=bank, buf=buf, st=(k == 0), sp_=(k == NK - 1): t.matmul(
                    pf[bank][:, 0:512],
                    wos[buf][:, k * 128:(k + 1) * 128],
                    z_sb[:, k * T + th * 512:k * T + (th + 1) * 512],
                    start=st, stop=sp_),
                    deps=[e_ld], reads=[res(("z", k))],
                    writes=([res(("pf", bank))] if k == 0 else []),
                    tag=f"wo{jf}_{th}_{k}")
            wos_consumed[buf] = last
            e_e = s.ev("dve", lambda v, jf=jf, th=th, sl=sl, bank=bank: v.tensor_add(
                x1[:, jf * T + th * 512:jf * T + (th + 1) * 512],
                x0r[sl][:, th * 512:(th + 1) * 512], pf[bank][:, 0:512]),
                deps=[last, e_x0], reads=[res(("pf", bank)), res(("x0r", sl))],
                writes=[res(("x1", jf))], tag=f"x1e{jf}_{th}")
            e_x1[(jf, th)] = e_e
    if phases <= 5:
        return locals()
    e_x1c = []
    for k in range(NK):
        e_c = s.ev("dve", lambda v, k=k: v.tensor_copy(
            x1bf[:, k * T:(k + 1) * T], x1[:, k * T:(k + 1) * T]),
            deps=[e_x1[(k, 0)], e_x1[(k, 1)]], reads=[res(("x1", k))],
            writes=[res(("x1bf", k))], tag=f"x1c{k}")
        e_x1c.append(e_c)
    CG2 = [(0, 512), (512, 1024)]
    e_bc2 = []
    x1bf_readers = {k: [] for k in range(NK)}
    for cg, (c0, c1) in enumerate(CG2):
        n = c1 - c0
        lastm = lasts_ = None
        for k in range(NK):
            sl = k % 2
            e_sq = s.ev("act", lambda a, k=k, c0=c0, n=n, sl=sl: a.activation(
                sq2[sl][:, 0:n], x1bf[:, k * T + c0:k * T + c0 + n],
                AF.Square), deps=[e_x1c[k]], writes=[res(("sq2", sl))],
                tag=f"q2{cg}_{k}")
            x1bf_readers[k].append(e_sq)
            lastm = s.ev("pe", lambda t, k=k, c0=c0, n=n, st=(k == 0), sp_=(k == NK - 1): t.matmul(
                pf[2][0:1, 0:n], onescol[:], x1bf[:, k * T + c0:k * T + c0 + n],
                start=st, stop=sp_), deps=[e_x1c[k], e_onescol],
                writes=([res(("pf", 2))] if k == 0 else []), tag=f"m2m{cg}_{k}")
            x1bf_readers[k].append(lastm)
            lasts_ = s.ev("pe", lambda t, k=k, n=n, sl=sl, st=(k == 0), sp_=(k == NK - 1): t.matmul(
                pf[3][0:1, 0:n], onescol[:], sq2[sl][:, 0:n],
                start=st, stop=sp_), deps=[e_sq], reads=[res(("sq2", sl))],
                writes=([res(("pf", 3))] if k == 0 else []), tag=f"s2m{cg}_{k}")
        e_em2 = s.ev("dve", lambda v, n=n: v.tensor_scalar_mul(
            m512[:, 0:n], pf[2][0:1, 0:n], INV_D), deps=[lastm],
            reads=[res(("pf", 2))], writes=[res("m2f")], tag=f"em2{cg}")
        e_es2 = s.ev("dve", lambda v, n=n: v.tensor_scalar_mul(
            s2row[:, 0:n], pf[3][0:1, 0:n], INV_D), deps=[lasts_],
            reads=[res(("pf", 3))], writes=[res("s2row")], tag=f"es2{cg}")
        e_mq2 = s.ev("act", lambda a, n=n: a.activation(
            scr2[:, 0:n], m512[:, 0:n], AF.Square), deps=[e_em2],
            reads=[res("m2f")], writes=[res("scr2")], tag=f"mq2{cg}")
        e_v2 = s.ev("dve", lambda v, n=n: v.tensor_sub(
            s2row[:, 0:n], s2row[:, 0:n], scr2[:, 0:n]), deps=[e_es2, e_mq2],
            writes=[res("s2row")], tag=f"v2{cg}")
        e_l2 = s.ev("act", lambda a, n=n: a.activation(
            scr2[:, 0:n], s2row[:, 0:n], AF.Ln, bias=c_eps[:]),
            deps=[e_v2, e_ceps, e_mq2], reads=[res("s2row")],
            writes=[res("scr2")], tag=f"l2{cg}")
        e_r2 = s.ev("act", lambda a, n=n: a.activation(
            s2row[:, 0:n], scr2[:, 0:n], AF.Exp, scale=c_nhalf[:]),
            deps=[e_l2, e_cnh], reads=[res("scr2")],
            writes=[res("s2row")], tag=f"r2{cg}")
        e_si = s.ev("act", lambda a, n=n: a.activation(
            scr2[:, 0:n], scr2[:, 0:n], AF.Exp, scale=c_phalf[:]),
            deps=[e_r2, e_cph], writes=[res("scr2")], tag=f"si{cg}")
        e_m2c = s.ev("act", lambda a, n=n, c0=c0: a.activation(
            m2row[0:1, c0:c0 + n], m512[:, 0:n], AF.Copy), deps=[e_em2],
            reads=[res("m2f")], writes=[res("m2row")], tag=f"m2c{cg}")
        e_m2ff = s.ev("dve", lambda v, n=n, c0=c0: v.tensor_copy(
            m2f[0:1, c0:c0 + n], m512[:, 0:n]), deps=[e_em2],
            reads=[res("m2f")], writes=[res("m2ffull")], tag=f"m2ff{cg}")
        for (dstrow, dstbc, rres, dep) in [
                (m512[:, 0:n], mbc2, "m2f", e_em2),
                (s2row[:, 0:n], sbc2, "s2row", e_r2),
                (scr2[:, 0:n], sigbc, "scr2", e_si)]:
            e_b = s.ev("pe", lambda t, dstrow=dstrow, n=n: t.matmul(
                pf[4][:, 0:n], ones1f[:], dstrow, start=True, stop=True),
                deps=[dep, e_ones1f], reads=[res(rres)],
                writes=[res(("pf", 4))], tag=f"bc2{cg}")
            e_v = s.ev("dve", lambda v, dstbc=dstbc, c0=c0, n=n: v.tensor_copy(
                dstbc[:, c0:c0 + n], pf[4][:, 0:n]), deps=[e_b],
                reads=[res(("pf", 4))], writes=[res(id(dstbc))], tag=f"bv2{cg}")
            e_bc2.append(e_v)
    lastrs = None
    for k in range(NK):
        lastrs = s.ev("pe", lambda t, k=k, st=(k == 0), sp_=(k == NK - 1): t.matmul(
            pf[3][0:8, 0:1], wrf_sb[:, k * 8:(k + 1) * 8], onescolf[:],
            start=st, stop=sp_), deps=[e_wrf, e_onescolf],
            writes=([res(("pf", 3))] if k == 0 else []), tag=f"rs{k}")
    e_rsv = s.ev("dve", lambda v: v.tensor_copy(rs_col[:], pf[3][0:8, 0:1]),
                 deps=[lastrs], reads=[res(("pf", 3))], writes=[res("rscol")],
                 tag="rsv")
    e_rst = s.ev("pe", lambda t: t.transpose(
        pf[3][0:1, 0:8], rs_col[:], identf[0:8, 0:8]), deps=[e_rsv, e_idf],
        reads=[res("rscol")], writes=[res(("pf", 3))], tag="rst")
    e_nrs = s.ev("dve", lambda v: v.tensor_scalar_mul(
        nrs_row[:], pf[3][0:1, 0:8], -1.0), deps=[e_rst],
        reads=[res(("pf", 3))], writes=[res("nrs")], tag="nrs")
    e_ln2 = []
    for k in range(NK):
        sl = k % 2
        e_t = s.ev("dve", lambda v, k=k, sl=sl: v.tensor_sub(
            x0r[sl][:], x1[:, k * T:(k + 1) * T], mbc2[:]),
            deps=e_bc2 + [e_x1[(k, 0)], e_x1[(k, 1)]],
            reads=[res(id(mbc2)), res(("x1", k))],
            writes=[res(("x0r", sl))], tag=f"n2s{k}")
        e_n = s.ev("dve", lambda v, k=k, sl=sl: v.tensor_mul(
            ln2[:, k * T:(k + 1) * T], x0r[sl][:], sbc2[:]),
            deps=[e_t], reads=[res(("x0r", sl)), res(id(sbc2))],
            writes=[res(("ln2", k))], tag=f"n2m{k}")
        e_ln2.append(e_n)
    e_sc_ev = []
    for cg in range(2):
        c0 = cg * 512
        last = None
        for k in range(NK):
            last = s.ev("pe", lambda t, k=k, c0=c0, st=(k == 0): t.matmul(
                pf[2][0:8, 0:512], wrf_sb[:, k * 8:(k + 1) * 8],
                x1[:, k * T + c0:k * T + c0 + 512], start=st, stop=False),
                deps=[e_wrf, e_x1[(k, 0)], e_x1[(k, 1)]],
                reads=[res(("x1", k))],
                writes=([res(("pf", 2))] if k == 0 else []), tag=f"rt{cg}_{k}")
        last = s.ev("pe", lambda t, c0=c0: t.matmul(
            pf[2][0:8, 0:512], nrs_row[:], m2f[0:1, c0:c0 + 512],
            start=False, stop=True),
            deps=[last, e_nrs, R["m2ffull"].w], reads=[res("nrs"), res("m2ffull")],
            tag=f"rtm{cg}")
        e_ev = s.ev("dve", lambda v, c0=c0: v.tensor_mul(
            scores[:, c0:c0 + 512], pf[2][0:8, 0:512], sbc2[0:8, c0:c0 + 512]),
            deps=[last] + e_bc2, reads=[res(("pf", 2)), res(id(sbc2))],
            writes=[res("scores")], tag=f"rte{cg}")
        e_sc_ev.append(e_ev)
    def emit_top2():
        for gg in range(8):
            e_t = s.ev("pe", lambda t, gg=gg: t.transpose(
                pf[3][:, 0:8], scores[0:8, gg * 128:(gg + 1) * 128],
                identf[0:8, 0:8]), deps=e_sc_ev + [e_idf], reads=[res("scores")],
                writes=[res(("pf", 3))], tag=f"st{gg}")
            e_v = s.ev("dve", lambda v, gg=gg: v.tensor_copy(
                sttok[:, gg * 8:(gg + 1) * 8], pf[3][:, 0:8]), deps=[e_t],
                reads=[res(("pf", 3))], writes=[res("sttok")], tag=f"stv{gg}")
            sl = sttok[:, gg * 8:(gg + 1) * 8]
            e1 = s.ev("dve", lambda v, sl=sl: v.reduce_max(r_m1[:], sl, axis=AX.X),
                      deps=[e_v], writes=[res("rm1")], tag=f"rm1{gg}")
            e2 = s.ev("dve", lambda v: v.tensor_scalar_mul(r_nm1[:], r_m1[:], -1.0),
                      deps=[e1], reads=[res("rm1")], writes=[res("rnm1")],
                      tag=f"rnm{gg}")
            e3 = s.ev("dve", lambda v, sl=sl: v.tensor_single_scalar(
                r_eq1[:], sl, r_m1[:], ALU.is_equal), deps=[e1],
                reads=[res("rm1")], writes=[res("req1")], tag=f"req1{gg}")
            e4 = s.ev("dve", lambda v, sl=sl: v.scalar_tensor_tensor(
                r_sc2[:], r_eq1[:], -1e30, sl, op0=ALU.mult, op1=ALU.add),
                deps=[e3], reads=[res("req1")], writes=[res("rsc2")],
                tag=f"rsc2{gg}")
            e5 = s.ev("dve", lambda v: v.reduce_max(r_m2[:], r_sc2[:], axis=AX.X),
                      deps=[e4], reads=[res("rsc2")], writes=[res("rm2")],
                      tag=f"rm2{gg}")
            e6 = s.ev("dve", lambda v, sl=sl: v.tensor_single_scalar(
                r_eq2[:], sl, r_m2[:], ALU.is_equal), deps=[e5],
                reads=[res("rm2")], writes=[res("req2")], tag=f"req2{gg}")
            e7 = s.ev("act", lambda a: a.activation(
                r_em[:], r_m2[:], AF.Exp, bias=r_nm1[:]), deps=[e5, e2],
                reads=[res("rm2"), res("rnm1")], writes=[res("rem")],
                tag=f"rem{gg}")
            e8 = s.ev("dve", lambda v: v.tensor_scalar_add(r_p1[:], r_em[:], 1.0),
                      deps=[e7], reads=[res("rem")], writes=[res("rp1")],
                      tag=f"rp1a{gg}")
            e9 = s.ev("dve", lambda v: v.reciprocal(r_p1[:], r_p1[:]),
                      deps=[e8], writes=[res("rp1")], tag=f"rp1b{gg}")
            e10 = s.ev("dve", lambda v: v.tensor_mul(r_p2[:], r_em[:], r_p1[:]),
                       deps=[e9], reads=[res("rem"), res("rp1")],
                       writes=[res("rp2")], tag=f"rp2{gg}")
            e11 = s.ev("dve", lambda v, gg=gg: v.tensor_scalar_mul(
                wtok[:, gg * 8:(gg + 1) * 8], r_eq1[:], r_p1[:]),
                deps=[e10, e3], reads=[res("req1"), res("rp1")],
                writes=[res(("wtok", gg))], tag=f"w1{gg}")
            e12 = s.ev("dve", lambda v, gg=gg: v.scalar_tensor_tensor(
                wtok[:, gg * 8:(gg + 1) * 8], r_eq2[:], r_p2[:],
                wtok[:, gg * 8:(gg + 1) * 8], op0=ALU.mult, op1=ALU.add),
                deps=[e11, e6], reads=[res("req2"), res("rp2")],
                writes=[res(("wtok", gg))], tag=f"w2{gg}")

    if phases <= 6:
        return locals()
    mslab_consumed = {}
    for m in range(2):
        for sl_ in range(3):
            mslab_consumed[(m, sl_)] = [e_x1[(15, 0)], e_x1[(15, 1)]]
    e_wbv_all = {}

    def emit_wbc(e):
        e_wbv = []
        for gg in range(8):
            ws = gg % 2
            e_wt = s.ev("pe", lambda t, gg=gg, e=e: t.transpose(
                pf[3][0:1, 0:128], wtok[:, gg * 8 + e:gg * 8 + e + 1],
                identf[:]), deps=[R[("wtok", gg)].w, e_idf],
                writes=[res(("pf", 3))], tag=f"wt{e}_{gg}")
            e_wv = s.ev("dve", lambda v, ws=ws: v.tensor_copy(
                wrow[ws][:], pf[3][0:1, 0:128]), deps=[e_wt],
                reads=[res(("pf", 3))], writes=[res(("wrow", ws))],
                tag=f"wv{e}_{gg}")
            e_wb = s.ev("pe", lambda t, gg=gg, ws=ws: t.matmul(
                pf[4][:, (gg % 4) * 128:(gg % 4 + 1) * 128], ones1f[:],
                wrow[ws][:], start=True, stop=True),
                deps=[e_wv, e_ones1f], reads=[res(("wrow", ws))],
                writes=([res(("pf", 4))] if gg % 4 == 0 else []),
                tag=f"wb{e}_{gg}")
            if gg % 4 == 3:
                e_bv = s.ev("dve", lambda v, gg=gg: v.tensor_copy(
                    wbc[:, (gg - 3) * 128:(gg + 1) * 128], pf[4][:, 0:512]),
                    deps=[e_wb], reads=[res(("pf", 4))],
                    writes=[res("wbc")], tag=f"wbv{e}_{gg}")
                e_wbv.append(e_bv)
        e_wbv_all[e] = e_wbv

    for e in range(E):
        if e > 0:
            emit_wbc(e)
        e_wbv = e_wbv_all.get(e, [])
        for jf in range(NJF):
            slot = jf % 3
            e_lg = s.ev("pool", lambda g_, e=e, jf=jf, slot=slot: g_.dma_start(
                out=mslab[0][:, slot * NK * 128:(slot + 1) * NK * 128],
                in_=d_wg[e, :, jf * NK * 128:(jf + 1) * NK * 128]),
                deps=mslab_consumed[(0, slot)], sem=f"ms0{slot}", incv=16,
                tag=f"lg{e}_{jf}")
            e_lu = s.ev("pool", lambda g_, e=e, jf=jf, slot=slot: g_.dma_start(
                out=mslab[1][:, slot * NK * 128:(slot + 1) * NK * 128],
                in_=d_wu[e, :, jf * NK * 128:(jf + 1) * NK * 128]),
                deps=mslab_consumed[(1, slot)], sem=f"ms1{slot}", incv=16,
                tag=f"lu{e}_{jf}")
            for th in range(2):
                sgs = (jf * 2 + th) % 2
                lastg = None
                for k in range(NK):
                    lastg = s.ev("pe", lambda t, slot=slot, k=k, th=th, st=(k == 0), sp_=(k == NK - 1): t.matmul(
                        pf[0][:, 0:512],
                        mslab[0][:, (slot * NK + k) * 128:(slot * NK + k + 1) * 128],
                        ln2[:, k * T + th * 512:k * T + (th + 1) * 512],
                        start=st, stop=sp_),
                        deps=[e_lg, e_ln2[k]],
                        writes=([res(("pf", 0))] if k == 0 else []),
                        tag=f"gm{e}_{jf}_{th}_{k}")
                e_sg = s.ev("act", lambda a_, sgs=sgs: a_.activation(
                    sgr[sgs][:], pf[0][:, 0:512], AF.Silu),
                    deps=[lastg], reads=[res(("pf", 0))],
                    writes=[res(("sgr", sgs))], tag=f"sg{e}_{jf}_{th}")
                lastu = None
                for k in range(NK):
                    lastu = s.ev("pe", lambda t, slot=slot, k=k, th=th, st=(k == 0), sp_=(k == NK - 1): t.matmul(
                        pf[1][:, 0:512],
                        mslab[1][:, (slot * NK + k) * 128:(slot * NK + k + 1) * 128],
                        ln2[:, k * T + th * 512:k * T + (th + 1) * 512],
                        start=st, stop=sp_),
                        deps=[e_lu, e_ln2[k]],
                        writes=([res(("pf", 1))] if k == 0 else []),
                        tag=f"um{e}_{jf}_{th}_{k}")
                if e == 0:
                    e_h1 = s.ev("dve", lambda v, jf=jf, th=th, sgs=sgs: v.tensor_mul(
                        hidden[:, jf * T + th * 512:jf * T + (th + 1) * 512],
                        sgr[sgs][:], pf[1][:, 0:512]),
                        deps=[lastu, e_sg], reads=[res(("pf", 1)), res(("sgr", sgs))],
                        writes=[res(("hid", e, jf))], tag=f"h1{e}_{jf}_{th}")
                else:
                    e_h1 = s.ev("dve", lambda v, sgs=sgs: v.tensor_mul(
                        h1r[sgs][:], sgr[sgs][:], pf[1][:, 0:512]),
                        deps=[lastu, e_sg], reads=[res(("pf", 1)), res(("sgr", sgs))],
                        writes=[res(("h1r", sgs))], tag=f"h1{e}_{jf}_{th}")
                    e_h2 = s.ev("dve", lambda v, e=e, jf=jf, th=th, sgs=sgs: v.tensor_mul(
                        hidden[:, (e * NJF + jf) * T + th * 512:(e * NJF + jf) * T + (th + 1) * 512],
                        h1r[sgs][:], wbc[:, th * 512:(th + 1) * 512]),
                        deps=[e_h1] + e_wbv, reads=[res(("h1r", sgs)), res("wbc")],
                        writes=[res(("hid", e, jf))], tag=f"h2{e}_{jf}_{th}")
                mslab_consumed[(0, slot)] = [lastg]
                mslab_consumed[(1, slot)] = [lastu]
        if e == 0:
            emit_top2()
            emit_wbc(0)
            e_wbv = e_wbv_all[0]
            for jf in range(NJF):
                for th in range(2):
                    s.ev("dve", lambda v, jf=jf, th=th: v.tensor_mul(
                        hidden[:, jf * T + th * 512:jf * T + (th + 1) * 512],
                        hidden[:, jf * T + th * 512:jf * T + (th + 1) * 512],
                        wbc[:, th * 512:(th + 1) * 512]),
                        deps=e_wbv, reads=[res("wbc")],
                        writes=[res(("hid", 0, jf))], tag=f"h2p{jf}_{th}")
    s.fence()

    if phases <= 7:
        return locals()
    e_sts = []
    dslab_consumed = [mslab_consumed[(0, 2)][0], mslab_consumed[(1, 2)][0]]
    dslab_first = [mslab_consumed[(0, 0)][0], mslab_consumed[(0, 1)][0],
                   mslab_consumed[(1, 0)][0], mslab_consumed[(1, 1)][0]]
    for j in range(NK):
        buf = j % 2
        e_ldp = []
        for p in range(3):
            e_ldp.append(s.ev("pool", lambda g_, j=j, buf=buf, p=p: g_.dma_start(
                out=mslab[buf][:, p * 2048:(p + 1) * 2048],
                in_=d_wd[:, j * E * NJF * 128 + p * 2048:
                         j * E * NJF * 128 + (p + 1) * 2048]),
                deps=[dslab_consumed[buf]] + (dslab_first if j < 2 else []),
                sem=f"m8{buf}{p}", incv=16, tag=f"dsl{j}_{p}"))
        for th in range(2):
            bank = (j * 2 + th) % 2
            last = None
            first = True
            for e in range(E):
                for kk in range(NJF):
                    last = s.ev("pe", lambda t, e=e, kk=kk, th=th, bank=bank, buf=buf, st=first: t.matmul(
                        pf[bank][:, 0:512],
                        mslab[buf][:, (e * NJF + kk) * 128:(e * NJF + kk + 1) * 128],
                        hidden[:, (e * NJF + kk) * T + th * 512:(e * NJF + kk) * T + (th + 1) * 512],
                        start=st, stop=False),
                        deps=[e_ldp[(e * NJF + kk) // 16]],
                        reads=[res(("hid", e, kk))],
                        writes=([res(("pf", bank))] if first else []),
                        tag=f"dm{j}_{th}_{e}_{kk}")
                    first = False
            last = s.ev("pe", lambda t, j=j, th=th, bank=bank: t.matmul(
                pf[bank][:, 0:512], ones1b[:],
                m2row[0:1, th * 512:(th + 1) * 512], start=False, stop=True),
                deps=[last, R["m2row"].w, e_ms[3]], reads=[res("m2row")],
                tag=f"dmm{j}_{th}")
            dslab_consumed[buf] = last
            ts_ = (j * 2 + th) % 2
            e_t1 = s.ev("dve", lambda v, j=j, th=th, ts_=ts_: v.tensor_mul(
                tout[ts_][:], ln2[:, j * T + th * 512:j * T + (th + 1) * 512],
                sigbc[:, th * 512:(th + 1) * 512]),
                deps=[R[("ln2", j)].w, R[id(sigbc)].w],
                reads=[res(("ln2", j)), res(id(sigbc))],
                writes=[res(("tout", ts_))], tag=f"t1_{j}_{th}")
            e_t2 = s.ev("dve", lambda v, ts_=ts_, bank=bank: v.tensor_add(
                ostg[ts_][:], tout[ts_][:], pf[bank][:, 0:512]),
                deps=[last, e_t1], reads=[res(("pf", bank)), res(("tout", ts_))],
                writes=[res(("ostg", ts_))], tag=f"t2_{j}_{th}")
            e_st = s.ev("sp", lambda sp, j=j, th=th, ts_=ts_: sp.dma_start(
                out=d_o[j * 128:(j + 1) * 128, th * 512:(th + 1) * 512],
                in_=ostg[ts_][:]), deps=[e_t2], reads=[res(("ostg", ts_))],
                sem=f"o{ts_}", incv=16, tag=f"st{j}_{th}")
            e_sts.append(e_st)
    e_end = s.ev("sp", lambda sp: sp.nop(), deps=e_sts, tag="end")

    return locals()


def finish(env):
    nc, s, sems = env["nc"], env["s"], env["sems"]
    s.finalize_plan()
    with nc.Block() as blk:
        @blk.tensor
        def _(t):
            s.emit("pe", t, sems)

        @blk.vector
        def _(v):
            s.emit("dve", v, sems)

        @blk.scalar
        def _(a):
            s.emit("act", a, sems)

        @blk.gpsimd
        def _(g):
            s.emit("pool", g, sems)

        @blk.sync
        def _(sp):
            s.emit("sp", sp, sems)
    nc.finalize()
    return nc


def stage(inputs):
    W_in = inputs["L0_attn_W_in"]
    W_out = inputs["L0_attn_W_out"]
    conv_w = inputs["L0_attn_conv_w"]
    conv_b = inputs["L0_attn_conv_b"]
    z_bias = inputs["L0_attn_z_bias"]
    Dd = inputs["L0_attn_D"]
    Wr = inputs["L0_ffn_w_gate_router"]
    Wg = inputs["L0_ffn_w_gate_proj"]
    Wu = inputs["L0_ffn_w_up_proj"]
    Wd = inputs["L0_ffn_w_down_proj"]
    u = inputs["L0_attn_u"]
    C = np.ascontiguousarray

    wxbc = C(np.asarray(W_in[:CONVD]).reshape(NJX, 128, NK, 128)
             .transpose(3, 0, 2, 1).reshape(128, -1))
    wz = C(np.asarray(W_in[CONVD:AOFF]).reshape(NJZ, 128, NK, 128)
           .transpose(3, 0, 2, 1).reshape(128, -1))
    wa = C(np.asarray(W_in[AOFF:]).reshape(32, NK, 128)
           .transpose(2, 1, 0).reshape(128, -1))
    wout = C(np.asarray(W_out).reshape(NK, 128, NK, 128)
             .transpose(3, 0, 2, 1).reshape(128, -1))
    wr = C(np.asarray(Wr).T.reshape(NK, 128, 8).transpose(1, 0, 2)
           .reshape(128, -1))
    wg = C(np.asarray(Wg).reshape(E, NJF, 128, NK, 128)
           .transpose(4, 0, 1, 3, 2).reshape(128, E, -1).transpose(1, 0, 2))
    wu = C(np.asarray(Wu).reshape(E, NJF, 128, NK, 128)
           .transpose(4, 0, 1, 3, 2).reshape(128, E, -1).transpose(1, 0, 2))
    wd = C(np.asarray(Wd).reshape(E, NK, 128, NJF, 128)
           .transpose(4, 1, 0, 3, 2).reshape(128, -1))
    cw = C(np.asarray(conv_w)[:, 0, :].reshape(NJX, 128, 4)
           .transpose(1, 0, 2).reshape(128, -1))
    cb = C(np.asarray(conv_b).reshape(NJX, 128).T)
    zb = C(np.asarray(z_bias).reshape(NJZ, 128).T)
    drep = C(np.repeat(np.asarray(Dd), HD).reshape(NK, 128).T)
    shared = dict(wxbc=wxbc, wz=wz, wa=wa, wout=wout, wr=wr, wg=wg, wu=wu,
                  wd=wd, cw=cw, cb=cb, zb=zb, drep=drep)
    un = np.asarray(u)
    per_core = []
    for core in range(8):
        b, h = core // 2, core % 2
        ue = np.zeros((D, TE), np.float32)
        if h == 0:
            ue[:, 3:] = un[b, 0:T, :].T
        else:
            ue[:, :] = un[b, T - 3:L, :].T
        m = dict(shared)
        m["u"] = C(ue)
        m["smask"] = np.full((128, 1), float(h), np.float32)
        per_core.append(m)
    return per_core


def kernel(**inputs):
    per_core = stage(inputs)
    env = build_nc(debug=0)
    nc = finish(env)
    res = run_bass_kernel_spmd(nc, per_core, core_ids=list(range(8)))
    u = np.asarray(inputs["L0_attn_u"])
    out = np.empty((B, L, D), np.float32)
    for core in range(8):
        b, h = core // 2, core % 2
        out[b, h * T:(h + 1) * T, :] = np.asarray(res.results[core]["o"]).T
    return out




# version 20
# speedup vs baseline: 1.0805x; 1.0009x over previous
import numpy as np
import ml_dtypes
import concourse.bass as bass
import concourse.mybir as mybir
from concourse.bass_utils import run_bass_kernel_spmd
from contextlib import ExitStack

F32 = mybir.dt.float32
BF16 = mybir.dt.bfloat16
AF = mybir.ActivationFunctionType
ALU = mybir.AluOpType
AX = mybir.AxisListType

B, L, D = 4, 2048, 2048
T = 1024
TE = T + 3
NH, HD, DS = 32, 64, 64
NK = 16
E, FF = 8, 768
NJZ, NJX, NJF = 16, 48, 6
CONVD, AOFF = 6144, 8192
HD = 64
Q = 64
NG = 8
NC = 16
EPS = 1e-5
INV_D = 1.0 / D


class Ev:
    __slots__ = ("eng", "fn", "deps", "sem", "incv", "needed", "val", "tag")

    def __init__(self, eng, fn, deps, sem, incv, tag):
        self.eng, self.fn, self.deps = eng, fn, deps
        self.sem, self.incv = sem, incv
        self.needed = False
        self.val = None
        self.tag = tag


class Res:
    __slots__ = ("w", "r")

    def __init__(self):
        self.w = None
        self.r = []


def succeed(new, *old):
    rs = []
    for o in old:
        rs.extend(o.r)
        if o.w is not None:
            rs.append(o.w)
    new.r = rs
    new.w = None


class Sched:
    def __init__(self):
        self.events = []
        self.last_per_eng = {}
        self.pending_fence = {}

    def ev(self, eng, fn, deps=(), reads=(), writes=(), sem=None, incv=1,
           tag=""):
        dl = [d for d in deps if d is not None]
        if eng in self.pending_fence:
            dl.extend(self.pending_fence.pop(eng))
        for r in reads:
            if r.w is not None:
                dl.append(r.w)
        e = Ev(eng, fn, dl, sem if sem is not None else eng, incv, tag)
        for r in reads:
            r.r.append(e)
        for w in writes:
            e.deps.extend(w.r)
            if w.w is not None:
                e.deps.append(w.w)
            w.w = e
            w.r = []
        self.events.append(e)
        self.last_per_eng[eng] = e
        return e

    def fence(self):
        lasts = [e for e in self.last_per_eng.values()]
        for eng in ("pe", "dve", "act", "pool", "sp"):
            self.pending_fence[eng] = list(lasts)

    def finalize_plan(self):
        for e in self.events:
            e.deps = [d for d in e.deps if d is not e]
            for d in e.deps:
                d.needed = True
            if e.sem != e.eng:
                e.needed = True
        counts = {}
        for e in self.events:
            if e.needed:
                counts[e.sem] = counts.get(e.sem, 0) + e.incv
                e.val = counts[e.sem]

    def emit(self, eng_name, eng, sems):
        hw = {}
        for e in self.events:
            if e.eng != eng_name:
                continue
            for d in e.deps:
                assert d.needed, (e.tag, d.tag)
                if hw.get(d.sem, -1) < d.val:
                    eng.wait_ge(sems[d.sem], d.val)
                    hw[d.sem] = d.val
            inst = e.fn(eng)
            if e.needed:
                inst.then_inc(sems[e.sem], e.incv)
                if eng_name == "pool" and e.sem == "pool":
                    eng.wait_ge(sems["pool"], e.val)
                    hw["pool"] = e.val


class Arena:
    def __init__(self, lo, hi):
        self.lo, self.hi, self.cur = lo, hi, lo

    def take(self, nbytes):
        off = (self.cur + 31) & ~31
        assert off + nbytes <= self.hi, (off, nbytes, self.hi)
        self.cur = off + nbytes
        return off

    def reset(self):
        self.cur = self.lo


def build_nc(debug=0, phases=99, p3sub=9):
    p3sub = 9 if p3sub == 9 else p3sub
    nc = bass.Bass("TRN2", target_bir_lowering=False, debug=False)
    s = Sched()
    KB = 1024
    dbg_outs = {}

    dram = lambda n, shp, dt=F32, kind="ExternalInput": nc.dram_tensor(
        n, shp, dt, kind=kind).ap()
    d_u = dram("u", [D, TE])
    d_wxbc = dram("wxbc", [128, NJX * NK * 128])
    d_wz = dram("wz", [128, NJZ * NK * 128])
    d_wa = dram("wa", [128, NK * 32])
    d_wout = dram("wout", [128, NK * NK * 128])
    d_wr = dram("wr", [128, NK * 8])
    d_wg = dram("wg", [E, 128, NJF * NK * 128])
    d_wu = dram("wu", [E, 128, NJF * NK * 128])
    d_wd = dram("wd", [128, NK * E * NJF * 128])
    d_cw = dram("cw", [128, NJX * 4])
    d_cb = dram("cb", [128, NJX])
    d_zb = dram("zb", [128, NJZ])
    d_drep = dram("drep", [128, NK])
    d_smask = dram("smask", [128, 1])
    d_o = dram("o", [D, T], kind="ExternalOutput")
    d_ccin = [dram(f"cc_in{i}", [128, NC * 32], BF16, kind="Internal")
              for i in range(2)]
    d_ccout = [dram(f"cc_out{i}", [256, NC * 32], BF16, kind="Internal")
               for i in range(2)]

    KBo = 1024

    SBUF_BASE = 16512
    def sb(name, shape, dt, off):
        nb = int(np.prod(shape[1:])) * (2 if dt == BF16 else 4)
        assert SBUF_BASE + off + nb <= 229344, (name, off, nb)
        return nc.alloc_sbuf_tensor_at(name, shape, dt, offset=SBUF_BASE + off)

    identb = sb("identb", [128, 128], BF16, 0)
    identf = sb("identf", [128, 128], F32, 256)
    ones1f = sb("ones1f", [1, 128], F32, 768)
    ones1b = sb("ones1b", [1, 128], BF16, 1280)
    onescol = sb("onescol", [128, 1], BF16, 1536)
    u01 = sb("u01", [128, 128], F32, 1568)
    c_eps = sb("c_eps", [1, 1], F32, 2080)
    c_nhalf = sb("c_nhalf", [1, 1], F32, 2112)
    c_phalf = sb("c_phalf", [1, 1], F32, 4064)
    onescolf = sb("onescolf", [128, 1], F32, 2144)
    cw_sb = sb("cw_sb", [128, NJX * 4], F32, 2176)
    cb_sb = sb("cb_sb", [128, NJX], F32, 2944)
    zb_sb = sb("zb_sb", [128, NJZ], F32, 3136)
    drep_sb = sb("drep_sb", [128, NK], F32, 3200)
    smask_sb = sb("smask_sb", [128, 1], F32, 3264)
    wr_sb = sb("wr_sb", [128, NK * 8], BF16, 3296)
    SL = 4480
    wslab = [sb(f"wslab{i}", [128, 2 * NK * 128], BF16, SL + i * 8 * KBo)
             for i in range(2)]
    wos = [sb(f"wos{i}", [128, NK * 128], BF16, SL + i * 4 * KBo)
           for i in range(2)]
    din = sb("din", [128, 256], F32, SL + 8 * KBo)
    rcol = sb("rcol", [128, 256], F32, SL + 9 * KBo)
    ecol = sb("ecol", [128, 256], F32, SL + 10 * KBo)
    dbnd = sb("dbnd", [128, 256], F32, SL + 11 * KBo)
    s_inb = sb("s_inb", [128, NC * 64], BF16, SL + 12 * KBo)
    scores = sb("scores", [8, T], F32, SL + 0 * KBo)
    sigbc = sb("sigbc", [128, T], BF16, SL + 4 * KBo)
    m2row = sb("m2row", [1, T], BF16, SL + 6 * KBo)
    wbc = sb("wbc", [128, T], BF16, SL + 8 * KBo)
    sttok = sb("sttok", [128, 64], F32, SL + 10 * KBo)
    wtok = sb("wtok", [128, 64], F32, SL + 10 * KBo + 256)
    wrow = [sb(f"wrow{i}", [1, 128], F32, SL + 10 * KBo + 512 + i * 512)
            for i in range(2)]
    r_m1 = sb("r_m1", [128, 1], F32, SL + 11 * KBo + 512)
    r_nm1 = sb("r_nm1", [128, 1], F32, SL + 11 * KBo + 544)
    r_m2 = sb("r_m2", [128, 1], F32, SL + 11 * KBo + 576)
    r_em = sb("r_em", [128, 1], F32, SL + 11 * KBo + 608)
    r_p1 = sb("r_p1", [128, 1], F32, SL + 11 * KBo + 640)
    r_p2 = sb("r_p2", [128, 1], F32, SL + 11 * KBo + 672)
    r_eq1 = sb("r_eq1", [128, 8], F32, SL + 11 * KBo + 704)
    r_eq2 = sb("r_eq2", [128, 8], F32, SL + 11 * KBo + 736)
    r_sc2 = sb("r_sc2", [128, 8], F32, SL + 11 * KBo + 768)
    m2f = sb("m2f_row", [1, T], F32, SL + 12 * KBo)
    rs_col = sb("rs_col", [8, 1], F32, SL + 11 * KBo + 800)
    nrs_row = sb("nrs_row", [1, 8], F32, SL + 11 * KBo + 832)
    wrf_sb = sb("wrf_sb", [128, NK * 8], F32, 3552)
    S1 = 20864
    ln1 = sb("ln1", [128, NK * TE], BF16, S1)
    y_tok = sb("y_tok", [128, NG * D], BF16, S1)
    ln2 = sb("ln2", [128, NK * T], BF16, S1)
    S2 = 53760
    z_sb = sb("z_sb", [128, NK * T], BF16, S2)
    mslab = [sb(f"mslab{i}", [128, 3 * NK * 128], BF16, S2 + i * 12 * KBo)
             for i in range(2)]
    S3 = 86528
    u_bf = sb("u_bf", [128, NK * TE], BF16, S3)
    x_cm = sb("x_cm", [128, NK * T], BF16, S3)
    b_cm = sb("b_cm", [128, NK * T], BF16, S3 + 32 * KBo)
    c_cm = sb("c_cm", [128, NK * T], BF16, S3 + 64 * KBo)
    x1 = sb("x1", [128, NK * T], F32, S3)
    x1bf = sb("x1bf", [128, NK * T], BF16, S3 + 64 * KBo)
    hidden = sb("hidden", [128, E * NJF * T], BF16, S3)
    S4 = 184832
    uring = [sb(f"uring{i}", [128, TE], F32, S4 + i * 4128) for i in range(2)]
    xpre = [sb(f"xpre{i}", [128, TE], BF16, S4 + i * 2080) for i in range(2)]
    ctmp = [sb(f"ctmp{i}", [128, T], F32, S4 + 4160 + i * 4096)
            for i in range(2)]
    x_tok = sb("x_tok", [128, 2048], BF16, S4)
    b_tok = sb("b_tok", [128, 2048], BF16, S4 + 4 * KBo)
    xd = sb("xd", [128, 2048], BF16, S4 + 8 * KBo)
    x0r = [sb(f"x0r{i}", [128, T], F32, S4 + i * 4096) for i in range(2)]
    sgr = [sb(f"sgr{i}", [128, 512], F32, S4 + i * 2048) for i in range(2)]
    h1r = [sb(f"h1r{i}", [128, 512], F32, S4 + 4096 + i * 2048)
           for i in range(2)]
    tout = [sb(f"tout{i}", [128, 512], F32, S4 + i * 2048) for i in range(2)]
    ostg = [sb(f"ostg{i}", [128, 512], F32, S4 + 4096 + i * 2048)
            for i in range(2)]
    S5 = 197184
    sqr = [sb(f"sqr{i}", [128, 512], BF16, S5 + i * KBo) for i in range(2)]
    mrow = sb("mrow", [1, 512], F32, S5 + 2 * KBo)
    srow = sb("srow", [1, 512], F32, S5 + 4 * KBo)
    scrow = sb("scrow", [1, 512], F32, S5 + 6 * KBo)
    mbc = sb("mbc", [128, TE], BF16, S5 + 8 * KBo)
    sbc = sb("sbc", [128, TE], BF16, S5 + 8 * KBo + 2080)
    wa_sb = sb("wa_sb", [128, NK * 32], BF16, S5 + 12 * KBo)
    a_sb = sb("a_sb", [32, T], F32, S5)
    zeros32 = sb("zeros32", [32, T], F32, S5 + 4 * KBo)
    g_sb = sb("g_sb", [32, T], F32, S5 + 8 * KBo)
    gt_sb = sb("gt_sb", [128, 256], F32, S5 + 12 * KBo)
    gmrow = sb("gmrow", [1, 256], F32, S5 + 4 * KBo)
    gerow = sb("gerow", [1, 256], F32, S5 + 5 * KBo)
    gmnrow = sb("gmnrow", [1, 256], F32, S5 + 6 * KBo)
    arow = sb("arow", [1, 256], F32, S5 + 7 * KBo)
    expabc = sb("expabc", [128, 256], F32, S5)
    gbc = [sb(f"gbc{i}", [128, 32], F32, S5 + 1 * KBo + i * 128)
           for i in range(2)]
    mtr = [sb(f"mtr{i}", [128, 128], BF16, S5 + 8 * KBo + i * 256)
           for i in range(16)]
    s_bf = sb("s_bf", [128, NC * 64], BF16, S5 + 1 * KBo + 512)
    ybr = [sb(f"ybr{i}", [128, 64], F32, S5 + 3 * KBo + 512 + i * 256)
           for i in range(4)]
    yfr = [sb(f"yfr{i}", [128, 128], F32, S5 + 4 * KBo + 512 + i * 512)
           for i in range(2)]
    mbc2 = sb("mbc2", [128, T], BF16, S5)
    sbc2 = sb("sbc2", [128, T], BF16, S5 + 2 * KBo)
    sq2 = [sb(f"sq2_{i}", [128, 512], BF16, S5 + 4 * KBo + i * KBo)
           for i in range(2)]
    m512 = sb("m512", [1, 512], F32, S5 + 6 * KBo)
    s2row = sb("s2row", [1, 512], F32, S5 + 8 * KBo)
    scr2 = sb("scr2", [1, 512], F32, S5 + 10 * KBo)

    ctx = ExitStack()
    pf = [ctx.enter_context(nc.psum_tensor(f"pf{i}", [128, 512], F32))
          for i in range(6)]
    pbt = [ctx.enter_context(nc.psum_tensor(f"pbt{i}", [128, 128], BF16))
           for i in range(2)]

    sem_names = ["pe", "dve", "act", "pool", "sp",
                 "c_dma", "u0", "u1", "ws0", "ws1", "ms0", "ms1",
                 "ms00", "ms01", "ms02", "ms10", "ms11", "ms12",
                 "m800", "m801", "m802", "m810", "m811", "m812",
                 "x00", "x01", "o0", "o1", "cc", "ccd", "ge_dma",
                 "cc1", "ccd1", "ccw1"]
    if debug:
        sem_names += ["dbg_dma", "dbg_gdma"]
    sems = {n: nc.alloc_semaphore(n) for n in sem_names}

    R = {}

    def res(key):
        if key not in R:
            R[key] = Res()
        return R[key]

    cload = []
    prevld = None
    for dst, src in [(wr_sb, d_wr), (wr_sb, d_wr), (cw_sb, d_cw),
                     (cb_sb, d_cb), (zb_sb, d_zb), (drep_sb, d_drep),
                     (smask_sb, d_smask)]:
        prevld = s.ev("pool", lambda g, d=dst, sr=src: g.dma_start(
            out=d[:], in_=sr), deps=[prevld], sem="c_dma", incv=16,
            tag="cload")
        cload.append(prevld)
    e_ms = []
    e_ms.append(s.ev("pool", lambda g: g.memset(identb[:], 1.0), tag="ms"))
    e_ms.append(s.ev("pool", lambda g: g.memset(identf[:], 1.0), tag="ms"))
    e_ms.append(s.ev("pool", lambda g: g.memset(ones1f[:], 1.0), tag="ms"))
    e_ms.append(s.ev("pool", lambda g: g.memset(ones1b[:], 1.0), tag="ms"))
    e_ms.append(s.ev("pool", lambda g: g.memset(onescol[:], 1.0), tag="ms"))
    e_ms.append(s.ev("pool", lambda g: g.memset(u01[:], 1.0), tag="ms"))
    e_ms.append(s.ev("pool", lambda g: g.nop(), tag="ms"))
    e_ms.append(s.ev("pool", lambda g: g.memset(c_eps[:], EPS), tag="ms"))
    e_ms.append(s.ev("pool", lambda g: g.memset(c_nhalf[:], -0.5), tag="ms"))
    e_ms.append(s.ev("pool", lambda g: g.memset(onescolf[:], 1.0), tag="ms"))
    e_cph = s.ev("pool", lambda g: g.memset(c_phalf[:], 0.5), tag="ms")
    e_ms.append(e_cph)
    e_ones1f, e_ones1b, e_onescol = e_ms[2], e_ms[3], e_ms[4]
    e_ceps, e_cnh, e_onescolf = e_ms[7], e_ms[8], e_ms[9]
    e_xx, e_wr, e_cw, e_cb, e_zbld, e_drepld, e_smld = cload
    e_wrf = s.ev("pool", lambda g: g.dma_start(out=wrf_sb[:], in_=d_wr),
                 deps=[e_smld], sem="c_dma", incv=16, tag="wrfld")
    e_idb = s.ev("pool", lambda g: g.affine_select(
        identb[:], identb[:], pattern=[[1, 128]], compare_op=ALU.is_equal,
        fill=0.0, base=0, channel_multiplier=-1), deps=e_ms, tag="idb")
    e_idf = s.ev("pool", lambda g: g.affine_select(
        identf[:], identf[:], pattern=[[1, 128]], compare_op=ALU.is_equal,
        fill=0.0, base=0, channel_multiplier=-1), deps=[e_idb], tag="idf")
    e_u01 = s.ev("pool", lambda g: g.affine_select(
        u01[:], u01[:], pattern=[[1, 128]], compare_op=ALU.is_ge,
        fill=0.0, base=0, channel_multiplier=-1), deps=[e_idf], tag="u01")
    e_const = cload + [e_u01, e_idf, e_idb]

    e_ucast = []
    for k in range(NK):
        sl = k % 2
        eng = "sp" if sl == 0 else "act"
        e_ld = s.ev(eng, lambda q, k=k, sl=sl: q.dma_start(
            out=uring[sl][:], in_=d_u[k * 128:(k + 1) * 128, :]),
            writes=[res(("uring", sl))], sem=f"u{sl}", incv=16, tag=f"uld{k}")
        e_c = s.ev("act", lambda a, k=k, sl=sl: a.activation(
            u_bf[:, k * TE:(k + 1) * TE], uring[sl][:], AF.Copy),
            deps=[e_ld], reads=[res(("uring", sl))],
            writes=[res(("ubf", k))], tag=f"ucast{k}")
        e_ucast.append(e_c)
    CGS = [(0, 512), (512, 1024), (1024, 1027)]
    e_nrm_all = []
    for cg, (c0, c1) in enumerate(CGS):
        n = c1 - c0
        prev_mm = None
        for k in range(NK):
            sl = k % 2
            e_sq = s.ev("act", lambda a, k=k, c0=c0, c1=c1, sl=sl: a.activation(
                sqr[sl][:, 0:c1 - c0], u_bf[:, k * TE + c0:k * TE + c1],
                AF.Square), deps=[e_ucast[k]], writes=[res(("sqr", sl))],
                tag=f"sq{cg}_{k}")
            e_m1 = s.ev("pe", lambda t, k=k, c0=c0, c1=c1, st=(k == 0), sp=(k == NK - 1): t.matmul(
                pf[0][0:1, 0:c1 - c0], onescol[:], u_bf[:, k * TE + c0:k * TE + c1],
                start=st, stop=sp), deps=[e_ucast[k], e_onescol],
                reads=[], writes=([res(("pf", 0))] if k == 0 else []),
                tag=f"mmM{cg}_{k}")
            e_m2 = s.ev("pe", lambda t, k=k, n=n, sl=sl, st=(k == 0), sp=(k == NK - 1): t.matmul(
                pf[1][0:1, 0:n], onescol[:], sqr[sl][:, 0:n],
                start=st, stop=sp), deps=[e_sq],
                reads=[res(("sqr", sl))],
                writes=([res(("pf", 1))] if k == 0 else []),
                tag=f"mmS{cg}_{k}")
            prev_mm = (e_m1, e_m2)
        e_em = s.ev("dve", lambda v, n=n: v.tensor_scalar_mul(
            mrow[:, 0:n], pf[0][0:1, 0:n], INV_D), deps=[prev_mm[0]],
            reads=[res(("pf", 0))], writes=[res("mrow")], tag=f"evm{cg}")
        e_es = s.ev("dve", lambda v, n=n: v.tensor_scalar_mul(
            srow[:, 0:n], pf[1][0:1, 0:n], INV_D), deps=[prev_mm[1]],
            reads=[res(("pf", 1))], writes=[res("srow")], tag=f"evs{cg}")
        e_msq = s.ev("act", lambda a, n=n: a.activation(
            scrow[:, 0:n], mrow[:, 0:n], AF.Square), deps=[e_em],
            writes=[res("scrow")], tag=f"msq{cg}")
        e_var = s.ev("dve", lambda v, n=n: v.tensor_sub(
            srow[:, 0:n], srow[:, 0:n], scrow[:, 0:n]), deps=[e_es, e_msq],
            writes=[res("srow")], tag=f"var{cg}")
        e_ln = s.ev("act", lambda a, n=n: a.activation(
            srow[:, 0:n], srow[:, 0:n], AF.Ln, bias=c_eps[:]),
            deps=[e_var, e_ceps], writes=[res("srow")], tag=f"vln{cg}")
        e_rs = s.ev("act", lambda a, n=n: a.activation(
            srow[:, 0:n], srow[:, 0:n], AF.Exp, scale=c_nhalf[:]),
            deps=[e_ln, e_cnh], writes=[res("srow")], tag=f"rstd{cg}")
        e_bm = s.ev("pe", lambda t, n=n: t.matmul(
            pf[2][:, 0:n], ones1f[:], mrow[:, 0:n], start=True, stop=True),
            deps=[e_em, e_ones1f], writes=[res(("pf", 2))], tag=f"bcm{cg}")
        e_vm = s.ev("dve", lambda v, n=n, c0=c0: v.tensor_copy(
            mbc[:, c0:c0 + n], pf[2][:, 0:n]), deps=[e_bm],
            reads=[res(("pf", 2))], writes=[res("mbc")], tag=f"bmc{cg}")
        e_bs = s.ev("pe", lambda t, n=n: t.matmul(
            pf[3][:, 0:n], ones1f[:], srow[:, 0:n], start=True, stop=True),
            deps=[e_rs], writes=[res(("pf", 3))], tag=f"bcs{cg}")
        e_vs = s.ev("dve", lambda v, n=n, c0=c0: v.tensor_copy(
            sbc[:, c0:c0 + n], pf[3][:, 0:n]), deps=[e_bs],
            reads=[res(("pf", 3))], writes=[res("sbc")], tag=f"bsc{cg}")
        e_nrm_all += [e_vm, e_vs]
    e_ln1 = []
    for k in range(NK):
        sl = k % 2
        e_t = s.ev("dve", lambda v, k=k, sl=sl: v.tensor_sub(
            uring[sl][:], u_bf[:, k * TE:(k + 1) * TE], mbc[:]),
            deps=e_nrm_all + [e_ucast[k]], writes=[res(("uring", sl))],
            reads=[res("mbc")], tag=f"nsub{k}")
        e_n = s.ev("dve", lambda v, k=k, sl=sl: v.tensor_mul(
            ln1[:, k * TE:(k + 1) * TE], uring[sl][:], sbc[:]),
            deps=[e_t], reads=[res(("uring", sl)), res("sbc")],
            writes=[res(("ln1", k))], tag=f"nmul{k}")
        e_ln1.append(e_n)
    if debug >= 2:
        dbg_outs["ln1"] = None

    s.fence()

    e_wa = s.ev("pool", lambda g: g.dma_start(out=wa_sb[:], in_=d_wa),
                deps=[e_smld], sem="c_dma", incv=16, tag="wald")
    e_apsum = []
    for cg in range(2):
        c0, c1 = 3 + cg * 512, 3 + (cg + 1) * 512
        last = None
        for k in range(NK):
            last = s.ev("pe", lambda t, k=k, c0=c0, c1=c1, st=(k == 0), sp=(k == NK - 1), cg=cg: t.matmul(
                pf[4][0:32, 0:512],
                wa_sb[:, k * 32:(k + 1) * 32],
                ln1[:, k * TE + c0:k * TE + c1], start=st, stop=sp),
                deps=[e_ln1[k], e_wa],
                writes=([res(("pf", 4))] if k == 0 else []), tag=f"Amm{cg}_{k}")
        e_ev = s.ev("dve", lambda v, cg=cg: v.tensor_copy(
            a_sb[:, cg * 512:(cg + 1) * 512], pf[4][0:32, 0:512]),
            deps=[last], reads=[res(("pf", 4))], writes=[res("araw")],
            tag=f"Aev{cg}")
        e_apsum.append(e_ev)
    e_abs = s.ev("act", lambda a: a.activation(
        zeros32[:], a_sb[:], AF.Abs), deps=e_apsum,
        reads=[res("araw")], writes=[res("z32")], tag="abs")
    e_exn = s.ev("act", lambda a: a.activation(
        zeros32[:], zeros32[:], AF.Exp, scale=-1.0), deps=[e_abs],
        writes=[res("z32")], tag="exn")
    e_lnp = s.ev("act", lambda a: a.activation(
        zeros32[:], zeros32[:], AF.Ln, bias=1.0), deps=[e_exn],
        writes=[res("z32")], tag="lnp")
    e_rel = s.ev("dve", lambda v: v.tensor_single_scalar(
        g_sb[:], a_sb[:], 0.0, ALU.max), deps=e_apsum, reads=[res("araw")],
        writes=[res("g32")], tag="rel")
    e_a = s.ev("dve", lambda v: v.scalar_tensor_tensor(
        a_sb[:], g_sb[:], -1.0, zeros32[:], op0=ALU.mult, op1=ALU.subtract),
        deps=[e_lnp, e_rel], reads=[res("z32"), res("g32")],
        writes=[res("araw")], tag="mka")
    e_z32 = s.ev("pool", lambda g: g.memset(zeros32[:], 0.0), deps=[e_a],
                 writes=[res("z32")], tag="z32")
    e_scan = s.ev("dve", lambda v: v.tensor_tensor_scan(
        g_sb[:], a_sb[:], zeros32[:], 0.0, ALU.add, ALU.add),
        deps=[e_z32, e_a], reads=[res("araw"), res("z32")],
        writes=[res("g32")], tag="scan")
    e_gt = []
    for gg in range(8):
        e_tr = s.ev("pe", lambda t, gg=gg: t.transpose(
            pf[5][:, 0:32],
            g_sb[:, gg * 128:(gg + 1) * 128], identf[0:32, 0:32]),
            deps=[e_scan, e_idf], writes=[res(("pf", 5))],
            tag=f"gtr{gg}")
        e_gv = s.ev("dve", lambda v, gg=gg: v.tensor_copy(
            gt_sb[:, gg * 32:(gg + 1) * 32], pf[5][:, 0:32]),
            deps=[e_tr], reads=[res(("pf", 5))], writes=[res("gt")],
            tag=f"gte{gg}")
        e_gt.append(e_gv)
    e_gm = s.ev("sp", lambda sp: sp.dma_start(
        out=gmrow[:], in_=gt_sb[63:64, :]),
        deps=e_gt, sem="ge_dma", incv=16, tag="gm")
    e_ge = s.ev("sp", lambda sp: sp.dma_start(
        out=gerow[:], in_=gt_sb[127:128, :]),
        deps=[e_gm], sem="ge_dma", incv=16, tag="ge")
    gmn_r = res("gmnrow")
    e_gn0 = s.ev("dve", lambda v: v.tensor_copy(
        gmnrow[:, 0:224], gmrow[:, 32:256]), deps=[e_gm, e_ge],
        writes=[gmn_r], tag="gn0")
    e_gn1 = s.ev("dve", lambda v: v.tensor_copy(
        gmnrow[:, 224:256], gerow[:, 224:256]), deps=[e_ge],
        writes=[gmn_r], tag="gn1")
    ar = res("arow")
    e_ar0 = s.ev("dve", lambda v: v.tensor_copy(arow[:, 0:32], gmnrow[:, 0:32]),
                 deps=[e_gn0, e_gn1], reads=[gmn_r], writes=[ar], tag="ar0")
    e_ar1 = s.ev("dve", lambda v: v.tensor_sub(
        arow[:, 32:256], gmnrow[:, 32:256], gmnrow[:, 0:224]),
        deps=[e_ar0], reads=[gmn_r], writes=[ar], tag="ar1")
    e_expa = s.ev("act", lambda a: a.activation(arow[:], arow[:], AF.Exp),
                  deps=[e_ar1], writes=[ar], tag="expa")
    e_eab = s.ev("pe", lambda t: t.matmul(
        pf[5][:, 0:256], ones1f[:], arow[:], start=True, stop=True),
        deps=[e_expa], writes=[res(("pf", 5))], tag="eabm")
    e_eabv = s.ev("dve", lambda v: v.tensor_copy(expabc[:], pf[5][:, 0:256]),
                  deps=[e_eab], reads=[res(("pf", 5))], writes=[res("expabc")],
                  tag="eabv")
    e_zdone = []
    nslab_z = NJZ // 2
    slab_consumed = [None, None]
    slab_loaded = [None, None]
    for sidx in range(nslab_z):
        buf = sidx % 2
        e_ld = s.ev("pool", lambda g, sidx=sidx, buf=buf: g.dma_start(
            out=wslab[buf][:], in_=d_wz[:, sidx * 2 * NK * 128:(sidx + 1) * 2 * NK * 128]),
            deps=[slab_consumed[buf]], sem=f"ws{buf}", incv=16,
            tag=f"zsl{sidx}")
        slab_loaded[buf] = e_ld
        for jloc in range(2):
            jz = sidx * 2 + jloc
            for cg in range(2):
                c0 = 3 + cg * 512
                bank = (jz * 2 + cg) % 2
                last = None
                for k in range(NK):
                    last = s.ev("pe", lambda t, jloc=jloc, k=k, c0=c0, bank=bank, buf=buf, st=(k == 0), sp=(k == NK - 1): t.matmul(
                        pf[bank][:, 0:512],
                        wslab[buf][:, (jloc * NK + k) * 128:(jloc * NK + k + 1) * 128],
                        ln1[:, k * TE + c0:k * TE + c0 + 512],
                        start=st, stop=sp),
                        deps=[e_ld, e_ln1[k]],
                        writes=([res(("pf", bank))] if k == 0 else []),
                        tag=f"zmm{jz}_{cg}_{k}")
                slab_consumed[buf] = last
                e_ev = s.ev("act", lambda a, jz=jz, cg=cg, bank=bank: a.activation(
                    z_sb[:, jz * T + cg * 512:jz * T + (cg + 1) * 512],
                    pf[bank][:, 0:512], AF.Silu, bias=zb_sb[:, jz:jz + 1]),
                    deps=[last, e_zbld], reads=[res(("pf", bank))],
                    writes=[res(("z", jz))], tag=f"zev{jz}_{cg}")
                e_zdone.append(e_ev)

    for sidx in range(NJX // 2):
        buf = sidx % 2
        e_ld = s.ev("pool", lambda g, sidx=sidx, buf=buf: g.dma_start(
            out=wslab[buf][:], in_=d_wxbc[:, sidx * 2 * NK * 128:(sidx + 1) * 2 * NK * 128]),
            deps=[slab_consumed[buf]], sem=f"ws{buf}", incv=16,
            tag=f"xsl{sidx}")
        slab_loaded[buf] = e_ld
        for jloc in range(2):
            jc = sidx * 2 + jloc
            pslot = jc % 2
            e_evs = []
            for cg, (c0, c1) in enumerate(CGS):
                bank = cg % 2
                n = c1 - c0
                last = None
                for k in range(NK):
                    last = s.ev("pe", lambda t, jloc=jloc, k=k, c0=c0, n=n, bank=bank, buf=buf, st=(k == 0), sp=(k == NK - 1): t.matmul(
                        pf[bank][:, 0:n],
                        wslab[buf][:, (jloc * NK + k) * 128:(jloc * NK + k + 1) * 128],
                        ln1[:, k * TE + c0:k * TE + c0 + n],
                        start=st, stop=sp),
                        deps=[e_ld, e_ln1[k]],
                        writes=([res(("pf", bank))] if k == 0 else []),
                        tag=f"xmm{jc}_{cg}_{k}")
                slab_consumed[buf] = last
                e_ev = s.ev("act", lambda a, c0=c0, n=n, bank=bank, pslot=pslot: a.activation(
                    xpre[pslot][:, c0:c0 + n], pf[bank][:, 0:n], AF.Copy),
                    deps=[last], reads=[res(("pf", bank))],
                    writes=[res(("xpre", pslot))], tag=f"xev{jc}_{cg}")
                e_evs.append(e_ev)
            tgt = [x_cm, b_cm, c_cm][jc // 16]
            jj = jc % 16
            e_c1 = s.ev("dve", lambda g, jc=jc, pslot=pslot: g.tensor_scalar(
                ctmp[pslot][:], xpre[pslot][:, 0:T], cw_sb[:, jc * 4:jc * 4 + 1],
                cb_sb[:, jc:jc + 1], ALU.mult, ALU.add),
                deps=e_evs + [e_cw, e_cb], reads=[res(("xpre", pslot))],
                writes=[res(("ctmp", pslot))], tag=f"cv0_{jc}")
            e_c2 = s.ev("dve", lambda g, jc=jc, pslot=pslot: g.scalar_tensor_tensor(
                ctmp[pslot][:], xpre[pslot][:, 1:T + 1],
                cw_sb[:, jc * 4 + 1:jc * 4 + 2], ctmp[pslot][:],
                op0=ALU.mult, op1=ALU.add), deps=[e_c1],
                reads=[res(("xpre", pslot))], writes=[res(("ctmp", pslot))],
                tag=f"cv1_{jc}")
            e_c3 = s.ev("dve", lambda g, jc=jc, pslot=pslot: g.scalar_tensor_tensor(
                ctmp[pslot][:], xpre[pslot][:, 2:T + 2],
                cw_sb[:, jc * 4 + 2:jc * 4 + 3], ctmp[pslot][:],
                op0=ALU.mult, op1=ALU.add), deps=[e_c2],
                reads=[res(("xpre", pslot))], writes=[res(("ctmp", pslot))],
                tag=f"cv2_{jc}")
            e_c4 = s.ev("dve", lambda g, jc=jc, pslot=pslot, tgt=tgt, jj=jj: g.scalar_tensor_tensor(
                tgt[:, jj * T:(jj + 1) * T], xpre[pslot][:, 3:T + 3],
                cw_sb[:, jc * 4 + 3:jc * 4 + 4], ctmp[pslot][:],
                op0=ALU.mult, op1=ALU.add), deps=[e_c3],
                reads=[res(("xpre", pslot)), res(("ctmp", pslot))],
                writes=[res(("cm", jc))], tag=f"cv3_{jc}")

    e_dbn = s.ev("act", lambda a: a.activation(dbnd[:], gt_sb[:], AF.Exp),
                 deps=e_gt + [slab_consumed[0], slab_consumed[1]],
                 reads=[res("gt")], writes=[res("dbnd")], tag="dbnd")
    e_decay = [e_dbn]
    ecps = [res(("ecp", c)) for c in range(8)]
    e_b = s.ev("pe", lambda t: t.matmul(
        pf[5][:, 0:256], ones1f[:], gmrow[:, 0:256],
        start=True, stop=True), deps=[e_gm, e_ones1f],
        writes=[res(("pf", 5))], tag="gb")
    e_dp = s.ev("dve", lambda v: v.tensor_sub(
        ecol[:, 0:256], gt_sb[:, 0:256], pf[5][:, 0:256]),
        deps=[e_b], reads=[res(("pf", 5)), res("gt")],
        writes=ecps, tag="dp")
    e_di = s.ev("act", lambda a: a.activation(
        din[:, 0:256], ecol[:, 0:256], AF.Exp),
        deps=[e_dp], reads=ecps, writes=[res("din")], tag="di")
    e_rc = s.ev("act", lambda a: a.activation(
        rcol[:, 0:256], ecol[:, 0:256], AF.Exp, scale=-1.0),
        deps=[e_dp], reads=ecps, writes=[res("rcol")], tag="rc")
    e_b2 = s.ev("pe", lambda t: t.matmul(
        pf[4][:, 0:256], ones1f[:], gmnrow[:, 0:256],
        start=True, stop=True), deps=[e_gn0, e_gn1, e_b],
        reads=[gmn_r], writes=[res(("pf", 4))], tag="ge")
    e_ep = s.ev("dve", lambda v: v.tensor_sub(
        ecol[:, 0:256], pf[4][:, 0:256], gt_sb[:, 0:256]),
        deps=[e_b2, e_di, e_rc], reads=[res(("pf", 4)), res("gt")],
        writes=ecps, tag="ep")
    e_ec = s.ev("act", lambda a: a.activation(
        ecol[:, 0:256], ecol[:, 0:256], AF.Exp),
        deps=[e_ep], writes=ecps, tag="ec")
    for c in range(8):
        res(("ecp", c)).w = e_ec
    e_decay += [e_di, e_rc, e_ec]

    e_szb = s.ev("pool", lambda g_: g_.memset(s_bf[:], 0.0),
                 deps=[e_decay[-1], e_eabv, e_scan] + e_apsum, tag="szb")
    for h in range(NH):
        res(("sbf", h)).w = e_szb

    wos_consumed = [slab_consumed[0], slab_consumed[0]]
    e_wosld = {}
    e_x0ld = {}
    for jf in range(2):
        buf = jf % 2
        e_wosld[jf] = s.ev("pool", lambda g_, jf=jf, buf=buf: g_.dma_start(
            out=wos[buf][:],
            in_=d_wout[:, jf * NK * 128:(jf + 1) * NK * 128]),
            deps=[wos_consumed[buf]], sem=f"ws{buf}", incv=16,
            tag=f"osl{jf}")

    s.fence()
    if phases <= 2:
        return locals()

    e_supd_last = {}
    bank_rd = {i: [] for i in range(8)}
    bank_wr = {i: None for i in range(8)}

    def epoch_dep(bank):
        d = bank_rd[bank] + ([bank_wr[bank]] if bank_wr[bank] else [])
        bank_rd[bank] = []
        return d

    pending_su = []

    def emit_pending_su():
        for fn in pending_su:
            fn()
        del pending_su[:]

    for c in range(NG):
        e_xtok_cp = {}
        for hp in range(16):
            pb = hp % 2
            e_t = s.ev("pe", lambda t, hp=hp, c=c, pb=pb: t.transpose(
                pbt[pb][:], x_cm[:, hp * T + c * 128:hp * T + c * 128 + 128],
                identb[:]), deps=[e_idb] + epoch_dep(6 + pb),
                reads=[res(("cm", hp))], tag=f"trx{c}_{hp}")
            e_cp = s.ev("act", lambda a, hp=hp, pb=pb: a.activation(
                x_tok[:, hp * 128:(hp + 1) * 128], pbt[pb][:], AF.Copy),
                deps=[e_t], writes=[res(("xtok", c, hp))], tag=f"trxe{c}_{hp}")
            bank_rd[6 + pb].append(e_cp)
            e_xtok_cp[hp] = e_cp
            e_tb = s.ev("pe", lambda t, hp=hp, c=c, pb=pb: t.transpose(
                pbt[pb][:], b_cm[:, hp * T + c * 128:hp * T + c * 128 + 128],
                identb[:]), deps=[e_idb] + epoch_dep(6 + pb),
                reads=[res(("cm", 16 + hp))], tag=f"trb{c}_{hp}")
            e_cb2 = s.ev("dve", lambda v, hp=hp, pb=pb: v.tensor_copy(
                b_tok[:, hp * 128:(hp + 1) * 128], pbt[pb][:]),
                deps=[e_tb], writes=[res(("btok", c, hp))], tag=f"trbe{c}_{hp}")
            bank_rd[6 + pb].append(e_cb2)
        for h in range(NH):
            hp, hh = h // 2, h % 2
            e_xd = s.ev("act", lambda a, h=h, hp=hp, hh=hh, c=c: a.activation(
                xd[:, hp * 128 + hh * 64:hp * 128 + (hh + 1) * 64],
                x_tok[:, hp * 128 + hh * 64:hp * 128 + (hh + 1) * 64],
                AF.Copy, scale=ecol[:, c * 32 + h:c * 32 + h + 1]),
                deps=[e_xtok_cp[hp], R[("ecp", c)].w],
                reads=[res(("xtok", c, hp))],
                writes=[res(("xd", c, h))], tag=f"xd{c}_{h}")
        emit_pending_su()
        for half in range(2):
            hs = list(range(half * 16, half * 16 + 16))
            e_m1 = {}
            e_mte = {}
            e_y2 = {}
            e_s1 = {}
            LAG = 4
            mtbank = [pf[0], pf[1]]
            for i in range(16 + LAG):
                if i < 16:
                    h = hs[i]
                    hh = h % 2
                    bk = h % 2
                    mts = h % 16
                    ccol = (h // 2) * T + c * 128
                    e_m1[h] = s.ev("pe", lambda t, h=h, hh=hh, bk=bk, ccol=ccol: t.matmul(
                        mtbank[bk][:, 0:128],
                        b_cm[hh * 64:(hh + 1) * 64, ccol:ccol + 128],
                        c_cm[hh * 64:(hh + 1) * 64, ccol:ccol + 128],
                        start=True, stop=True),
                        deps=epoch_dep(bk) +
                        [R[("cm", 16 + h // 2)].w, R[("cm", 32 + h // 2)].w],
                        tag=f"m1_{c}_{h}")
                    e_mte[h] = s.ev("dve", lambda v, h=h, c=c, bk=bk, mts=mts: v.scalar_tensor_tensor(
                        mtr[mts][:], mtbank[bk][:, 0:128],
                        rcol[:, c * 32 + h:c * 32 + h + 1], u01[:],
                        op0=ALU.mult, op1=ALU.mult),
                        deps=[e_m1[h], e_u01, R["rcol"].w],
                        writes=[res(("mtr", mts))], tag=f"mte{c}_{h}")
                    bank_rd[bk].append(e_mte[h])
                if i >= LAG:
                    h = hs[i - LAG]
                    hp, hh = h // 2, h % 2
                    yb = 2 + h % 2
                    sbk = 4 + h % 2
                    qc = ((h // 2) % 8) * 64
                    mts = h % 16
                    ccol = (h // 2) * T + c * 128
                    ddy = epoch_dep(yb) if (h - half * 16) < 2 else []
                    dds = epoch_dep(sbk) if (h - half * 16) < 2 else []
                    e_ya = s.ev("pe", lambda t, h=h, hp=hp, hh=hh, yb=yb, qc=qc, mts=mts: t.matmul(
                        pf[yb][:, qc:qc + 64], mtr[mts][:],
                        x_tok[:, hp * 128 + hh * 64:hp * 128 + (hh + 1) * 64],
                        start=True, stop=False),
                        deps=ddy + [e_mte[h], e_xtok_cp[hp]],
                        reads=[res(("mtr", mts))], tag=f"y1_{c}_{h}")
                    e_y2[h] = s.ev("pe", lambda t, h=h, hh=hh, yb=yb, qc=qc, ccol=ccol: t.matmul(
                        pf[yb][:, qc:qc + 64],
                        c_cm[hh * 64:(hh + 1) * 64, ccol:ccol + 128],
                        s_bf[hh * 64:(hh + 1) * 64, (h // 2) * 64:(h // 2 + 1) * 64],
                        start=False, stop=True),
                        deps=[e_ya], reads=[res(("sbf", h))],
                        tag=f"y2_{c}_{h}")
                    e_s1[h] = s.ev("pe", lambda t, h=h, hp=hp, hh=hh, sbk=sbk, qc=qc: t.matmul(
                        pf[sbk][hh * 64:(hh + 1) * 64, qc:qc + 64],
                        b_tok[:, hp * 128 + hh * 64:hp * 128 + (hh + 1) * 64],
                        xd[:, hp * 128 + hh * 64:hp * 128 + (hh + 1) * 64],
                        start=True, stop=True),
                        deps=dds + [R[("btok", c, hp)].w, R[("xd", c, h)].w],
                        tag=f"s1_{c}_{h}")
            last_y = {2: e_y2[half * 16 + 14], 3: e_y2[half * 16 + 15]}
            last_s = {4: e_s1[half * 16 + 14], 5: e_s1[half * 16 + 15]}
            for h in hs:
                yb = 2 + h % 2
                sbk = 4 + h % 2
                hh = h % 2
                qc = ((h // 2) % 8) * 64
                e_ye = s.ev("act", lambda a, h=h, c=c, yb=yb, qc=qc: a.activation(
                    y_tok[:, c * D + h * 64:c * D + (h + 1) * 64],
                    pf[yb][:, qc:qc + 64], AF.Copy,
                    scale=din[:, c * 32 + h:c * 32 + h + 1]),
                    deps=[last_y[yb], R["din"].w],
                    writes=[res(("ytk", c, h))], tag=f"ye{c}_{h}")
                bank_rd[yb].append(e_ye)
                def mk_su(h=h, hh=hh, c=c, sbk=sbk, qc=qc,
                          ls=last_s[sbk], ey2=e_y2[h]):
                    e_su = s.ev("dve", lambda v: v.scalar_tensor_tensor(
                        s_bf[hh * 64:(hh + 1) * 64, (h // 2) * 64:(h // 2 + 1) * 64],
                        s_bf[hh * 64:(hh + 1) * 64, (h // 2) * 64:(h // 2 + 1) * 64],
                        expabc[hh * 64:(hh + 1) * 64, c * 32 + h:c * 32 + h + 1],
                        pf[sbk][hh * 64:(hh + 1) * 64, qc:qc + 64],
                        op0=ALU.mult, op1=ALU.add),
                        deps=[ls, e_eabv, ey2],
                        writes=[res(("sbf", h))], tag=f"su{c}_{h}")
                    bank_rd[sbk].append(e_su)
                    e_supd_last[h] = e_su
                if half == 0:
                    mk_su()
                else:
                    pending_su.append(mk_su)
    emit_pending_su()
    if phases <= 3:
        return locals()
    def emit_p4b(c):
        for hp in range(16):
            pb = hp % 2
            yfs = hp % 2
            e_t = s.ev("pe", lambda t, c=c, hp=hp, pb=pb: t.transpose(
                pbt[pb][:], y_tok[:, c * D + hp * 128:c * D + (hp + 1) * 128],
                identb[:]),
                deps=[R[("ytk", c, 2 * hp)].w, R[("ytk", c, 2 * hp + 1)].w] +
                     epoch_dep(6 + pb),
                tag=f"ytr{c}_{hp}")
            e_o1 = s.ev("dve", lambda v, c=c, hp=hp, pb=pb, yfs=yfs: v.scalar_tensor_tensor(
                yfr[yfs][:], x_cm[:, hp * T + c * 128:hp * T + c * 128 + 128],
                drep_sb[:, hp:hp + 1], pbt[pb][:],
                op0=ALU.mult, op1=ALU.add),
                deps=[e_t, e_drepld], reads=[res(("cm", hp))],
                writes=[res(("yfr", yfs))], tag=f"yo1{c}_{hp}")
            bank_rd[6 + pb].append(e_o1)
            e_o2 = s.ev("dve", lambda v, c=c, hp=hp, yfs=yfs: v.tensor_mul(
                z_sb[:, hp * T + c * 128:hp * T + c * 128 + 128],
                yfr[yfs][:],
                z_sb[:, hp * T + c * 128:hp * T + c * 128 + 128]),
                deps=[e_o1], reads=[res(("yfr", yfs))],
                writes=[res(("z", hp))], tag=f"yo2{c}_{hp}")
    e_cc_parts = []
    for half in range(2):
        col0 = half * 512
        e_ccw = s.ev("sp", lambda sp, half=half, col0=col0: sp.dma_start(
            out=d_ccin[half], in_=s_bf[:, col0:col0 + 512]),
            deps=[e_supd_last[h] for h in range(half * 16, half * 16 + 16)],
            sem="ge_dma" if half == 0 else "ccw1", incv=16, tag=f"ccw{half}")
        e_cc = s.ev("pool", lambda g_, half=half: g_.collective_compute(
            "AllGather", ALU.bypass, [[0, 1], [2, 3], [4, 5], [6, 7]],
            [d_ccin[half].opt()], [d_ccout[half].opt()]),
            deps=[e_ccw], sem="cc" if half == 0 else "cc1", incv=1,
            tag=f"cc{half}")
        e_sld = s.ev("pool", lambda g_, half=half, col0=col0: g_.dma_start(
            out=s_inb[:, col0:col0 + 512], in_=d_ccout[half][0:128, :]),
            deps=[e_cc, slab_consumed[1]],
            sem="ccd" if half == 0 else "ccd1", incv=16, tag=f"sld{half}")
        e_smul = s.ev("act", lambda a, col0=col0: a.activation(
            s_inb[:, col0:col0 + 512], s_inb[:, col0:col0 + 512],
            AF.Copy, scale=smask_sb[:]),
            deps=[e_sld, e_smld], writes=[res(("sinb", half))],
            tag=f"smul{half}")
        e_cc_parts.append(e_smul)
    for half in range(2):
        for c in range(NG):
            e_b = {}
            hs = list(range(half * 16, half * 16 + 16))
            for h in hs:
                hh = h % 2
                yb = 2 * half + (h % 2)
                qc = ((h % 16) // 2) * 64
                ccol = (h // 2) * T + c * 128
                dd = epoch_dep(yb) if (h % 16) < 2 else []
                e_b[h] = s.ev("pe", lambda t, h=h, hh=hh, yb=yb, qc=qc, ccol=ccol: t.matmul(
                    pf[yb][:, qc:qc + 64],
                    c_cm[hh * 64:(hh + 1) * 64, ccol:ccol + 128],
                    s_inb[hh * 64:(hh + 1) * 64, (h // 2) * 64:(h // 2 + 1) * 64],
                    start=True, stop=True),
                    deps=dd + [e_cc_parts[half], R[("cm", 32 + h // 2)].w],
                    tag=f"bnd{c}_{h}")
            last_b = {2 * half: e_b[half * 16 + 14],
                      2 * half + 1: e_b[half * 16 + 15]}
            for h in hs:
                yb = 2 * half + (h % 2)
                qc = ((h % 16) // 2) * 64
                if h % 2 == 0:
                    e_ba = s.ev("dve", lambda v, h=h, c=c, yb=yb, qc=qc: v.scalar_tensor_tensor(
                        y_tok[:, c * D + h * 64:c * D + (h + 1) * 64],
                        pf[yb][:, qc:qc + 64],
                        dbnd[:, c * 32 + h:c * 32 + h + 1],
                        y_tok[:, c * D + h * 64:c * D + (h + 1) * 64],
                        op0=ALU.mult, op1=ALU.add),
                        deps=[last_b[yb], R["dbnd"].w],
                        writes=[res(("ytk", c, h))], tag=f"ba{c}_{h}")
                    bank_rd[yb].append(e_ba)
                else:
                    ybs = (h // 2) % 4
                    e_be = s.ev("act", lambda a, h=h, c=c, yb=yb, qc=qc, ybs=ybs: a.activation(
                        ybr[ybs][:], pf[yb][:, qc:qc + 64], AF.Copy,
                        scale=dbnd[:, c * 32 + h:c * 32 + h + 1]),
                        deps=[last_b[yb], R["dbnd"].w],
                        writes=[res(("ybr", ybs))], tag=f"be{c}_{h}")
                    bank_rd[yb].append(e_be)
                    e_ba = s.ev("dve", lambda v, h=h, c=c, ybs=ybs: v.tensor_add(
                        y_tok[:, c * D + h * 64:c * D + (h + 1) * 64],
                        y_tok[:, c * D + h * 64:c * D + (h + 1) * 64],
                        ybr[ybs][:]),
                        deps=[e_be], reads=[res(("ybr", ybs))],
                        writes=[res(("ytk", c, h))], tag=f"ba{c}_{h}")
            if half == 1 and c >= 1:
                emit_p4b(c - 1)

    emit_p4b(NG - 1)
    s.fence()
    if phases <= 35:
        return locals()

    s.fence()
    if phases <= 4:
        return locals()

    e_x1 = {}
    for jf in range(NK):
        buf = jf % 2
        sl = jf % 2
        if jf in e_wosld:
            e_ld = e_wosld[jf]
        else:
            e_ld = s.ev("pool", lambda g_, jf=jf, buf=buf: g_.dma_start(
                out=wos[buf][:],
                in_=d_wout[:, jf * NK * 128:(jf + 1) * NK * 128]),
                deps=[wos_consumed[buf]], sem=f"ws{buf}", incv=16,
                tag=f"osl{jf}")
        e_x0 = s.ev("sp", lambda sp, jf=jf, sl=sl: sp.dma_start(
            out=x0r[sl][:], in_=d_u[jf * 128:(jf + 1) * 128, 3:TE]),
            writes=[res(("x0r", sl))], sem=f"x0{sl}", incv=16,
            tag=f"x0{jf}")
        for th in range(2):
            bank = (jf * 2 + th) % 2
            last = None
            for k in range(NK):
                last = s.ev("pe", lambda t, k=k, th=th, bank=bank, buf=buf, st=(k == 0), sp_=(k == NK - 1): t.matmul(
                    pf[bank][:, 0:512],
                    wos[buf][:, k * 128:(k + 1) * 128],
                    z_sb[:, k * T + th * 512:k * T + (th + 1) * 512],
                    start=st, stop=sp_),
                    deps=[e_ld], reads=[res(("z", k))],
                    writes=([res(("pf", bank))] if k == 0 else []),
                    tag=f"wo{jf}_{th}_{k}")
            wos_consumed[buf] = last
            e_e = s.ev("dve", lambda v, jf=jf, th=th, sl=sl, bank=bank: v.tensor_add(
                x1[:, jf * T + th * 512:jf * T + (th + 1) * 512],
                x0r[sl][:, th * 512:(th + 1) * 512], pf[bank][:, 0:512]),
                deps=[last, e_x0], reads=[res(("pf", bank)), res(("x0r", sl))],
                writes=[res(("x1", jf))], tag=f"x1e{jf}_{th}")
            e_x1[(jf, th)] = e_e
    if phases <= 5:
        return locals()
    e_x1c = []
    for k in range(NK):
        e_c = s.ev("dve", lambda v, k=k: v.tensor_copy(
            x1bf[:, k * T:(k + 1) * T], x1[:, k * T:(k + 1) * T]),
            deps=[e_x1[(k, 0)], e_x1[(k, 1)]], reads=[res(("x1", k))],
            writes=[res(("x1bf", k))], tag=f"x1c{k}")
        e_x1c.append(e_c)
    CG2 = [(0, 512), (512, 1024)]
    e_bc2 = []
    x1bf_readers = {k: [] for k in range(NK)}
    for cg, (c0, c1) in enumerate(CG2):
        n = c1 - c0
        lastm = lasts_ = None
        for k in range(NK):
            sl = k % 2
            e_sq = s.ev("act", lambda a, k=k, c0=c0, n=n, sl=sl: a.activation(
                sq2[sl][:, 0:n], x1bf[:, k * T + c0:k * T + c0 + n],
                AF.Square), deps=[e_x1c[k]], writes=[res(("sq2", sl))],
                tag=f"q2{cg}_{k}")
            x1bf_readers[k].append(e_sq)
            lastm = s.ev("pe", lambda t, k=k, c0=c0, n=n, st=(k == 0), sp_=(k == NK - 1): t.matmul(
                pf[2][0:1, 0:n], onescol[:], x1bf[:, k * T + c0:k * T + c0 + n],
                start=st, stop=sp_), deps=[e_x1c[k], e_onescol],
                writes=([res(("pf", 2))] if k == 0 else []), tag=f"m2m{cg}_{k}")
            x1bf_readers[k].append(lastm)
            lasts_ = s.ev("pe", lambda t, k=k, n=n, sl=sl, st=(k == 0), sp_=(k == NK - 1): t.matmul(
                pf[3][0:1, 0:n], onescol[:], sq2[sl][:, 0:n],
                start=st, stop=sp_), deps=[e_sq], reads=[res(("sq2", sl))],
                writes=([res(("pf", 3))] if k == 0 else []), tag=f"s2m{cg}_{k}")
        e_em2 = s.ev("dve", lambda v, n=n: v.tensor_scalar_mul(
            m512[:, 0:n], pf[2][0:1, 0:n], INV_D), deps=[lastm],
            reads=[res(("pf", 2))], writes=[res("m2f")], tag=f"em2{cg}")
        e_es2 = s.ev("dve", lambda v, n=n: v.tensor_scalar_mul(
            s2row[:, 0:n], pf[3][0:1, 0:n], INV_D), deps=[lasts_],
            reads=[res(("pf", 3))], writes=[res("s2row")], tag=f"es2{cg}")
        e_mq2 = s.ev("act", lambda a, n=n: a.activation(
            scr2[:, 0:n], m512[:, 0:n], AF.Square), deps=[e_em2],
            reads=[res("m2f")], writes=[res("scr2")], tag=f"mq2{cg}")
        e_v2 = s.ev("dve", lambda v, n=n: v.tensor_sub(
            s2row[:, 0:n], s2row[:, 0:n], scr2[:, 0:n]), deps=[e_es2, e_mq2],
            writes=[res("s2row")], tag=f"v2{cg}")
        e_l2 = s.ev("act", lambda a, n=n: a.activation(
            scr2[:, 0:n], s2row[:, 0:n], AF.Ln, bias=c_eps[:]),
            deps=[e_v2, e_ceps, e_mq2], reads=[res("s2row")],
            writes=[res("scr2")], tag=f"l2{cg}")
        e_r2 = s.ev("act", lambda a, n=n: a.activation(
            s2row[:, 0:n], scr2[:, 0:n], AF.Exp, scale=c_nhalf[:]),
            deps=[e_l2, e_cnh], reads=[res("scr2")],
            writes=[res("s2row")], tag=f"r2{cg}")
        e_si = s.ev("act", lambda a, n=n: a.activation(
            scr2[:, 0:n], scr2[:, 0:n], AF.Exp, scale=c_phalf[:]),
            deps=[e_r2, e_cph], writes=[res("scr2")], tag=f"si{cg}")
        e_m2c = s.ev("act", lambda a, n=n, c0=c0: a.activation(
            m2row[0:1, c0:c0 + n], m512[:, 0:n], AF.Copy), deps=[e_em2],
            reads=[res("m2f")], writes=[res("m2row")], tag=f"m2c{cg}")
        e_m2ff = s.ev("dve", lambda v, n=n, c0=c0: v.tensor_copy(
            m2f[0:1, c0:c0 + n], m512[:, 0:n]), deps=[e_em2],
            reads=[res("m2f")], writes=[res("m2ffull")], tag=f"m2ff{cg}")
        for (dstrow, dstbc, rres, dep) in [
                (m512[:, 0:n], mbc2, "m2f", e_em2),
                (s2row[:, 0:n], sbc2, "s2row", e_r2),
                (scr2[:, 0:n], sigbc, "scr2", e_si)]:
            e_b = s.ev("pe", lambda t, dstrow=dstrow, n=n: t.matmul(
                pf[4][:, 0:n], ones1f[:], dstrow, start=True, stop=True),
                deps=[dep, e_ones1f], reads=[res(rres)],
                writes=[res(("pf", 4))], tag=f"bc2{cg}")
            e_v = s.ev("dve", lambda v, dstbc=dstbc, c0=c0, n=n: v.tensor_copy(
                dstbc[:, c0:c0 + n], pf[4][:, 0:n]), deps=[e_b],
                reads=[res(("pf", 4))], writes=[res(id(dstbc))], tag=f"bv2{cg}")
            e_bc2.append(e_v)
    lastrs = None
    for k in range(NK):
        lastrs = s.ev("pe", lambda t, k=k, st=(k == 0), sp_=(k == NK - 1): t.matmul(
            pf[3][0:8, 0:1], wrf_sb[:, k * 8:(k + 1) * 8], onescolf[:],
            start=st, stop=sp_), deps=[e_wrf, e_onescolf],
            writes=([res(("pf", 3))] if k == 0 else []), tag=f"rs{k}")
    e_rsv = s.ev("dve", lambda v: v.tensor_copy(rs_col[:], pf[3][0:8, 0:1]),
                 deps=[lastrs], reads=[res(("pf", 3))], writes=[res("rscol")],
                 tag="rsv")
    e_rst = s.ev("pe", lambda t: t.transpose(
        pf[3][0:1, 0:8], rs_col[:], identf[0:8, 0:8]), deps=[e_rsv, e_idf],
        reads=[res("rscol")], writes=[res(("pf", 3))], tag="rst")
    e_nrs = s.ev("dve", lambda v: v.tensor_scalar_mul(
        nrs_row[:], pf[3][0:1, 0:8], -1.0), deps=[e_rst],
        reads=[res(("pf", 3))], writes=[res("nrs")], tag="nrs")
    e_ln2 = []
    for k in range(NK):
        sl = k % 2
        e_t = s.ev("dve", lambda v, k=k, sl=sl: v.tensor_sub(
            x0r[sl][:], x1[:, k * T:(k + 1) * T], mbc2[:]),
            deps=e_bc2 + [e_x1[(k, 0)], e_x1[(k, 1)]],
            reads=[res(id(mbc2)), res(("x1", k))],
            writes=[res(("x0r", sl))], tag=f"n2s{k}")
        e_n = s.ev("dve", lambda v, k=k, sl=sl: v.tensor_mul(
            ln2[:, k * T:(k + 1) * T], x0r[sl][:], sbc2[:]),
            deps=[e_t], reads=[res(("x0r", sl)), res(id(sbc2))],
            writes=[res(("ln2", k))], tag=f"n2m{k}")
        e_ln2.append(e_n)
    e_sc_ev = []
    for cg in range(2):
        c0 = cg * 512
        last = None
        for k in range(NK):
            last = s.ev("pe", lambda t, k=k, c0=c0, st=(k == 0): t.matmul(
                pf[2][0:8, 0:512], wrf_sb[:, k * 8:(k + 1) * 8],
                x1[:, k * T + c0:k * T + c0 + 512], start=st, stop=False),
                deps=[e_wrf, e_x1[(k, 0)], e_x1[(k, 1)]],
                reads=[res(("x1", k))],
                writes=([res(("pf", 2))] if k == 0 else []), tag=f"rt{cg}_{k}")
        last = s.ev("pe", lambda t, c0=c0: t.matmul(
            pf[2][0:8, 0:512], nrs_row[:], m2f[0:1, c0:c0 + 512],
            start=False, stop=True),
            deps=[last, e_nrs, R["m2ffull"].w], reads=[res("nrs"), res("m2ffull")],
            tag=f"rtm{cg}")
        e_ev = s.ev("dve", lambda v, c0=c0: v.tensor_mul(
            scores[:, c0:c0 + 512], pf[2][0:8, 0:512], sbc2[0:8, c0:c0 + 512]),
            deps=[last] + e_bc2, reads=[res(("pf", 2)), res(id(sbc2))],
            writes=[res("scores")], tag=f"rte{cg}")
        e_sc_ev.append(e_ev)
    def emit_top2():
        for gg in range(8):
            e_t = s.ev("pe", lambda t, gg=gg: t.transpose(
                pf[3][:, 0:8], scores[0:8, gg * 128:(gg + 1) * 128],
                identf[0:8, 0:8]), deps=e_sc_ev + [e_idf], reads=[res("scores")],
                writes=[res(("pf", 3))], tag=f"st{gg}")
            e_v = s.ev("dve", lambda v, gg=gg: v.tensor_copy(
                sttok[:, gg * 8:(gg + 1) * 8], pf[3][:, 0:8]), deps=[e_t],
                reads=[res(("pf", 3))], writes=[res("sttok")], tag=f"stv{gg}")
            sl = sttok[:, gg * 8:(gg + 1) * 8]
            e1 = s.ev("dve", lambda v, sl=sl: v.reduce_max(r_m1[:], sl, axis=AX.X),
                      deps=[e_v], writes=[res("rm1")], tag=f"rm1{gg}")
            e2 = s.ev("dve", lambda v: v.tensor_scalar_mul(r_nm1[:], r_m1[:], -1.0),
                      deps=[e1], reads=[res("rm1")], writes=[res("rnm1")],
                      tag=f"rnm{gg}")
            e3 = s.ev("dve", lambda v, sl=sl: v.tensor_single_scalar(
                r_eq1[:], sl, r_m1[:], ALU.is_equal), deps=[e1],
                reads=[res("rm1")], writes=[res("req1")], tag=f"req1{gg}")
            e4 = s.ev("dve", lambda v, sl=sl: v.scalar_tensor_tensor(
                r_sc2[:], r_eq1[:], -1e30, sl, op0=ALU.mult, op1=ALU.add),
                deps=[e3], reads=[res("req1")], writes=[res("rsc2")],
                tag=f"rsc2{gg}")
            e5 = s.ev("dve", lambda v: v.reduce_max(r_m2[:], r_sc2[:], axis=AX.X),
                      deps=[e4], reads=[res("rsc2")], writes=[res("rm2")],
                      tag=f"rm2{gg}")
            e6 = s.ev("dve", lambda v, sl=sl: v.tensor_single_scalar(
                r_eq2[:], sl, r_m2[:], ALU.is_equal), deps=[e5],
                reads=[res("rm2")], writes=[res("req2")], tag=f"req2{gg}")
            e7 = s.ev("act", lambda a: a.activation(
                r_em[:], r_m2[:], AF.Exp, bias=r_nm1[:]), deps=[e5, e2],
                reads=[res("rm2"), res("rnm1")], writes=[res("rem")],
                tag=f"rem{gg}")
            e8 = s.ev("dve", lambda v: v.tensor_scalar_add(r_p1[:], r_em[:], 1.0),
                      deps=[e7], reads=[res("rem")], writes=[res("rp1")],
                      tag=f"rp1a{gg}")
            e9 = s.ev("dve", lambda v: v.reciprocal(r_p1[:], r_p1[:]),
                      deps=[e8], writes=[res("rp1")], tag=f"rp1b{gg}")
            e10 = s.ev("dve", lambda v: v.tensor_mul(r_p2[:], r_em[:], r_p1[:]),
                       deps=[e9], reads=[res("rem"), res("rp1")],
                       writes=[res("rp2")], tag=f"rp2{gg}")
            e11 = s.ev("dve", lambda v, gg=gg: v.tensor_scalar_mul(
                wtok[:, gg * 8:(gg + 1) * 8], r_eq1[:], r_p1[:]),
                deps=[e10, e3], reads=[res("req1"), res("rp1")],
                writes=[res(("wtok", gg))], tag=f"w1{gg}")
            e12 = s.ev("dve", lambda v, gg=gg: v.scalar_tensor_tensor(
                wtok[:, gg * 8:(gg + 1) * 8], r_eq2[:], r_p2[:],
                wtok[:, gg * 8:(gg + 1) * 8], op0=ALU.mult, op1=ALU.add),
                deps=[e11, e6], reads=[res("req2"), res("rp2")],
                writes=[res(("wtok", gg))], tag=f"w2{gg}")

    if phases <= 6:
        return locals()
    mslab_consumed = {}
    for m in range(2):
        for sl_ in range(3):
            mslab_consumed[(m, sl_)] = [e_x1[(15, 0)], e_x1[(15, 1)]]
    e_wbv_all = {}

    def emit_wbc(e):
        e_wbv = []
        for gg in range(8):
            ws = gg % 2
            e_wt = s.ev("pe", lambda t, gg=gg, e=e: t.transpose(
                pf[3][0:1, 0:128], wtok[:, gg * 8 + e:gg * 8 + e + 1],
                identf[:]), deps=[R[("wtok", gg)].w, e_idf],
                writes=[res(("pf", 3))], tag=f"wt{e}_{gg}")
            e_wv = s.ev("dve", lambda v, ws=ws: v.tensor_copy(
                wrow[ws][:], pf[3][0:1, 0:128]), deps=[e_wt],
                reads=[res(("pf", 3))], writes=[res(("wrow", ws))],
                tag=f"wv{e}_{gg}")
            e_wb = s.ev("pe", lambda t, gg=gg, ws=ws: t.matmul(
                pf[4][:, (gg % 4) * 128:(gg % 4 + 1) * 128], ones1f[:],
                wrow[ws][:], start=True, stop=True),
                deps=[e_wv, e_ones1f], reads=[res(("wrow", ws))],
                writes=([res(("pf", 4))] if gg % 4 == 0 else []),
                tag=f"wb{e}_{gg}")
            if gg % 4 == 3:
                e_bv = s.ev("dve", lambda v, gg=gg: v.tensor_copy(
                    wbc[:, (gg - 3) * 128:(gg + 1) * 128], pf[4][:, 0:512]),
                    deps=[e_wb], reads=[res(("pf", 4))],
                    writes=[res("wbc")], tag=f"wbv{e}_{gg}")
                e_wbv.append(e_bv)
        e_wbv_all[e] = e_wbv

    for e in range(E):
        if e > 0:
            emit_wbc(e)
        e_wbv = e_wbv_all.get(e, [])
        for jf in range(NJF):
            slot = jf % 3
            e_lg = s.ev("pool", lambda g_, e=e, jf=jf, slot=slot: g_.dma_start(
                out=mslab[0][:, slot * NK * 128:(slot + 1) * NK * 128],
                in_=d_wg[e, :, jf * NK * 128:(jf + 1) * NK * 128]),
                deps=mslab_consumed[(0, slot)], sem=f"ms0{slot}", incv=16,
                tag=f"lg{e}_{jf}")
            e_lu = s.ev("pool", lambda g_, e=e, jf=jf, slot=slot: g_.dma_start(
                out=mslab[1][:, slot * NK * 128:(slot + 1) * NK * 128],
                in_=d_wu[e, :, jf * NK * 128:(jf + 1) * NK * 128]),
                deps=mslab_consumed[(1, slot)], sem=f"ms1{slot}", incv=16,
                tag=f"lu{e}_{jf}")
            for th in range(2):
                sgs = (jf * 2 + th) % 2
                lastg = None
                for k in range(NK):
                    lastg = s.ev("pe", lambda t, slot=slot, k=k, th=th, st=(k == 0), sp_=(k == NK - 1): t.matmul(
                        pf[0][:, 0:512],
                        mslab[0][:, (slot * NK + k) * 128:(slot * NK + k + 1) * 128],
                        ln2[:, k * T + th * 512:k * T + (th + 1) * 512],
                        start=st, stop=sp_),
                        deps=[e_lg, e_ln2[k]],
                        writes=([res(("pf", 0))] if k == 0 else []),
                        tag=f"gm{e}_{jf}_{th}_{k}")
                e_sg = s.ev("act", lambda a_, sgs=sgs: a_.activation(
                    sgr[sgs][:], pf[0][:, 0:512], AF.Silu),
                    deps=[lastg], reads=[res(("pf", 0))],
                    writes=[res(("sgr", sgs))], tag=f"sg{e}_{jf}_{th}")
                lastu = None
                for k in range(NK):
                    lastu = s.ev("pe", lambda t, slot=slot, k=k, th=th, st=(k == 0), sp_=(k == NK - 1): t.matmul(
                        pf[1][:, 0:512],
                        mslab[1][:, (slot * NK + k) * 128:(slot * NK + k + 1) * 128],
                        ln2[:, k * T + th * 512:k * T + (th + 1) * 512],
                        start=st, stop=sp_),
                        deps=[e_lu, e_ln2[k]],
                        writes=([res(("pf", 1))] if k == 0 else []),
                        tag=f"um{e}_{jf}_{th}_{k}")
                if e == 0:
                    e_h1 = s.ev("dve", lambda v, jf=jf, th=th, sgs=sgs: v.tensor_mul(
                        hidden[:, jf * T + th * 512:jf * T + (th + 1) * 512],
                        sgr[sgs][:], pf[1][:, 0:512]),
                        deps=[lastu, e_sg], reads=[res(("pf", 1)), res(("sgr", sgs))],
                        writes=[res(("hid", e, jf))], tag=f"h1{e}_{jf}_{th}")
                else:
                    e_h1 = s.ev("dve", lambda v, sgs=sgs: v.tensor_mul(
                        h1r[sgs][:], sgr[sgs][:], pf[1][:, 0:512]),
                        deps=[lastu, e_sg], reads=[res(("pf", 1)), res(("sgr", sgs))],
                        writes=[res(("h1r", sgs))], tag=f"h1{e}_{jf}_{th}")
                    e_h2 = s.ev("dve", lambda v, e=e, jf=jf, th=th, sgs=sgs: v.tensor_mul(
                        hidden[:, (e * NJF + jf) * T + th * 512:(e * NJF + jf) * T + (th + 1) * 512],
                        h1r[sgs][:], wbc[:, th * 512:(th + 1) * 512]),
                        deps=[e_h1] + e_wbv, reads=[res(("h1r", sgs)), res("wbc")],
                        writes=[res(("hid", e, jf))], tag=f"h2{e}_{jf}_{th}")
                mslab_consumed[(0, slot)] = [lastg]
                mslab_consumed[(1, slot)] = [lastu]
        if e == 0:
            emit_top2()
            emit_wbc(0)
            e_wbv = e_wbv_all[0]
            for jf in range(NJF):
                for th in range(2):
                    s.ev("dve", lambda v, jf=jf, th=th: v.tensor_mul(
                        hidden[:, jf * T + th * 512:jf * T + (th + 1) * 512],
                        hidden[:, jf * T + th * 512:jf * T + (th + 1) * 512],
                        wbc[:, th * 512:(th + 1) * 512]),
                        deps=e_wbv, reads=[res("wbc")],
                        writes=[res(("hid", 0, jf))], tag=f"h2p{jf}_{th}")
    s.fence()

    if phases <= 7:
        return locals()
    e_sts = []
    dslab_consumed = [mslab_consumed[(0, 2)][0], mslab_consumed[(1, 2)][0]]
    dslab_first = [mslab_consumed[(0, 0)][0], mslab_consumed[(0, 1)][0],
                   mslab_consumed[(1, 0)][0], mslab_consumed[(1, 1)][0]]
    for j in range(NK):
        buf = j % 2
        e_ldp = []
        for p in range(3):
            e_ldp.append(s.ev("pool", lambda g_, j=j, buf=buf, p=p: g_.dma_start(
                out=mslab[buf][:, p * 2048:(p + 1) * 2048],
                in_=d_wd[:, j * E * NJF * 128 + p * 2048:
                         j * E * NJF * 128 + (p + 1) * 2048]),
                deps=[dslab_consumed[buf]] + (dslab_first if j < 2 else []),
                sem=f"m8{buf}{p}", incv=16, tag=f"dsl{j}_{p}"))
        for th in range(2):
            bank = (j * 2 + th) % 2
            last = None
            first = True
            for e in range(E):
                for kk in range(NJF):
                    last = s.ev("pe", lambda t, e=e, kk=kk, th=th, bank=bank, buf=buf, st=first: t.matmul(
                        pf[bank][:, 0:512],
                        mslab[buf][:, (e * NJF + kk) * 128:(e * NJF + kk + 1) * 128],
                        hidden[:, (e * NJF + kk) * T + th * 512:(e * NJF + kk) * T + (th + 1) * 512],
                        start=st, stop=False),
                        deps=[e_ldp[(e * NJF + kk) // 16]],
                        reads=[res(("hid", e, kk))],
                        writes=([res(("pf", bank))] if first else []),
                        tag=f"dm{j}_{th}_{e}_{kk}")
                    first = False
            last = s.ev("pe", lambda t, j=j, th=th, bank=bank: t.matmul(
                pf[bank][:, 0:512], ones1b[:],
                m2row[0:1, th * 512:(th + 1) * 512], start=False, stop=True),
                deps=[last, R["m2row"].w, e_ms[3]], reads=[res("m2row")],
                tag=f"dmm{j}_{th}")
            dslab_consumed[buf] = last
            ts_ = (j * 2 + th) % 2
            e_t1 = s.ev("dve", lambda v, j=j, th=th, ts_=ts_: v.tensor_mul(
                tout[ts_][:], ln2[:, j * T + th * 512:j * T + (th + 1) * 512],
                sigbc[:, th * 512:(th + 1) * 512]),
                deps=[R[("ln2", j)].w, R[id(sigbc)].w],
                reads=[res(("ln2", j)), res(id(sigbc))],
                writes=[res(("tout", ts_))], tag=f"t1_{j}_{th}")
            e_t2 = s.ev("dve", lambda v, ts_=ts_, bank=bank: v.tensor_add(
                ostg[ts_][:], tout[ts_][:], pf[bank][:, 0:512]),
                deps=[last, e_t1], reads=[res(("pf", bank)), res(("tout", ts_))],
                writes=[res(("ostg", ts_))], tag=f"t2_{j}_{th}")
            e_st = s.ev("sp", lambda sp, j=j, th=th, ts_=ts_: sp.dma_start(
                out=d_o[j * 128:(j + 1) * 128, th * 512:(th + 1) * 512],
                in_=ostg[ts_][:]), deps=[e_t2], reads=[res(("ostg", ts_))],
                sem=f"o{ts_}", incv=16, tag=f"st{j}_{th}")
            e_sts.append(e_st)
    e_end = s.ev("sp", lambda sp: sp.nop(), deps=e_sts, tag="end")

    return locals()


def finish(env):
    nc, s, sems = env["nc"], env["s"], env["sems"]
    s.finalize_plan()
    with nc.Block() as blk:
        @blk.tensor
        def _(t):
            s.emit("pe", t, sems)

        @blk.vector
        def _(v):
            s.emit("dve", v, sems)

        @blk.scalar
        def _(a):
            s.emit("act", a, sems)

        @blk.gpsimd
        def _(g):
            s.emit("pool", g, sems)

        @blk.sync
        def _(sp):
            s.emit("sp", sp, sems)
    nc.finalize()
    return nc


def stage(inputs):
    W_in = inputs["L0_attn_W_in"]
    W_out = inputs["L0_attn_W_out"]
    conv_w = inputs["L0_attn_conv_w"]
    conv_b = inputs["L0_attn_conv_b"]
    z_bias = inputs["L0_attn_z_bias"]
    Dd = inputs["L0_attn_D"]
    Wr = inputs["L0_ffn_w_gate_router"]
    Wg = inputs["L0_ffn_w_gate_proj"]
    Wu = inputs["L0_ffn_w_up_proj"]
    Wd = inputs["L0_ffn_w_down_proj"]
    u = inputs["L0_attn_u"]
    C = np.ascontiguousarray

    wxbc = C(np.asarray(W_in[:CONVD]).reshape(NJX, 128, NK, 128)
             .transpose(3, 0, 2, 1).reshape(128, -1))
    wz = C(np.asarray(W_in[CONVD:AOFF]).reshape(NJZ, 128, NK, 128)
           .transpose(3, 0, 2, 1).reshape(128, -1))
    wa = C(np.asarray(W_in[AOFF:]).reshape(32, NK, 128)
           .transpose(2, 1, 0).reshape(128, -1))
    wout = C(np.asarray(W_out).reshape(NK, 128, NK, 128)
             .transpose(3, 0, 2, 1).reshape(128, -1))
    wr = C(np.asarray(Wr).T.reshape(NK, 128, 8).transpose(1, 0, 2)
           .reshape(128, -1))
    wg = C(np.asarray(Wg).reshape(E, NJF, 128, NK, 128)
           .transpose(4, 0, 1, 3, 2).reshape(128, E, -1).transpose(1, 0, 2))
    wu = C(np.asarray(Wu).reshape(E, NJF, 128, NK, 128)
           .transpose(4, 0, 1, 3, 2).reshape(128, E, -1).transpose(1, 0, 2))
    wd = C(np.asarray(Wd).reshape(E, NK, 128, NJF, 128)
           .transpose(4, 1, 0, 3, 2).reshape(128, -1))
    cw = C(np.asarray(conv_w)[:, 0, :].reshape(NJX, 128, 4)
           .transpose(1, 0, 2).reshape(128, -1))
    cb = C(np.asarray(conv_b).reshape(NJX, 128).T)
    zb = C(np.asarray(z_bias).reshape(NJZ, 128).T)
    drep = C(np.repeat(np.asarray(Dd), HD).reshape(NK, 128).T)
    shared = dict(wxbc=wxbc, wz=wz, wa=wa, wout=wout, wr=wr, wg=wg, wu=wu,
                  wd=wd, cw=cw, cb=cb, zb=zb, drep=drep)
    un = np.asarray(u)
    per_core = []
    for core in range(8):
        b, h = core // 2, core % 2
        ue = np.zeros((D, TE), np.float32)
        if h == 0:
            ue[:, 3:] = un[b, 0:T, :].T
        else:
            ue[:, :] = un[b, T - 3:L, :].T
        m = dict(shared)
        m["u"] = C(ue)
        m["smask"] = np.full((128, 1), float(h), np.float32)
        per_core.append(m)
    return per_core


def kernel(**inputs):
    per_core = stage(inputs)
    env = build_nc(debug=0)
    nc = finish(env)
    res = run_bass_kernel_spmd(nc, per_core, core_ids=list(range(8)))
    u = np.asarray(inputs["L0_attn_u"])
    out = np.empty((B, L, D), np.float32)
    for core in range(8):
        b, h = core // 2, core % 2
        out[b, h * T:(h + 1) * T, :] = np.asarray(res.results[core]["o"]).T
    return out


